# version 10
# baseline (speedup 1.0000x reference)
import math
import os
from contextlib import ExitStack

import numpy as np
import ml_dtypes

import concourse.bass as bass
import concourse.mybir as mybir
from concourse.bass_utils import run_bass_kernel_spmd

F32 = mybir.dt.float32
BF16 = mybir.dt.bfloat16
F16 = mybir.dt.float16
AF = mybir.ActivationFunctionType
ALU = mybir.AluOpType
AX = mybir.AxisListType.X

NCORE = 8
S = 16384
D = 1024
R = 16
TOK = 2048
HC = 8
IFF = 2816
CI = 22
EPS = 1e-5

LN1E4 = float(np.log(10000.0))
TWO_PI = 2.0 * math.pi
C1 = float(np.float32(int(TWO_PI * 2**9) / 2**9))
C2 = float(np.float32(int((TWO_PI - C1) * 2**20) / 2**20))
C3 = float(TWO_PI - C1 - C2)
PI = float(math.pi)
HALF_PI = float(math.pi / 2)
INV_2PI = float(1.0 / TWO_PI)

ENGS = ("sync", "tensor", "vector", "scalar", "gpsimd")


def build_nc(debug=False):
    nc = bass.Bass("TRN2", target_bir_lowering=False, debug=False,
                   detect_race_conditions=False)

    x_own = nc.dram_tensor("x_own", [R, 128, D], F32, kind="ExternalInput")
    pos_own = nc.dram_tensor("pos_own", [1, TOK], F32, kind="ExternalInput")
    dmask = nc.dram_tensor("dmask", [R, 8, 128, 128], BF16, kind="ExternalInput")
    phi_hi = nc.dram_tensor("phi_hi", [128, 4], F32, kind="ExternalInput")
    phi_lo = nc.dram_tensor("phi_lo", [128, 4], F32, kind="ExternalInput")
    ident_b = nc.dram_tensor("ident_b", [128, 128], BF16, kind="ExternalInput")
    ident_f = nc.dram_tensor("ident_f", [128, 128], F32, kind="ExternalInput")
    clips = nc.dram_tensor("clips", [1, 2], F32, kind="ExternalInput")
    wq_t = nc.dram_tensor("wq_t", [D, D], F32, kind="ExternalInput")
    wk_t = nc.dram_tensor("wk_t", [D, D], F32, kind="ExternalInput")
    wv_t = nc.dram_tensor("wv_t", [D, D], F32, kind="ExternalInput")
    wg_ih = nc.dram_tensor("wg_ih", [IFF, D], F32, kind="ExternalInput")
    wu_ih = nc.dram_tensor("wu_ih", [IFF, D], F32, kind="ExternalInput")
    wd_oi = nc.dram_tensor("wd_oi", [D, IFF], F32, kind="ExternalInput")

    out_t = nc.dram_tensor("out", [TOK, D], F32, kind="ExternalOutput")
    if debug:
        dbg_qr = nc.dram_tensor("dbg_qr", [HC * 128, TOK], F32, kind="ExternalOutput")
        dbg_sml = nc.dram_tensor("dbg_sml", [1, 64], F32, kind="ExternalOutput")
        dbg_negb = nc.dram_tensor("dbg_negb", [128, 1], F32, kind="ExternalOutput")
        dbg_den = nc.dram_tensor("dbg_den", [128, 16], F32, kind="ExternalOutput")
        dbg_osb = nc.dram_tensor("dbg_osb", [128, R * 1024], F32, kind="ExternalOutput")
        dbg_xmid = nc.dram_tensor("dbg_xmid", [128, R * 1024], F32, kind="ExternalOutput")
        dbg_hq = nc.dram_tensor("dbg_hq", [128, CI * 128], F32, kind="ExternalOutput")
        dbg_xn = nc.dram_tensor("dbg_xn", [HC * 128, TOK], F32, kind="ExternalOutput")
        dbg_nrm = nc.dram_tensor("dbg_nrm", [3, 16], F32, kind="ExternalOutput")
        dbg_row = nc.dram_tensor("dbg_row", [1, 64], F32, kind="ExternalOutput")
        dbg_xt = nc.dram_tensor("dbg_xt", [2, 128, 1024], F32, kind="ExternalOutput")
        dbg_xnt = nc.dram_tensor("dbg_xnt", [2, 128, 1024], F32, kind="ExternalOutput")
        dbg_cos = nc.dram_tensor("dbg_cos", [4 * 128, 1024], F32, kind="ExternalOutput")
        dbg_sin = nc.dram_tensor("dbg_sin", [4 * 128, 1024], F32, kind="ExternalOutput")

    kr_in = nc.dram_tensor("kr_in", [D, TOK], BF16)
    kr_out = nc.dram_tensor("kr_out", [NCORE * D, TOK], BF16)
    v_in = nc.dram_tensor("v_in", [TOK, D], BF16)
    v_out = nc.dram_tensor("v_out", [NCORE * TOK, D], BF16)
    nrm_in = nc.dram_tensor("nrm_in", [1, 16], F32)
    nrm_out = nc.dram_tensor("nrm_out", [1, 16], F32)
    m1_in = nc.dram_tensor("m1_in", [1, 16], F32)
    m1_out = nc.dram_tensor("m1_out", [1, 16], F32)
    m2_in = nc.dram_tensor("m2_in", [1, 16], F32)
    m2_out = nc.dram_tensor("m2_out", [1, 16], F32)
    ig_scr = nc.dram_tensor("ig_scr", [HC, 128, IFF], BF16)
    iu_scr = nc.dram_tensor("iu_scr", [HC, 128, IFF], BF16)
    id_scr = nc.dram_tensor("id_scr", [CI, 128, D], BF16)

    prog = {e: [] for e in ENGS}
    DMAS = ("dC", "dX0", "dX1", "dKV", "dV0", "dV1", "dB0", "dB1", "dP0", "dP1",
            "dI0", "dI1", "dIG0", "dIG1", "dID0", "dID1", "dRW", "dWT",
            "dCC", "dOUT", "dDBG")
    n = {k: 0 for k in DMAS}
    n.update({"cc": 0, "pe": 0, "ve": 0, "sc": 0})

    st = ExitStack()
    E = st.enter_context
    sem = {}
    for s_ in list(n.keys()):
        sem[s_] = E(nc.semaphore(s_))

    def em(eng, fn):
        prog[eng].append(fn)

    def dma(eng, out, in_, ctr):
        n[ctr] += 16
        em(eng, lambda e, o=out, i=in_: e.dma_start(out=o, in_=i)
           .then_inc(sem[ctr], 16))
        return n[ctr]

    def wait(eng, ctr, val):
        em(eng, lambda e, c=ctr, v=val: e.wait_ge(sem[c], v))

    def inc(eng, ctr, fn):
        n[ctr] += 1
        em(eng, lambda e, f=fn, c=ctr: f(e).then_inc(sem[c], 1))
        return n[ctr]

    def join():
        marks = {}
        for e_, c_ in (("tensor", "pe"), ("vector", "ve"), ("scalar", "sc")):
            marks[c_] = inc(e_, c_, lambda e: e.drain())
        snap = dict(n)
        for e_ in ENGS:
            for c_, v_ in snap.items():
                wait(e_, c_, v_)

    _PHASE = os.environ.get("K_PHASE", "all")

    class _Trunc(Exception):
        pass

    def gate(tag):
        if _PHASE == tag:
            join()
            raise _Trunc()

    ident_bs = E(nc.sbuf_tensor("ident_bs", [128, 128], BF16))
    ident_fs = E(nc.sbuf_tensor("ident_fs", [128, 128], F32))
    ones_row = E(nc.sbuf_tensor("ones_row", [1, 128], F32))
    ones_col = E(nc.sbuf_tensor("ones_col", [128, 1], BF16))
    negB = E(nc.sbuf_tensor("negB", [128, 1], F32))
    phih_sb = E(nc.sbuf_tensor("phih_sb", [128, 4], F32))
    phil_sb = E(nc.sbuf_tensor("phil_sb", [128, 4], F32))
    clips_sb = E(nc.sbuf_tensor("clips_sb", [1, 2], F32))
    sml = E(nc.sbuf_tensor("sml", [1, 64], F32))
    bc1 = E(nc.sbuf_tensor("bc1", [128, 8], F32))
    stage16 = E(nc.sbuf_tensor("stage16", [1, 16], F32))
    mg_t = E(nc.sbuf_tensor("mg_t", [128, 128], F32))
    mu_t = E(nc.sbuf_tensor("mu_t", [128, 128], F32))
    md_t = E(nc.sbuf_tensor("md_t", [128, 128], F32))
    sg_f = E(nc.sbuf_tensor("sg_f", [128, 22], F32))
    su_f = E(nc.sbuf_tensor("su_f", [128, 22], F32))
    s1sg = E(nc.sbuf_tensor("s1sg", [128, 22], F32))
    s1su = E(nc.sbuf_tensor("s1su", [128, 22], F32))
    lnm = E(nc.sbuf_tensor("lnm", [128, 16], F32))
    lnr = E(nc.sbuf_tensor("lnr", [128, 16], F32))
    hmax = E(nc.sbuf_tensor("hmax", [128, 1], F32))
    pad128 = E(nc.sbuf_tensor("pad128", [128, 128], F32))
    epsb = E(nc.sbuf_tensor("epsb", [128, 1], F32))
    sdrow = E(nc.sbuf_tensor("sdrow", [1, D], F32))
    gath = E(nc.sbuf_tensor("gath", [128, 128], F32))
    snap_sb = E(nc.sbuf_tensor("snap_sb", [1, 64], F32))
    QR = E(nc.sbuf_tensor("QR", [128, HC, TOK], BF16))
    spc = E(nc.sbuf_tensor("spc", [128, 1], F32))


    def vpad():
        em("vector", lambda e: e.tensor_copy(out=spc[:, :], in_=ones_col[:, :]))

    def pmax_to_sml(src_col_ap, dst, tpf):
        vpad()
        cve = inc("vector", "ve", lambda e: e.tensor_copy(
            out=pad128[:, 0:1], in_=src_col_ap))
        wait("tensor", "ve", cve)
        wait("tensor", "sc", n["sc"])
        cpe = inc("tensor", "pe", lambda e: e.transpose(
            tpf[:, :], pad128[:, :], ident_fs[:, :]))
        wait("vector", "pe", cpe)
        r_ = inc("vector", "ve", lambda e, dst=dst: e.reduce_max(
            out=sml[0:1, dst:dst + 1], in_=tpf[0:1, 0:128], axis=AX))
        vpad()
        return r_

    def bcast(src_ap, dstcol, bcp):
        wait("tensor", "sc", n["sc"])
        wait("tensor", "ve", n["ve"])
        cpe = inc("tensor", "pe", lambda e: e.matmul(
            bcp[:, 0:1], ones_row[0:1, :], src_ap, start=True, stop=True,
            skip_group_check=True))
        wait("scalar", "pe", cpe)
        return inc("scalar", "sc", lambda e, dstcol=dstcol: e.copy(
            out=bc1[:, dstcol:dstcol + 1], in_=bcp[:, 0:1]))

    try:
        dma("sync", ident_bs[:, :], ident_b[:, :], "dC")
        dma("sync", ident_fs[:, :], ident_f[:, :], "dC")
        dma("sync", phih_sb[:, :], phi_hi[:, :], "dC")
        dma("sync", phil_sb[:, :], phi_lo[:, :], "dC")
        d_consts = dma("sync", clips_sb[:, :], clips[:, :], "dC")
        wait("tensor", "dC", d_consts)
        wait("scalar", "dC", d_consts)

        wait("vector", "dC", d_consts)
        em("vector", lambda e: e.memset(ones_row[:, :], 1.0))
        em("vector", lambda e: e.memset(ones_col[:, :], 1.0))
        em("vector", lambda e: e.memset(pad128[:, :], 0.0))
        em("vector", lambda e: e.memset(sml[:, :], 0.0))
        em("vector", lambda e: e.memset(mg_t[:, :], 0.0))
        em("vector", lambda e: e.memset(mu_t[:, :], 0.0))
        em("vector", lambda e: e.memset(md_t[:, :], 0.0))
        em("vector", lambda e: e.memset(tmpc_init[0][:, :], 0.0)
           if False else e.memset(bc1[:, :], 0.0))
        em("vector", lambda e: e.memset(epsb[:, :], EPS))
        c_consts = inc("vector", "ve", lambda e: e.memset(stage16[:, :], 0.0))

        p1 = ExitStack()
        E1 = p1.enter_context
        pos_sb = E1(nc.sbuf_tensor("pos_sb", [1, TOK], F32))
        XN = E1(nc.sbuf_tensor("XN", [128, HC, TOK], BF16))
        costab = E1(nc.sbuf_tensor("costab", [128, 4, 1024], F32))
        sintab = E1(nc.sbuf_tensor("sintab", [128, 4, 1024], F32))
        w_bf = E1(nc.sbuf_tensor("w_bf", [128, 2, HC, 1024], BF16))
        xt = E1(nc.sbuf_tensor("xt", [128, 2, 1024], F32))
        xnt = E1(nc.sbuf_tensor("xnt", [128, 1024], BF16))
        sqv = E1(nc.sbuf_tensor("sqv", [128, 2, 1024], BF16))
        t1 = E1(nc.sbuf_tensor("t1", [128, 1024], F32))
        t2 = E1(nc.sbuf_tensor("t2", [128, 1024], F32))
        krs = E1(nc.sbuf_tensor("krs", [128, 2, 1024], BF16))
        vst = E1(nc.sbuf_tensor("vst", [128, 2, 1024], BF16))
        posb = E1(nc.sbuf_tensor("posb", [128, 1024], F32))
        angb = E1(nc.sbuf_tensor("angb", [128, 1024], F32))
        kkb = E1(nc.sbuf_tensor("kkb", [128, 1024], F32))
        redb = E1(nc.sbuf_tensor("redb", [128, 1024], F32))
        stt1 = E1(nc.sbuf_tensor("stt1", [128, 16], F32))
        stt2 = E1(nc.sbuf_tensor("stt2", [128, 16], F32))
        angi = E1(nc.sbuf_tensor("angi", [128, 1024], mybir.dt.int32))
        tmpc = E1(nc.sbuf_tensor("tmpc", [128, 8], F32))

        dma("sync", pos_sb[:, :], pos_own[:, :], "dC")
        for hc in range(HC):
            dma("gpsimd", w_bf[:, 0, hc, :], wq_t[hc * 128:(hc + 1) * 128, :], "dWT")
        for hc in range(HC):
            dma("gpsimd", w_bf[:, 1, hc, :], wk_t[hc * 128:(hc + 1) * 128, :], "dWT")
        d_wqk = n["dWT"]

        ps1a = ExitStack()
        trp = ps1a.enter_context(nc.psum_tensor("trp", [128, 512], BF16))
        trp2 = ps1a.enter_context(nc.psum_tensor("trp2", [128, 512], BF16))
        g_x = [0, 0]
        g_xs = [0, 0]
        g_xn = 0
        g_tr = [0, 0]
        for t in range(R):
            sl = t % 2
            xc = "dX0" if sl == 0 else "dX1"
            wait("sync", "ve", g_x[sl])
            wait("sync", "sc", g_xs[sl])
            d_x = dma("sync", xt[:, sl, :], x_own[t, :, :], xc)
            wait("scalar", xc, d_x)
            if debug and _PHASE == "qkv" and t < 2:
                wait("gpsimd", xc, d_x)
                dX = dma("gpsimd", dbg_xt[t, :, :], xt[:, sl, :], "dDBG")
                wait("sync", "dDBG", dX)
            c_sq = inc("scalar", "sc", lambda e, sl=sl: e.activation(
                out=t1[:, :], in_=xt[:, sl, :], func=AF.Square))
            wait("vector", xc, d_x)
            wait("vector", "sc", c_sq)
            em("vector", lambda e, sl=sl, t=t: e.reduce_sum(
                stt1[:, t:t + 1], xt[:, sl, :], axis=AX))
            em("vector", lambda e, t=t: e.reduce_sum(
                stt2[:, t:t + 1], t1[:, :], axis=AX))
            em("vector", lambda e, t=t: e.tensor_scalar(
                out=stt1[:, t:t + 1], in0=stt1[:, t:t + 1], scalar1=1.0 / D,
                scalar2=None, op0=ALU.mult))
            em("vector", lambda e, t=t: e.tensor_scalar(
                out=stt2[:, t:t + 1], in0=stt2[:, t:t + 1], scalar1=1.0 / D,
                scalar2=None, op0=ALU.mult))
            em("vector", lambda e, t=t: e.scalar_tensor_tensor(
                out=tmpc[:, 0:1], in0=stt1[:, t:t + 1], scalar=-1.0,
                in1=stt1[:, t:t + 1], op0=ALU.mult, op1=ALU.mult))
            vpad()
            c_var = inc("vector", "ve", lambda e, t=t: e.tensor_tensor(
                out=stt2[:, t:t + 1], in0=stt2[:, t:t + 1], in1=tmpc[:, 0:1],
                op=ALU.add))
            wait("scalar", "ve", c_var)
            c_rt = inc("scalar", "sc", lambda e, t=t: e.activation(
                out=tmpc[:, 1:2], in_=stt2[:, t:t + 1], func=AF.Sqrt,
                bias=epsb[:, 0:1]))
            wait("vector", "sc", c_rt)
            em("vector", lambda e: e.reciprocal(tmpc[:, 2:3], tmpc[:, 1:2]))
            em("vector", lambda e, t=t: e.tensor_scalar(
                out=tmpc[:, 3:4], in0=stt2[:, t:t + 1], scalar1=EPS, scalar2=-0.5,
                op0=ALU.add, op1=ALU.mult))
            em("vector", lambda e: e.tensor_tensor(
                out=tmpc[:, 4:5], in0=tmpc[:, 2:3], in1=tmpc[:, 2:3], op=ALU.mult))
            vpad()
            em("vector", lambda e: e.tensor_tensor(
                out=tmpc[:, 4:5], in0=tmpc[:, 4:5], in1=tmpc[:, 3:4], op=ALU.mult))
            vpad()
            em("vector", lambda e: e.tensor_scalar(
                out=tmpc[:, 4:5], in0=tmpc[:, 4:5], scalar1=1.5, scalar2=None,
                op0=ALU.add))
            vpad()
            em("vector", lambda e: e.tensor_tensor(
                out=tmpc[:, 2:3], in0=tmpc[:, 2:3], in1=tmpc[:, 4:5], op=ALU.mult))
            vpad()
            c_bias = inc("vector", "ve", lambda e, t=t: e.scalar_tensor_tensor(
                out=tmpc[:, 5:6], in0=stt1[:, t:t + 1], scalar=-1.0,
                in1=tmpc[:, 2:3], op0=ALU.mult, op1=ALU.mult))
            g_x[sl] = c_bias
            wait("scalar", "ve", c_bias)
            wait("scalar", "pe", g_xn)
            c_xn = inc("scalar", "sc", lambda e, sl=sl: e.activation(
                out=xnt[:, :], in_=xt[:, sl, :], func=AF.Identity,
                bias=tmpc[:, 5:6], scale=tmpc[:, 2:3]))
            g_xs[sl] = c_xn
            wait("tensor", "sc", c_xn)
            if debug and _PHASE == "qkv" and t < 2:
                wait("gpsimd", "sc", c_xn)
                dX = dma("gpsimd", dbg_xnt[t, :, :], xnt[:, :], "dDBG")
                wait("scalar", "dDBG", dX)
            for g in range(2):
                tp = trp if g == 0 else trp2
                wait("tensor", "sc", g_tr[g])
                for k4 in range(4):
                    hc = g * 4 + k4
                    fn = (lambda e, tp=tp, k4=k4, hc=hc: e.transpose(
                        tp[:, k4 * 128:(k4 + 1) * 128],
                        xnt[:, hc * 128:(hc + 1) * 128], ident_bs[:, :]))
                    if k4 == 3:
                        g_xn = inc("tensor", "pe", fn)
                    else:
                        em("tensor", fn)
                wait("scalar", "pe", g_xn)
                g_tr[g] = inc("scalar", "sc", lambda e, tp=tp, g=g, t=t: e.copy(
                    out=XN[:, g * 4:(g + 1) * 4, t * 128:(t + 1) * 128],
                    in_=tp[:, :].rearrange("p (a b) -> p a b", a=4)))
            if os.environ.get("KJOIN", "0") == "1":
                join()
        ps1a.close()
        join()

        ps1n = ExitStack()
        nps = [ps1n.enter_context(nc.psum_tensor(f"nps{i}", [128, 1024], F32))
               for i in range(2)]
        tpfn = ps1n.enter_context(nc.psum_tensor("tpfn", [128, 128], F32))
        g_nev = [0, 0]
        for proj in range(2):
            for t in range(R):
                sl = t % 2
                wait("tensor", "sc", g_nev[sl])
                for hc in range(HC):
                    for k in range(2):
                        fn = (lambda e, sl=sl, hc=hc, k=k, t=t, proj=proj: e.matmul(
                            nps[sl][:, k * 512:(k + 1) * 512],
                            XN[:, hc, t * 128:(t + 1) * 128],
                            w_bf[:, proj, hc, k * 512:(k + 1) * 512],
                            start=(hc == 0), stop=(hc == HC - 1),
                            skip_group_check=True))
                        if hc == HC - 1 and k == 1:
                            c_nmm = inc("tensor", "pe", fn)
                        else:
                            em("tensor", fn)
                wait("scalar", "pe", c_nmm)
                wait("scalar", "ve", n["ve"])
                g_nev[sl] = c_nsq = inc("scalar", "sc", lambda e, sl=sl: e.activation(
                    out=t1[:, :], in_=nps[sl][:, :], func=AF.Square))
                wait("vector", "sc", c_nsq)
                em("vector", lambda e: e.reduce_sum(
                    out=tmpc[:, 6:7], in_=t1[:, :], axis=AX))
                vpad()
                if t == 0:
                    cnv = inc("vector", "ve", lambda e: e.tensor_copy(
                        out=tmpc[:, 7:8], in_=tmpc[:, 6:7]))
                else:
                    cnv = inc("vector", "ve", lambda e: e.tensor_tensor(
                        out=tmpc[:, 7:8], in0=tmpc[:, 7:8], in1=tmpc[:, 6:7],
                        op=ALU.max))
            pmax_to_sml(tmpc[:, 7:8], proj, tpfn)
        ps1n.close()

        wait("tensor", "sc", n["sc"])
        wait("tensor", "ve", n["ve"])
        c_mkst = inc("vector", "ve", lambda e: e.tensor_copy(
            out=stage16[0:1, 0:1], in_=sml[0:1, 1:2]))
        wait("gpsimd", "ve", c_mkst)
        d_nrm = dma("gpsimd", nrm_in[:, :], stage16[:, :], "dCC")
        wait("gpsimd", "dCC", d_nrm)
        n["cc"] += 1
        em("gpsimd", lambda e: e.collective_compute(
            "AllReduce", ALU.max, replica_groups=[list(range(NCORE))],
            ins=[nrm_in.ap()], outs=[nrm_out.ap()]).then_inc(sem["cc"], 1))
        wait("gpsimd", "cc", 1)
        d_nrmo = dma("gpsimd", stage16[:, :], nrm_out[:, :], "dCC")
        if debug:
            dma("gpsimd", dbg_nrm[0:1, :], nrm_in[:, :], "dDBG")
            dma("gpsimd", dbg_nrm[1:2, :], nrm_out[:, :], "dDBG")
            wait("gpsimd", "dDBG", n["dDBG"])
        wait("scalar", "dCC", d_nrmo)
        wait("scalar", "ve", n["ve"])
        em("scalar", lambda e: e.activation(
            out=sml[0:1, 2:3], in_=sml[0:1, 0:1], func=AF.Sqrt))
        c_sq2 = inc("scalar", "sc", lambda e: e.activation(
            out=sml[0:1, 3:4], in_=stage16[0:1, 0:1], func=AF.Sqrt))
        wait("vector", "sc", c_sq2)
        c_nb = inc("vector", "ve", lambda e: e.scalar_tensor_tensor(
            out=sml[0:1, 4:5], in0=sml[0:1, 2:3], scalar=-1.02 / 32.0,
            in1=sml[0:1, 3:4], op0=ALU.mult, op1=ALU.mult))
        if debug:
            em("vector", lambda e: e.memset(snap_sb[:, :], 0.0))
            em("vector", lambda e: e.tensor_copy(out=snap_sb[0:1, 0:16],
                                                 in_=sml[0:1, 0:16]))
            c_snap = inc("vector", "ve", lambda e: e.tensor_copy(
                out=snap_sb[0:1, 16:32], in_=stage16[0:1, 0:16]))
            wait("gpsimd", "ve", c_snap)
            dma("gpsimd", dbg_row[:, :], snap_sb[0:1, :], "dDBG")
        wait("tensor", "ve", c_nb)
        with nc.psum_tensor("bcp0", [128, 16], F32) as bcp0:
            c_nbm = inc("tensor", "pe", lambda e: e.matmul(
                bcp0[:, 0:1], ones_row[0:1, :], sml[0:1, 4:5], start=True,
                stop=True, skip_group_check=True))
            wait("scalar", "pe", c_nbm)
            c_negB = inc("scalar", "sc", lambda e: e.copy(
                out=negB[:, :], in_=bcp0[:, 0:1]))
        wait("tensor", "sc", c_negB)
        if debug and _PHASE == "negb":
            wait("gpsimd", "sc", c_negB)
            dma("gpsimd", dbg_sml[:, :], sml[:, :], "dDBG")
            dma("gpsimd", dbg_negb[:, :], negB[:, :], "dDBG")
            wait("gpsimd", "dDBG", n["dDBG"])
        gate("negb")


        ps1b = ExitStack()
        qe_ps = ps1b.enter_context(nc.psum_tensor("qe_ps", [128, 1024], F32))
        qo_ps = ps1b.enter_context(nc.psum_tensor("qo_ps", [128, 1024], F32))

        n_kr_dma = 0

        for half in range(2):
            for k in range(2):
                c_bcmm = inc("tensor", "pe", lambda e, k=k, half=half: e.matmul(
                    qe_ps[:, k * 512:(k + 1) * 512], ones_row[0:1, :],
                    pos_sb[0:1, half * 1024 + k * 512:half * 1024 + (k + 1) * 512],
                    start=True, stop=True, skip_group_check=True))
            wait("vector", "pe", c_bcmm)
            c_posb = inc("vector", "ve", lambda e: e.tensor_copy(
                out=posb[:, :], in_=qe_ps[:, :]))
            for pc in range(4):
                em("vector", lambda e, pc=pc: e.tensor_scalar(
                    out=redb[:, :], in0=posb[:, :], scalar1=phil_sb[:, pc:pc + 1],
                    scalar2=None, op0=ALU.mult))
                em("vector", lambda e, pc=pc: e.tensor_scalar(
                    out=angb[:, :], in0=posb[:, :], scalar1=phih_sb[:, pc:pc + 1],
                    scalar2=None, op0=ALU.mult))
                em("vector", lambda e: e.tensor_tensor(
                    out=angb[:, :], in0=angb[:, :], in1=redb[:, :], op=ALU.add))
                em("vector", lambda e: e.tensor_scalar(
                    out=kkb[:, :], in0=angb[:, :], scalar1=INV_2PI, scalar2=None,
                    op0=ALU.mult))
                em("vector", lambda e: e.tensor_copy(out=angi[:, :], in_=kkb[:, :]))
                em("vector", lambda e: e.tensor_copy(out=kkb[:, :], in_=angi[:, :]))
                em("vector", lambda e: e.tensor_scalar(
                    out=kkb[:, :], in0=kkb[:, :], scalar1=-1.0, scalar2=None,
                    op0=ALU.mult))
                em("vector", lambda e: e.scalar_tensor_tensor(
                    out=redb[:, :], in0=kkb[:, :], scalar=C1, in1=angb[:, :],
                    op0=ALU.mult, op1=ALU.add))
                em("vector", lambda e: e.scalar_tensor_tensor(
                    out=redb[:, :], in0=kkb[:, :], scalar=C2, in1=redb[:, :],
                    op0=ALU.mult, op1=ALU.add))
                em("vector", lambda e: e.scalar_tensor_tensor(
                    out=redb[:, :], in0=kkb[:, :], scalar=C3, in1=redb[:, :],
                    op0=ALU.mult, op1=ALU.add))
                em("vector", lambda e: e.tensor_scalar(
                    out=t1[:, :], in0=redb[:, :], scalar1=PI, scalar2=None,
                    op0=ALU.is_gt))
                em("vector", lambda e: e.scalar_tensor_tensor(
                    out=redb[:, :], in0=t1[:, :], scalar=-TWO_PI, in1=redb[:, :],
                    op0=ALU.mult, op1=ALU.add))
                em("vector", lambda e: e.tensor_scalar(
                    out=t1[:, :], in0=redb[:, :], scalar1=-PI, scalar2=None,
                    op0=ALU.is_lt))
                em("vector", lambda e: e.scalar_tensor_tensor(
                    out=t1[:, :], in0=t1[:, :], scalar=TWO_PI, in1=redb[:, :],
                    op0=ALU.mult, op1=ALU.add))
                em("vector", lambda e: e.tensor_scalar(
                    out=redb[:, :], in0=t1[:, :], scalar1=HALF_PI, scalar2=None,
                    op0=ALU.add))
                em("vector", lambda e: e.tensor_scalar(
                    out=t2[:, :], in0=redb[:, :], scalar1=PI, scalar2=None,
                    op0=ALU.is_gt))
                c_wr = inc("vector", "ve", lambda e: e.scalar_tensor_tensor(
                    out=t2[:, :], in0=t2[:, :], scalar=-TWO_PI, in1=redb[:, :],
                    op0=ALU.mult, op1=ALU.add))
                wait("scalar", "ve", c_wr)
                em("scalar", lambda e, pc=pc: e.activation(
                    out=sintab[:, pc, :], in_=t1[:, :], func=AF.Sin))
                c_tab = inc("scalar", "sc", lambda e, pc=pc: e.activation(
                    out=costab[:, pc, :], in_=t2[:, :], func=AF.Sin))
                wait("vector", "sc", c_tab)

            wait("tensor", "ve", c_posb)
            for proj in range(2):
                is_q = (proj == 0)
                if proj == 1:
                    wait("tensor", "dWT", d_wqk)
                for pc in range(4):
                    for (ps, dcx) in ((qe_ps, pc), (qo_ps, pc + 4)):
                        for hc in range(HC):
                            for k in range(2):
                                fn = (lambda e, ps=ps, dcx=dcx, hc=hc, k=k,
                                      proj=proj, half=half: e.matmul(
                                          ps[:, k * 512:(k + 1) * 512],
                                          w_bf[:, proj, hc,
                                               dcx * 128:(dcx + 1) * 128],
                                          XN[:, hc, half * 1024 + k * 512:
                                             half * 1024 + (k + 1) * 512],
                                          start=(hc == 0), stop=(hc == HC - 1),
                                          skip_group_check=True))
                                if (ps is qo_ps) and hc == HC - 1 and k == 1:
                                    c_qk = inc("tensor", "pe", fn)
                                else:
                                    em("tensor", fn)
                    wait("vector", "pe", c_qk)
                    if not is_q:
                        wait("vector", "dKV", n_kr_dma)
                        wait("vector", "sc", n["sc"])
                    em("vector", lambda e, pc=pc: e.tensor_tensor(
                        out=t1[:, :], in0=qe_ps[:, :], in1=costab[:, pc, :],
                        op=ALU.mult))
                    em("vector", lambda e, pc=pc: e.tensor_tensor(
                        out=t2[:, :], in0=qo_ps[:, :], in1=sintab[:, pc, :],
                        op=ALU.mult))
                    if is_q:
                        em("vector", lambda e, pc=pc, half=half: e.tensor_tensor(
                            out=QR[:, pc, half * 1024:(half + 1) * 1024],
                            in0=t1[:, :], in1=t2[:, :], op=ALU.add))
                    else:
                        em("vector", lambda e: e.tensor_tensor(
                            out=krs[:, 0, :], in0=t1[:, :], in1=t2[:, :], op=ALU.add))
                    em("vector", lambda e, pc=pc: e.tensor_tensor(
                        out=t1[:, :], in0=qo_ps[:, :], in1=costab[:, pc, :],
                        op=ALU.mult))
                    em("vector", lambda e, pc=pc: e.tensor_tensor(
                        out=t2[:, :], in0=qe_ps[:, :], in1=sintab[:, pc, :],
                        op=ALU.mult))
                    if is_q:
                        c_rp = inc("vector", "ve", lambda e, pc=pc, half=half:
                                   e.tensor_tensor(
                                       out=QR[:, pc + 4,
                                              half * 1024:(half + 1) * 1024],
                                       in0=t1[:, :], in1=t2[:, :], op=ALU.subtract))
                    else:
                        c_rp = inc("vector", "ve", lambda e: e.tensor_tensor(
                            out=krs[:, 1, :], in0=t1[:, :], in1=t2[:, :],
                            op=ALU.subtract))
                    wait("tensor", "ve", c_rp)
                    if not is_q:
                        for pi, dcx in ((0, pc), (1, pc + 4)):
                            wait("sync", "ve", c_rp)
                            n_kr_dma = dma(
                                "sync",
                                kr_in[dcx * 128:(dcx + 1) * 128,
                                      half * 1024:(half + 1) * 1024],
                                krs[:, pi, :], "dKV")

        ps1b.close()
        wait("tensor", "ve", n["ve"])
        wait("gpsimd", "dKV", n["dKV"])
        n["cc"] += 1
        em("gpsimd", lambda e: e.collective_compute(
            "AllGather", ALU.bypass, replica_groups=[list(range(NCORE))],
            ins=[kr_in.ap()], outs=[kr_out.ap()]).then_inc(sem["cc"], 1))

        wait("gpsimd", "pe", n["pe"])
        for hc in range(HC):
            d_wv = dma("gpsimd", w_bf[:, 0, hc, :],
                       wv_t[hc * 128:(hc + 1) * 128, :], "dWT")

        ps1c = ExitStack()
        vps = [ps1c.enter_context(nc.psum_tensor(f"vps{i}", [128, 1024], F32))
               for i in range(2)]
        g_vev = [0, 0]
        g_vdm = [0, 0]
        wait("tensor", "dWT", d_wv)
        n_v_dma = n_kr_dma
        for t in range(R):
            sl = t % 2
            wait("tensor", "sc", g_vev[sl])
            for hc in range(HC):
                for k in range(2):
                    fn = (lambda e, sl=sl, hc=hc, k=k, t=t: e.matmul(
                        vps[sl][:, k * 512:(k + 1) * 512],
                        XN[:, hc, t * 128:(t + 1) * 128],
                        w_bf[:, 0, hc, k * 512:(k + 1) * 512],
                        start=(hc == 0), stop=(hc == HC - 1),
                        skip_group_check=True))
                    if hc == HC - 1 and k == 1:
                        c_vmm = inc("tensor", "pe", fn)
                    else:
                        em("tensor", fn)
            wait("scalar", "pe", c_vmm)
            wait("scalar", "dV0" if sl == 0 else "dV1", g_vdm[sl])
            g_vev[sl] = inc("scalar", "sc", lambda e, sl=sl: e.copy(
                out=vst[:, sl, :], in_=vps[sl][:, :]))
            wait("sync", "sc", g_vev[sl])
            g_vdm[sl] = n_v_dma = dma(
                "sync", v_in[t * 128:(t + 1) * 128, :], vst[:, sl, :],
                "dV0" if sl == 0 else "dV1")
        ps1c.close()
        wait("gpsimd", "dV0", n["dV0"])
        wait("gpsimd", "dV1", n["dV1"])
        n["cc"] += 1
        em("gpsimd", lambda e: e.collective_compute(
            "AllGather", ALU.bypass, replica_groups=[list(range(NCORE))],
            ins=[v_in.ap()], outs=[v_out.ap()]).then_inc(sem["cc"], 1))

        if debug:
            wait("gpsimd", "ve", n["ve"])
            wait("gpsimd", "sc", n["sc"])
            for hc in range(HC):
                dma("gpsimd", dbg_xn[hc * 128:(hc + 1) * 128, :], XN[:, hc, :], "dDBG")
            for pc in range(4):
                dma("gpsimd", dbg_cos[pc * 128:(pc + 1) * 128, :], costab[:, pc, :], "dDBG")
                dma("gpsimd", dbg_sin[pc * 128:(pc + 1) * 128, :], sintab[:, pc, :], "dDBG")
            wait("gpsimd", "dDBG", n["dDBG"])
        if debug and _PHASE == "qkv":
            with nc.sbuf_tensor("dbq0", [128, TOK], F32) as dbq0:
                for dc in range(HC):
                    c_d = inc("vector", "ve", lambda e, dc=dc: e.tensor_copy(
                        out=dbq0[:, :], in_=QR[:, dc, :]))
                    wait("gpsimd", "ve", c_d)
                    dD = dma("gpsimd", dbg_qr[dc * 128:(dc + 1) * 128, :],
                             dbq0[:, :], "dDBG")
                    wait("vector", "dDBG", dD)
        gate("qkv")


        p1.close()
        join()

        scope_o = ExitStack()
        Ea = scope_o.enter_context
        O_sb = Ea(nc.sbuf_tensor("O_sb", [128, R, 1024], F32))
        den_sb = Ea(nc.sbuf_tensor("den_sb", [128, 16], F32))
        rden = Ea(nc.sbuf_tensor("rden", [128, 16], F32))

        p3 = ExitStack()
        E3 = p3.enter_context
        kr_sb = E3(nc.sbuf_tensor("kr_sb", [128, 2, HC, 8, 128], BF16))
        v_sb = E3(nc.sbuf_tensor("v_sb", [128, 2, 8, 1024], BF16))
        msk_sb = E3(nc.sbuf_tensor("msk_sb", [128, 2, 8, 128], BF16))
        P_sb = E3(nc.sbuf_tensor("P_sb", [128, 2, 512], BF16))

        s_ps = [E3(nc.psum_tensor(f"s_ps{i}", [128, 512], F32)) for i in range(2)]
        o_ps = [E3(nc.psum_tensor(f"o_ps{i}", [128, 1024], F32)) for i in range(2)]
        den_ps = E3(nc.psum_tensor("den_ps", [128, 16], F32))

        def load_krm(b):
            sl = b % 2
            bc_ = "dB0" if sl == 0 else "dB1"
            for m in range(8):
                src = bass.AP(kr_out, 128 * b + m * (1024 * TOK),
                              [[TOK, 128], [128 * TOK, 8], [1, 128]])
                dma("sync", kr_sb[:, sl, :, m, :], src, bc_)
            src = bass.AP(dmask, b * (8 * 128 * 128),
                          [[128, 128], [128 * 128, 8], [1, 128]])
            return dma("sync", msk_sb[:, sl, :, :], src, bc_)

        def load_v(b):
            sl = b % 2
            bc_ = "dB0" if sl == 0 else "dB1"
            src = bass.AP(v_out, 128 * b * D,
                          [[D, 128], [TOK * D, 8], [1, D]])
            return dma("sync", v_sb[:, sl, :, :], src, bc_)

        def load_block(b):
            d_krm[b] = load_krm(b)
            d_blk[b] = load_v(b)

        d_krm = {}
        d_blk = {}
        wait("sync", "cc", 2)
        d_krm[0] = load_krm(0)
        d_krm[1] = load_krm(1)
        wait("sync", "cc", 3)
        d_blk[0] = load_v(0)
        d_blk[1] = load_v(1)
        pe_blk_end = {}
        ve_blk_end = {}

        c_evac_prev = 0
        last_exp = {0: 0, 1: 0}
        last_pv = {0: 0, 1: 0}
        for b in range(16):
            sl = b % 2
            ntile = 16 - b
            qlo = 128 * b
            chunks = []
            q = qlo
            if ntile % 2 == 1:
                chunks.append((q, 128))
                q += 128
            while q < TOK:
                chunks.append((q, 256))
                q += 256

            wait("tensor", "dB0" if b % 2 == 0 else "dB1", d_krm[b])
            v_waited = False
            for (qoff, qw) in chunks:
                nqs = qw // 128
                diag = (qoff == qlo)
                for pair in range(4):
                    par = pair % 2
                    sps = s_ps[par]
                    wait("tensor", "sc", last_exp[par])
                    for kk in range(2):
                        ks = 2 * pair + kk
                        for dc in range(HC):
                            fn = (lambda e, sps=sps, kk=kk, ks=ks, dc=dc, sl=sl,
                                  qoff=qoff, qw=qw: e.matmul(
                                      sps[:, kk * qw:(kk + 1) * qw],
                                      kr_sb[:, sl, dc, ks, :],
                                      QR[:, dc, qoff:qoff + qw],
                                      start=(dc == 0), stop=(dc == HC - 1),
                                      skip_group_check=True))
                            if kk == 1 and dc == HC - 1:
                                c_s = inc("tensor", "pe", fn)
                            else:
                                em("tensor", fn)
                    wait("scalar", "pe", c_s)
                    wait("scalar", "pe", last_pv[par])
                    last_exp[par] = inc(
                        "scalar", "sc",
                        lambda e, sps=sps, par=par, qw=qw: e.activation(
                            out=P_sb[:, par, 0:2 * qw], in_=sps[:, 0:2 * qw],
                            func=AF.Exp, bias=negB[:, 0:1], scale=1.0 / 32.0))
                    mask_mark = None
                    if diag:
                        wait("vector", "sc", last_exp[par])
                        for kk in range(2):
                            ks = 2 * pair + kk
                            fn = (lambda e, par=par, kk=kk, ks=ks, sl=sl, qw=qw:
                                  e.tensor_tensor(
                                      out=P_sb[:, par, kk * qw:kk * qw + 128],
                                      in0=P_sb[:, par, kk * qw:kk * qw + 128],
                                      in1=msk_sb[:, sl, ks, :], op=ALU.mult))
                            if kk == 1:
                                mask_mark = inc("vector", "ve", fn)
                            else:
                                em("vector", fn)
                    if not v_waited:
                        wait("tensor", "dB0" if b % 2 == 0 else "dB1", d_blk[b])
                        v_waited = True
                    wait("tensor", "sc", last_exp[par])
                    if mask_mark is not None:
                        wait("tensor", "ve", mask_mark)
                    if pair == 0:
                        wait("tensor", "ve", c_evac_prev)
                    for kk in range(2):
                        ks = 2 * pair + kk
                        for qs in range(nqs):
                            lo = kk * qw + qs * 128
                            st0 = (pair == 0 and kk == 0)
                            for dh in range(2):
                                fn = (lambda e, par=par, ks=ks, qs=qs,
                                      lo=lo, dh=dh, sl=sl, st0=st0: e.matmul(
                                          o_ps[qs][:, dh * 512:(dh + 1) * 512],
                                          P_sb[:, par, lo:lo + 128],
                                          v_sb[:, sl, ks, dh * 512:(dh + 1) * 512],
                                          start=st0, stop=True,
                                          skip_group_check=True))
                                em("tensor", fn)
                            fn = (lambda e, par=par, qs=qs, lo=lo,
                                  st0=st0: e.matmul(
                                      den_ps[:, qs:qs + 1],
                                      P_sb[:, par, lo:lo + 128],
                                      ones_col[:, 0:1],
                                      start=(st0 and qs == 0), stop=True,
                                      skip_group_check=True))
                            if kk == 1 and qs == nqs - 1:
                                last_pv[par] = c_pv_last = inc("tensor", "pe", fn)
                            else:
                                em("tensor", fn)
                wait("vector", "pe", c_pv_last)
                for qs in range(nqs):
                    tidx = qoff // 128 + qs
                    if b == 0:
                        em("vector", lambda e, qs=qs, tidx=tidx: e.tensor_copy(
                            out=O_sb[:, tidx, :], in_=o_ps[qs][:, :]))
                        fn = (lambda e, qs=qs, tidx=tidx: e.tensor_copy(
                            out=den_sb[:, tidx:tidx + 1], in_=den_ps[:, qs:qs + 1]))
                    else:
                        em("vector", lambda e, qs=qs, tidx=tidx: e.tensor_tensor(
                            out=O_sb[:, tidx, :], in0=O_sb[:, tidx, :],
                            in1=o_ps[qs][:, :], op=ALU.add))
                        fn = (lambda e, qs=qs, tidx=tidx: e.tensor_tensor(
                            out=den_sb[:, tidx:tidx + 1],
                            in0=den_sb[:, tidx:tidx + 1],
                            in1=den_ps[:, qs:qs + 1], op=ALU.add))
                    if qs == nqs - 1:
                        c_evac_prev = inc("vector", "ve", fn)
                    else:
                        em("vector", fn)
            pe_blk_end[b] = n["pe"]
            ve_blk_end[b] = n["ve"]
            if b + 2 <= 15:
                wait("sync", "pe", pe_blk_end[b])
                wait("sync", "ve", ve_blk_end[b])
                load_block(b + 2)

        p3.close()
        join()
        if debug and _PHASE == "attn":
            dma("gpsimd", dbg_qr[:, :],
                v_out[0:2048, :].rearrange("(a b) d -> a (b d)", b=2), "dDBG")
            dma("gpsimd", dbg_xn[:, :],
                v_in[:, :].rearrange("(a b) d -> a (b d)", b=2), "dDBG")
            with nc.sbuf_tensor("dbo0", [128, 1024], F32) as dbo0:
                dma("gpsimd", dbg_den[:, :], den_sb[:, :], "dDBG")
                for t in range(R):
                    c_d = inc("vector", "ve", lambda e, t=t: e.tensor_copy(
                        out=dbo0[:, :], in_=O_sb[:, t, :]))
                    wait("gpsimd", "ve", c_d)
                    dD = dma("gpsimd", dbg_osb[:, t * 1024:(t + 1) * 1024],
                             dbo0[:, :], "dDBG")
                    wait("vector", "dDBG", dD)
        gate("attn")

        xmid = E(nc.sbuf_tensor("xmid", [128, R, 1024], F32, side="right"))
        p4 = ExitStack()
        E4 = p4.enter_context
        xrl = E4(nc.sbuf_tensor("xrl", [128, 2, 1024], F32))
        t3 = E4(nc.sbuf_tensor("t3", [128, 2, 1024], F32))

        c_rd = inc("vector", "ve", lambda e: e.reciprocal(rden[:, :], den_sb[:, :]))
        wait("scalar", "ve", c_rd)
        g_fx = [0, 0]
        g_t3 = [0, 0]
        for t in range(R):
            sl = t % 2
            wait("sync", "ve", g_fx[sl])
            xc = "dX0" if sl == 0 else "dX1"
            d_x = dma("sync", xrl[:, sl, :], x_own[t, :, :], xc)
            wait("scalar", "ve", g_t3[sl])
            c_sc = inc("scalar", "sc", lambda e, t=t, sl=sl: e.activation(
                out=t3[:, sl, :], in_=O_sb[:, t, :], func=AF.Identity,
                scale=rden[:, t:t + 1]))
            wait("vector", xc, d_x)
            wait("vector", "sc", c_sc)
            g_t3[sl] = g_fx[sl] = inc(
                "vector", "ve", lambda e, t=t, sl=sl: e.tensor_tensor(
                    out=xmid[:, t, :], in0=t3[:, sl, :], in1=xrl[:, sl, :],
                    op=ALU.add))
        if debug:
            with nc.sbuf_tensor("dbq", [128, TOK], F32) as dbq, \
                 nc.sbuf_tensor("dbo", [128, 1024], F32) as dbo:
                for dc in range(HC):
                    c_d = inc("vector", "ve", lambda e, dc=dc: e.tensor_copy(
                        out=dbq[:, :], in_=QR[:, dc, :]))
                    wait("gpsimd", "ve", c_d)
                    dD = dma("gpsimd", dbg_qr[dc * 128:(dc + 1) * 128, :],
                             dbq[:, :], "dDBG")
                    wait("vector", "dDBG", dD)
                dma("gpsimd", dbg_sml[:, :], sml[:, :], "dDBG")
                dma("gpsimd", dbg_negb[:, :], negB[:, :], "dDBG")
                dD = dma("gpsimd", dbg_den[:, :], den_sb[:, :], "dDBG")
                for t in range(R):
                    c_d = inc("vector", "ve", lambda e, t=t: e.tensor_copy(
                        out=dbo[:, :], in_=O_sb[:, t, :]))
                    wait("gpsimd", "ve", c_d)
                    dD = dma("gpsimd", dbg_osb[:, t * 1024:(t + 1) * 1024],
                             dbo[:, :], "dDBG")
                    wait("vector", "dDBG", dD)
                dD = dma("gpsimd", dbg_xmid[:, :],
                         xmid[:, :, :].rearrange("p a b -> p (a b)"), "dDBG")
                wait("gpsimd", "dDBG", dD)

        p4.close()
        scope_o.close()
        join()
        gate("xmid")

        p5 = ExitStack()
        E5 = p5.enter_context
        wsl = E5(nc.sbuf_tensor("wsl", [128, 2, IFF], F32))
        vvb = E5(nc.sbuf_tensor("vvb", [128, IFF], F32))
        ffb = E5(nc.sbuf_tensor("ffb", [128, IFF], F32))
        intsl = E5(nc.sbuf_tensor("intsl", [128, 2, IFF], BF16))
        stg5 = E5(nc.sbuf_tensor("stg5", [128, 2, 22, 128], BF16))
        srow = E5(nc.sbuf_tensor("srow", [1, IFF], F32))
        wqi = E5(nc.sbuf_tensor("wqi", [128, IFF], mybir.dt.int32))
        tmpq = E5(nc.sbuf_tensor("tmpq", [128, 4], F32))

        ps5 = ExitStack()
        tps = [ps5.enter_context(nc.psum_tensor(f"tps{i}", [128, 512], BF16))
               for i in range(2)]
        tpf = ps5.enter_context(nc.psum_tensor("tpf", [128, 128], F32))

        g_w = [0, 0]
        g_int = [0, 0]
        g_stW = [0, 0]
        g_tp = [0, 0]
        for (mat, dst, nch, wdt, scr, nblk, pstr) in (
                (wg_ih, mg_t, 22, D, ig_scr, HC, IFF),
                (wu_ih, mu_t, 22, D, iu_scr, HC, IFF),
                (wd_oi, md_t, 8, IFF, id_scr, CI, D)):
            for c in range(nch):
                slw = c % 2
                pc_ = "dP0" if slw == 0 else "dP1"
                ic_ = "dI0" if slw == 0 else "dI1"
                wait("sync", "ve", g_w[slw])
                d_w = dma("sync", wsl[:, slw, 0:wdt],
                          mat[c * 128:(c + 1) * 128, :], pc_)
                wait("vector", pc_, d_w)
                em("vector", lambda e, dst=dst, c=c, slw=slw, wdt=wdt:
                   e.reduce_max(out=dst[:, c:c + 1], in_=wsl[:, slw, 0:wdt],
                                axis=AX, apply_absolute_value=True))
                vpad()
                em("vector", lambda e, dst=dst, c=c, slw=slw: e.tensor_scalar(
                    out=tmpq[:, 2 * slw:2 * slw + 1], in0=dst[:, c:c + 1],
                    scalar1=1.0 / 127.0, scalar2=None, op0=ALU.mult))
                vpad()
                em("vector", lambda e, slw=slw: e.reciprocal(
                    tmpq[:, 2 * slw + 1:2 * slw + 2],
                    tmpq[:, 2 * slw:2 * slw + 1]))
                vpad()
                g_w[slw] = inc("vector", "ve", lambda e, slw=slw, wdt=wdt:
                               e.tensor_scalar(
                                   out=vvb[:, 0:wdt], in0=wsl[:, slw, 0:wdt],
                                   scalar1=tmpq[:, 2 * slw + 1:2 * slw + 2],
                                   scalar2=None, op0=ALU.mult))
                em("vector", lambda e, wdt=wdt: e.tensor_copy(
                    out=wqi[:, 0:wdt], in_=vvb[:, 0:wdt]))
                em("vector", lambda e, wdt=wdt: e.tensor_copy(
                    out=ffb[:, 0:wdt], in_=wqi[:, 0:wdt]))
                em("vector", lambda e, wdt=wdt: e.tensor_tensor(
                    out=vvb[:, 0:wdt], in0=vvb[:, 0:wdt], in1=ffb[:, 0:wdt],
                    op=ALU.subtract))
                em("vector", lambda e, wdt=wdt: e.scalar_tensor_tensor(
                    out=ffb[:, 0:wdt], in0=vvb[:, 0:wdt], scalar=0.5,
                    in1=ffb[:, 0:wdt], op0=ALU.is_ge, op1=ALU.add))
                em("vector", lambda e, wdt=wdt: e.tensor_scalar(
                    out=vvb[:, 0:wdt], in0=vvb[:, 0:wdt], scalar1=-0.5,
                    scalar2=None, op0=ALU.is_lt))
                wait("vector", "pe", g_int[slw])
                c_q = inc("vector", "ve", lambda e, slw=slw, wdt=wdt:
                          e.tensor_tensor(
                              out=intsl[:, slw, 0:wdt], in0=ffb[:, 0:wdt],
                              in1=vvb[:, 0:wdt], op=ALU.subtract))
                wait("tensor", "ve", c_q)
                first_cp = True
                for g0 in range(0, nblk, 4):
                    gw = min(4, nblk - g0)
                    pb = (g0 // 4) % 2
                    wait("tensor", "sc", g_tp[pb])
                    for k in range(gw):
                        fn = (lambda e, pb=pb, k=k, g0=g0, slw=slw: e.transpose(
                            tps[pb][:, k * 128:(k + 1) * 128],
                            intsl[:, slw, (g0 + k) * 128:(g0 + k + 1) * 128],
                            ident_bs[:, :]))
                        if k == gw - 1:
                            c_tp = inc("tensor", "pe", fn)
                        else:
                            em("tensor", fn)
                    if g0 + 4 >= nblk:
                        g_int[slw] = c_tp
                    wait("scalar", "pe", c_tp)
                    if first_cp:
                        wait("scalar", ic_, g_stW[slw])
                        first_cp = False
                    g_tp[pb] = inc("scalar", "sc",
                                   lambda e, pb=pb, g0=g0, gw=gw, slw=slw:
                                   e.copy(out=stg5[:, slw, g0:g0 + gw, :],
                                          in_=tps[pb][:, 0:gw * 128].rearrange(
                                              "p (a b) -> p a b", a=gw)))
                wait("sync", "sc", g_tp[(max(0, nblk - 4) // 4) % 2])
                dstap = bass.AP(scr, c * 128,
                                [[pstr, 128], [128 * pstr, nblk], [1, 128]])
                g_stW[slw] = dma("sync", dstap, stg5[:, slw, 0:nblk, :], ic_)

            if dst is mu_t:
                em("vector", lambda e: e.tensor_scalar(
                    out=sg_f[:, :], in0=mg_t[:, 0:22], scalar1=1.0 / 127.0,
                    scalar2=None, op0=ALU.mult))
                c_fold = inc("vector", "ve", lambda e: e.tensor_scalar(
                    out=su_f[:, :], in0=mu_t[:, 0:22], scalar1=1.0 / 127.0,
                    scalar2=None, op0=ALU.mult))

        wait("tensor", "sc", n["sc"])
        wait("tensor", "ve", n["ve"])
        cpe = inc("tensor", "pe", lambda e: e.transpose(
            tpf[:, :], md_t[:, :], ident_fs[:, :]))
        wait("scalar", "pe", cpe)
        c_gth = inc("scalar", "sc", lambda e: e.copy(
            out=gath[:, :], in_=tpf[:, :]))
        wait("sync", "sc", c_gth)
        d_row = dma("sync", srow[0:1, 0:1024], gath[0:8, 0:128], "dRW")
        wait("vector", "dRW", d_row)
        c_sdr = inc("vector", "ve", lambda e: e.tensor_scalar(
            out=sdrow[0:1, 0:D], in0=srow[0:1, 0:D], scalar1=1.0 / 127.0,
            scalar2=None, op0=ALU.mult))
        ps5.close()
        p5.close()
        join()
        gate("wprep")

        p6 = ExitStack()
        E6 = p6.enter_context
        hq = E6(nc.sbuf_tensor("hq", [128, CI, TOK], F16))
        s6ab = ExitStack()
        tq1 = s6ab.enter_context(nc.sbuf_tensor("tq1", [128, 1024], F32))
        tq2 = s6ab.enter_context(nc.sbuf_tensor("tq2", [128, 1024], F32))
        xqt = s6ab.enter_context(nc.sbuf_tensor("xqt", [128, 1024], BF16))
        xqi = s6ab.enter_context(nc.sbuf_tensor("xqi", [128, 1024], mybir.dt.int32))

        def hqb(ci, lo, hi):
            return hq[:, ci, lo:hi].bitcast(BF16)

        ps6a = ExitStack()
        tpf6 = ps6a.enter_context(nc.psum_tensor("tpf6", [128, 128], F32))
        for t in range(R):
            c_sq = inc("scalar", "sc", lambda e, t=t: e.activation(
                out=tq1[:, :], in_=xmid[:, t, :], func=AF.Square))
            wait("vector", "sc", c_sq)
            em("vector", lambda e, t=t: e.reduce_sum(
                lnm[:, t:t + 1], xmid[:, t, :], axis=AX))
            em("vector", lambda e, t=t: e.reduce_sum(
                lnr[:, t:t + 1], tq1[:, :], axis=AX))
            em("vector", lambda e, t=t: e.tensor_scalar(
                out=lnm[:, t:t + 1], in0=lnm[:, t:t + 1], scalar1=1.0 / D,
                scalar2=None, op0=ALU.mult))
            em("vector", lambda e, t=t: e.tensor_scalar(
                out=lnr[:, t:t + 1], in0=lnr[:, t:t + 1], scalar1=1.0 / D,
                scalar2=None, op0=ALU.mult))
            em("vector", lambda e, t=t: e.scalar_tensor_tensor(
                out=bc1[:, 4:5], in0=lnm[:, t:t + 1], scalar=-1.0,
                in1=lnm[:, t:t + 1], op0=ALU.mult, op1=ALU.mult))
            vpad()
            c_var = inc("vector", "ve", lambda e, t=t: e.tensor_tensor(
                out=lnr[:, t:t + 1], in0=lnr[:, t:t + 1], in1=bc1[:, 4:5],
                op=ALU.add))
            wait("scalar", "ve", c_var)
            c_rt = inc("scalar", "sc", lambda e, t=t: e.activation(
                out=bc1[:, 5:6], in_=lnr[:, t:t + 1], func=AF.Sqrt,
                bias=epsb[:, 0:1]))
            wait("vector", "sc", c_rt)
            em("vector", lambda e: e.reciprocal(bc1[:, 6:7], bc1[:, 5:6]))
            em("vector", lambda e, t=t: e.tensor_scalar(
                out=bc1[:, 7:8], in0=lnr[:, t:t + 1], scalar1=EPS, scalar2=-0.5,
                op0=ALU.add, op1=ALU.mult))
            em("vector", lambda e: e.tensor_tensor(
                out=bc1[:, 4:5], in0=bc1[:, 6:7], in1=bc1[:, 6:7], op=ALU.mult))
            vpad()
            em("vector", lambda e: e.tensor_tensor(
                out=bc1[:, 4:5], in0=bc1[:, 4:5], in1=bc1[:, 7:8], op=ALU.mult))
            vpad()
            em("vector", lambda e: e.tensor_scalar(
                out=bc1[:, 4:5], in0=bc1[:, 4:5], scalar1=1.5, scalar2=None,
                op0=ALU.add))
            vpad()
            em("vector", lambda e, t=t: e.tensor_tensor(
                out=lnr[:, t:t + 1], in0=bc1[:, 6:7], in1=bc1[:, 4:5],
                op=ALU.mult))
            em("vector", lambda e, t=t: e.tensor_reduce(
                out=bc1[:, 4:5], in_=xmid[:, t, :], axis=AX, op=ALU.max))
            em("vector", lambda e, t=t: e.tensor_reduce(
                out=bc1[:, 5:6], in_=xmid[:, t, :], axis=AX, op=ALU.min))
            em("vector", lambda e, t=t: e.tensor_tensor(
                out=bc1[:, 4:5], in0=bc1[:, 4:5], in1=lnm[:, t:t + 1],
                op=ALU.subtract))
            em("vector", lambda e, t=t: e.tensor_tensor(
                out=bc1[:, 5:6], in0=lnm[:, t:t + 1], in1=bc1[:, 5:6],
                op=ALU.subtract))
            vpad()
            em("vector", lambda e: e.tensor_tensor(
                out=bc1[:, 4:5], in0=bc1[:, 4:5], in1=bc1[:, 5:6], op=ALU.max))
            vpad()
            em("vector", lambda e, t=t: e.tensor_tensor(
                out=bc1[:, 4:5], in0=bc1[:, 4:5], in1=lnr[:, t:t + 1],
                op=ALU.mult))
            vpad()
            if t == 0:
                em("vector", lambda e: e.tensor_copy(out=hmax[:, :],
                                                     in_=bc1[:, 4:5]))
            else:
                em("vector", lambda e: e.tensor_tensor(
                    out=hmax[:, :], in0=hmax[:, :], in1=bc1[:, 4:5], op=ALU.max))
        pmax_to_sml(hmax[:, 0:1], 11, tpf6)
        c_m1 = inc("vector", "ve", lambda e: e.tensor_copy(
            out=stage16[0:1, 0:1], in_=sml[0:1, 11:12]))
        wait("gpsimd", "ve", c_m1)
        d_m1 = dma("gpsimd", m1_in[:, :], stage16[:, :], "dCC")
        wait("gpsimd", "dCC", d_m1)
        n["cc"] += 1
        em("gpsimd", lambda e: e.collective_compute(
            "AllReduce", ALU.max, replica_groups=[list(range(NCORE))],
            ins=[m1_in.ap()], outs=[m1_out.ap()]).then_inc(sem["cc"], 1))
        wait("gpsimd", "cc", n["cc"])
        d_m1o = dma("gpsimd", stage16[:, :], m1_out[:, :], "dCC")
        wait("scalar", "dCC", d_m1o)
        c_sg1 = inc("scalar", "sc", lambda e: e.activation(
            out=sml[0:1, 10:11], in_=clips_sb[0:1, 0:1], func=AF.Sigmoid))
        wait("vector", "sc", c_sg1)
        em("vector", lambda e: e.scalar_tensor_tensor(
            out=sml[0:1, 5:6], in0=sml[0:1, 10:11], scalar=1.0 / 127.0,
            in1=stage16[0:1, 0:1], op0=ALU.mult, op1=ALU.mult))
        vpad()
        c_s1 = inc("vector", "ve", lambda e: e.reciprocal(
            sml[0:1, 6:7], sml[0:1, 5:6]))
        wait("tensor", "ve", c_s1)
        bcast(sml[0:1, 6:7], 0, tpf6)
        c_bc1 = bcast(sml[0:1, 5:6], 1, tpf6)
        wait("vector", "sc", c_bc1)
        em("vector", lambda e: e.tensor_scalar(
            out=s1sg[:, :], in0=sg_f[:, :], scalar1=bc1[:, 1:2], scalar2=None,
            op0=ALU.mult))
        c_s1f = inc("vector", "ve", lambda e: e.tensor_scalar(
            out=s1su[:, :], in0=su_f[:, :], scalar1=bc1[:, 1:2], scalar2=None,
            op0=ALU.mult))
        ps6a.close()

        ps6b = ExitStack()
        trb = ps6b.enter_context(nc.psum_tensor("trb", [128, 512], BF16))
        trb2 = ps6b.enter_context(nc.psum_tensor("trb2", [128, 512], BF16))
        g_tr6 = [0, 0]
        g_xq = 0
        for t in range(R):
            em("vector", lambda e, t=t: e.tensor_scalar(
                out=bc1[:, 4:5], in0=lnr[:, t:t + 1], scalar1=bc1[:, 0:1],
                scalar2=None, op0=ALU.mult))
            vpad()
            c_ab = inc("vector", "ve", lambda e, t=t: e.scalar_tensor_tensor(
                out=bc1[:, 5:6], in0=lnm[:, t:t + 1], scalar=-1.0,
                in1=bc1[:, 4:5], op0=ALU.mult, op1=ALU.mult))
            wait("scalar", "ve", c_ab)
            c_v = inc("scalar", "sc", lambda e, t=t: e.activation(
                out=tq1[:, :], in_=xmid[:, t, :], func=AF.Identity,
                bias=bc1[:, 5:6], scale=bc1[:, 4:5]))
            wait("vector", "sc", c_v)
            em("vector", lambda e: e.tensor_copy(out=xqi[:, :], in_=tq1[:, :]))
            em("vector", lambda e: e.tensor_copy(out=tq2[:, :], in_=xqi[:, :]))
            em("vector", lambda e: e.tensor_tensor(
                out=tq1[:, :], in0=tq1[:, :], in1=tq2[:, :], op=ALU.subtract))
            em("vector", lambda e: e.scalar_tensor_tensor(
                out=tq2[:, :], in0=tq1[:, :], scalar=0.5, in1=tq2[:, :],
                op0=ALU.is_ge, op1=ALU.add))
            em("vector", lambda e: e.tensor_scalar(
                out=tq1[:, :], in0=tq1[:, :], scalar1=-0.5, scalar2=None,
                op0=ALU.is_lt))
            em("vector", lambda e: e.tensor_tensor(
                out=tq1[:, :], in0=tq2[:, :], in1=tq1[:, :], op=ALU.subtract))
            wait("vector", "pe", g_xq)
            c_xq = inc("vector", "ve", lambda e: e.tensor_scalar(
                out=xqt[:, :], in0=tq1[:, :], scalar1=127.0, scalar2=-128.0,
                op0=ALU.min, op1=ALU.max))
            wait("tensor", "ve", c_xq)
            for g in range(2):
                tp = trb if g == 0 else trb2
                wait("tensor", "sc", g_tr6[g])
                for k4 in range(4):
                    hc = g * 4 + k4
                    fn = (lambda e, tp=tp, k4=k4, hc=hc: e.transpose(
                        tp[:, k4 * 128:(k4 + 1) * 128],
                        xqt[:, hc * 128:(hc + 1) * 128], ident_bs[:, :]))
                    if k4 == 3:
                        g_xq = inc("tensor", "pe", fn)
                    else:
                        em("tensor", fn)
                wait("scalar", "pe", g_xq)
                g_tr6[g] = inc("scalar", "sc", lambda e, tp=tp, g=g, t=t: e.copy(
                    out=QR[:, g * 4:(g + 1) * 4, t * 128:(t + 1) * 128],
                    in_=tp[:, :].rearrange("p (a b) -> p a b", a=4)))
        ps6b.close()
        s6ab.close()
        join()
        if debug and _PHASE == "xq":
            with nc.sbuf_tensor("dbq6", [128, TOK], F32) as dbq6:
                dma("gpsimd", dbg_sml[:, :], sml[:, :], "dDBG")
                for dc in range(HC):
                    c_d = inc("vector", "ve", lambda e, dc=dc: e.tensor_copy(
                        out=dbq6[:, :], in_=QR[:, dc, :]))
                    wait("gpsimd", "ve", c_d)
                    dD = dma("gpsimd", dbg_qr[dc * 128:(dc + 1) * 128, :],
                             dbq6[:, :], "dDBG")
                    wait("vector", "dDBG", dD)
        gate("xq")

        ps6c = ExitStack()
        g_psm = [ps6c.enter_context(nc.psum_tensor(f"g_ps{i}", [128, 512], F32))
                 for i in range(2)]
        u_psm = [ps6c.enter_context(nc.psum_tensor(f"u_ps{i}", [128, 512], F32))
                 for i in range(2)]
        igt = ps6c.enter_context(nc.sbuf_tensor("igt", [128, 2, HC, 128], BF16))
        iut = ps6c.enter_context(nc.sbuf_tensor("iut", [128, 2, HC, 128], BF16))
        sgs = ps6c.enter_context(nc.sbuf_tensor("sgs", [128, 2, 512], F32))

        g_slab = [0, 0]
        g_gu = [0, 0]
        g_sgs = [0, 0]
        wait("scalar", "ve", c_s1f)
        ii = 0
        for ci in range(CI):
            sl2 = ci % 2
            wait("sync", "pe", g_slab[sl2])
            src = bass.AP(ig_scr, ci * 128,
                          [[IFF, 128], [128 * IFF, HC], [1, 128]])
            gc_ = "dIG0" if sl2 == 0 else "dIG1"
            dma("sync", igt[:, sl2, :, :], src, gc_)
            src = bass.AP(iu_scr, ci * 128,
                          [[IFF, 128], [128 * IFF, HC], [1, 128]])
            d_slab = dma("sync", iut[:, sl2, :, :], src, gc_)
            wait("tensor", gc_, d_slab)
            wait("tensor", "ve", c_s1f)
            for tt4 in range(4):
                sl = ii % 2
                ii += 1
                wait("tensor", "ve", g_gu[sl])
                for (psm, wt) in ((g_psm, igt), (u_psm, iut)):
                    for hc in range(HC):
                        fn = (lambda e, psm=psm, wt=wt, hc=hc, sl=sl, sl2=sl2,
                              tt4=tt4: e.matmul(
                                  psm[sl][:, :], wt[:, sl2, hc, :],
                                  QR[:, hc, tt4 * 512:(tt4 + 1) * 512],
                                  start=(hc == 0), stop=(hc == HC - 1),
                                  skip_group_check=True))
                        if psm is u_psm and hc == HC - 1:
                            c_gu = inc("tensor", "pe", fn)
                        else:
                            em("tensor", fn)
                if tt4 == 3:
                    g_slab[sl2] = c_gu
                wait("scalar", "pe", c_gu)
                wait("scalar", "ve", g_sgs[sl])
                c_silu = inc("scalar", "sc", lambda e, sl=sl, ci=ci: e.activation(
                    out=sgs[:, sl, :], in_=g_psm[sl][:, :], func=AF.Silu,
                    scale=s1sg[:, ci:ci + 1]))
                wait("vector", "sc", c_silu)
                em("vector", lambda e, sl=sl, ci=ci, tt4=tt4: e.scalar_tensor_tensor(
                    out=hq[:, ci, tt4 * 512:(tt4 + 1) * 512], in0=u_psm[sl][:, :],
                    scalar=s1su[:, ci:ci + 1], in1=sgs[:, sl, :], op0=ALU.mult,
                    op1=ALU.mult))
                g_sgs[sl] = g_gu[sl] = c_h = inc(
                    "vector", "ve",
                    lambda e, ci=ci, tt4=tt4: e.reduce_max(
                        out=bc1[:, 4:5], in_=hq[:, ci, tt4 * 512:(tt4 + 1) * 512],
                        axis=AX, apply_absolute_value=True))
                vpad()
                if ci == 0 and tt4 == 0:
                    em("vector", lambda e: e.tensor_copy(out=hmax[:, :],
                                                         in_=bc1[:, 4:5]))
                else:
                    em("vector", lambda e: e.tensor_tensor(
                        out=hmax[:, :], in0=hmax[:, :], in1=bc1[:, 4:5],
                        op=ALU.max))
        ps6c.close()

        s6d = ExitStack()
        tq1 = s6d.enter_context(nc.sbuf_tensor("tq1b", [128, 1024], F32))
        tq2 = s6d.enter_context(nc.sbuf_tensor("tq2b", [128, 1024], F32))
        hqi = s6d.enter_context(nc.sbuf_tensor("hqi", [128, 1024], mybir.dt.int32))
        ps6d = ExitStack()
        tpf7 = ps6d.enter_context(nc.psum_tensor("tpf7", [128, 128], F32))
        bcp7 = ps6d.enter_context(nc.psum_tensor("bcp7", [128, 512], F32))
        pmax_to_sml(hmax[:, 0:1], 11, tpf7)
        c_m2 = inc("vector", "ve", lambda e: e.tensor_copy(
            out=stage16[0:1, 0:1], in_=sml[0:1, 11:12]))
        wait("gpsimd", "ve", c_m2)
        d_m2 = dma("gpsimd", m2_in[:, :], stage16[:, :], "dCC")
        wait("gpsimd", "dCC", d_m2)
        n["cc"] += 1
        em("gpsimd", lambda e: e.collective_compute(
            "AllReduce", ALU.max, replica_groups=[list(range(NCORE))],
            ins=[m2_in.ap()], outs=[m2_out.ap()]).then_inc(sem["cc"], 1))
        wait("gpsimd", "cc", n["cc"])
        d_m2o = dma("gpsimd", stage16[:, :], m2_out[:, :], "dCC")
        wait("scalar", "dCC", d_m2o)
        c_sg2 = inc("scalar", "sc", lambda e: e.activation(
            out=sml[0:1, 10:11], in_=clips_sb[0:1, 1:2], func=AF.Sigmoid))
        wait("vector", "sc", c_sg2)
        em("vector", lambda e: e.scalar_tensor_tensor(
            out=sml[0:1, 8:9], in0=sml[0:1, 10:11], scalar=1.0 / 127.0,
            in1=stage16[0:1, 0:1], op0=ALU.mult, op1=ALU.mult))
        vpad()
        c_s2 = inc("vector", "ve", lambda e: e.reciprocal(
            sml[0:1, 9:10], sml[0:1, 8:9]))
        wait("tensor", "ve", c_s2)
        bcast(sml[0:1, 9:10], 2, tpf7)
        c_sd1 = inc("vector", "ve", lambda e: e.tensor_scalar(
            out=sdrow[0:1, :], in0=sdrow[0:1, :], scalar1=sml[0:1, 8:9],
            scalar2=None, op0=ALU.mult))
        ps6d.close()

        wait("vector", "sc", n["sc"])
        for ci in range(CI):
            for th in range(2):
                lo, hi = th * 1024, (th + 1) * 1024
                em("vector", lambda e, ci=ci, lo=lo, hi=hi: e.tensor_scalar(
                    out=tq1[:, :], in0=hq[:, ci, lo:hi], scalar1=bc1[:, 2:3],
                    scalar2=None, op0=ALU.mult))
                em("vector", lambda e: e.tensor_copy(out=hqi[:, :], in_=tq1[:, :]))
                em("vector", lambda e: e.tensor_copy(out=tq2[:, :], in_=hqi[:, :]))
                em("vector", lambda e: e.tensor_tensor(
                    out=tq1[:, :], in0=tq1[:, :], in1=tq2[:, :], op=ALU.subtract))
                em("vector", lambda e: e.scalar_tensor_tensor(
                    out=tq2[:, :], in0=tq1[:, :], scalar=0.5, in1=tq2[:, :],
                    op0=ALU.is_ge, op1=ALU.add))
                em("vector", lambda e: e.tensor_scalar(
                    out=tq1[:, :], in0=tq1[:, :], scalar1=-0.5, scalar2=None,
                    op0=ALU.is_lt))
                em("vector", lambda e: e.tensor_tensor(
                    out=tq1[:, :], in0=tq2[:, :], in1=tq1[:, :], op=ALU.subtract))
                em("vector", lambda e, ci=ci, lo=lo, hi=hi: e.tensor_scalar(
                    out=hqb(ci, lo, hi), in0=tq1[:, :], scalar1=127.0,
                    scalar2=-128.0, op0=ALU.min, op1=ALU.max))
        if debug:
            for g0 in range(0, CI, 8):
                g1 = min(g0 + 8, CI)
                for ci in range(g0, g1):
                    em("vector", lambda e, ci=ci, g0=g0: e.tensor_copy(
                        out=tq1[:, (ci - g0) * 128:(ci - g0 + 1) * 128],
                        in_=hqb(ci, 0, 128)))
                c_d = inc("vector", "ve", lambda e: e.engine_nop())
                wait("gpsimd", "ve", c_d)
                dD = dma("gpsimd", dbg_hq[:, g0 * 128:g1 * 128],
                         tq1[:, 0:(g1 - g0) * 128], "dDBG")
                wait("vector", "dDBG", dD)
        c_hq = inc("vector", "ve", lambda e: e.engine_nop())
        s6d.close()
        join()
        gate("hq")

        sdb = E6(nc.sbuf_tensor("sdb", [128, 1024], F32))
        t6e = E6(nc.sbuf_tensor("t6e", [128, 1024], F32))
        idt = E6(nc.sbuf_tensor("idt", [128, 2, 1024], BF16))
        wait("tensor", "ve", c_sd1)
        with nc.psum_tensor("bcpE", [128, 512], F32) as bcpE:
            for k in range(2):
                c_bs = inc("tensor", "pe", lambda e, k=k: e.matmul(
                    bcpE[:, :], ones_row[0:1, :],
                    sdrow[0:1, k * 512:(k + 1) * 512],
                    start=True, stop=True, skip_group_check=True))
                wait("vector", "pe", c_bs)
                c_bv = inc("vector", "ve", lambda e, k=k: e.tensor_copy(
                    out=sdb[:, k * 512:(k + 1) * 512], in_=bcpE[:, :]))
                wait("tensor", "ve", c_bv)
        ps6e = ExitStack()
        o4 = [ps6e.enter_context(nc.psum_tensor(f"o4_{i}", [128, 1024], F32))
              for i in range(4)]
        g_id = [0, 0]
        d_out = 0
        for tg in range(4):
            for ci in range(CI):
                sl = ci % 2
                wait("sync", "pe", g_id[sl])
                dc_ = "dID0" if sl == 0 else "dID1"
                d_id = dma("sync", idt[:, sl, :], id_scr[ci, :, :], dc_)
                wait("tensor", dc_, d_id)
                if ci == 0:
                    wait("tensor", "ve", n["ve"])
                for tt in range(4):
                    t = tg * 4 + tt
                    for dh in range(2):
                        fn = (lambda e, tt=tt, t=t, ci=ci, sl=sl, dh=dh: e.matmul(
                            o4[tt][:, dh * 512:(dh + 1) * 512],
                            hqb(ci, t * 128, (t + 1) * 128),
                            idt[:, sl, dh * 512:(dh + 1) * 512],
                            start=(ci == 0), stop=True,
                            skip_group_check=True))
                        if tt == 3 and dh == 1:
                            c_dn = inc("tensor", "pe", fn)
                        else:
                            em("tensor", fn)
                g_id[sl] = c_dn
            wait("vector", "pe", c_dn)
            for tt in range(4):
                t = tg * 4 + tt
                em("vector", lambda e, tt=tt: e.tensor_tensor(
                    out=t6e[:, :], in0=o4[tt][:, :], in1=sdb[:, :],
                    op=ALU.mult))
                c_fin = inc("vector", "ve", lambda e, t=t: e.tensor_tensor(
                    out=xmid[:, t, :], in0=xmid[:, t, :], in1=t6e[:, :],
                    op=ALU.add))
                wait("gpsimd", "ve", c_fin)
                d_out = dma("gpsimd", out_t[t * 128:(t + 1) * 128, :],
                            xmid[:, t, :], "dOUT")
            wait("tensor", "ve", n["ve"])
        ps6e.close()
        p6.close()

        wait("gpsimd", "dOUT", d_out)
        wait("sync", "dOUT", d_out)


    except _Trunc:
        st.pop_all()

    with nc.Block() as block:
        clr = range(min(s_.num for s_ in sem.values()), 256)
        nc.gpsimd.dma_reset(clr)
        nc.gpsimd.sem_clear(clr)
        nc.all_engine_barrier()

        @block.sync
        def _(eng):
            for fn in prog["sync"]:
                fn(eng)

        @block.tensor
        def _(eng):
            for fn in prog["tensor"]:
                fn(eng)

        @block.vector
        def _(eng):
            for fn in prog["vector"]:
                fn(eng)

        @block.scalar
        def _(eng):
            for fn in prog["scalar"]:
                fn(eng)

        @block.gpsimd
        def _(eng):
            for fn in prog["gpsimd"]:
                fn(eng)

    st.close()
    global _LAST_N, _LAST_SEM
    _LAST_N = dict(n)
    _LAST_SEM = {k: v for k, v in sem.items()}
    return nc



_CACHE = {}


def _host_prep(L0_attn_x, L0_attn_wq, L0_attn_wk, L0_attn_wv, L0_ffn_clip_x,
               L0_ffn_clip_down, L0_ffn_w_gate, L0_ffn_w_up, L0_ffn_w_down):
    x = np.ascontiguousarray(np.asarray(L0_attn_x, dtype=np.float32)
                             .reshape(S, D))
    wq = np.asarray(L0_attn_wq, dtype=np.float32)
    wk = np.asarray(L0_attn_wk, dtype=np.float32)
    wv = np.asarray(L0_attn_wv, dtype=np.float32)
    wg = np.asarray(L0_ffn_w_gate, dtype=np.float32)
    wu = np.asarray(L0_ffn_w_up, dtype=np.float32)
    wd = np.asarray(L0_ffn_w_down, dtype=np.float32)

    perm = np.concatenate([np.arange(0, D, 2), np.arange(1, D, 2)])
    wq_t = np.ascontiguousarray(wq[perm].T)
    wk_t = np.ascontiguousarray(wk[perm].T)
    wv_t = np.ascontiguousarray(wv.T)

    ident_b = np.eye(128, dtype=ml_dtypes.bfloat16)
    ident_f = np.eye(128, dtype=np.float32)
    i_ = np.arange(512, dtype=np.float64)
    phi64 = np.power(10000.0, -2.0 * (i_ - 1.0) / 1024.0)
    phi_hi = phi64.astype(np.float32)
    phi_lo = (phi64 - phi_hi.astype(np.float64)).astype(np.float32)
    phi_hi = np.ascontiguousarray(phi_hi.reshape(4, 128).T)
    phi_lo = np.ascontiguousarray(phi_lo.reshape(4, 128).T)
    clips = np.array([[np.float32(np.asarray(L0_ffn_clip_x).ravel()[0]),
                       np.float32(np.asarray(L0_ffn_clip_down).ravel()[0])]],
                     dtype=np.float32)

    in_maps = []
    rows_of = []
    for j in range(NCORE):
        rows = np.concatenate([
            np.arange(1024 * (r - 1) + 128 * ((j + r) % 8),
                      1024 * (r - 1) + 128 * ((j + r) % 8) + 128)
            for r in range(1, R + 1)])
        rows_of.append(rows)
        x_own = np.ascontiguousarray(x[rows]).reshape(R, 128, D)
        pos_own = rows.astype(np.float32).reshape(1, TOK)
        dm = np.zeros((R, 8, 128, 128), dtype=np.float32)
        tri = np.triu(np.ones((128, 128), dtype=np.float32))
        for r in range(1, R + 1):
            o = (j + r) % 8
            for m in range(8):
                op = (m + r) % 8
                if op < o:
                    dm[r - 1, m] = 1.0
                elif op == o:
                    dm[r - 1, m] = tri
        in_maps.append({
            "x_own": x_own, "pos_own": pos_own,
            "dmask": dm.astype(ml_dtypes.bfloat16),
            "phi_hi": phi_hi, "phi_lo": phi_lo,
            "ident_b": ident_b, "ident_f": ident_f,
            "clips": clips, "wq_t": wq_t, "wk_t": wk_t, "wv_t": wv_t,
            "wg_ih": wg, "wu_ih": wu, "wd_oi": wd,
        })
    return in_maps, rows_of


def kernel(L0_attn_x, L0_attn_wq, L0_attn_wk, L0_attn_wv, L0_ffn_clip_x,
           L0_ffn_clip_down, L0_ffn_w_gate, L0_ffn_w_up, L0_ffn_w_down):
    in_maps, rows_of = _host_prep(
        L0_attn_x, L0_attn_wq, L0_attn_wk, L0_attn_wv, L0_ffn_clip_x,
        L0_ffn_clip_down, L0_ffn_w_gate, L0_ffn_w_up, L0_ffn_w_down)
    if "nc" not in _CACHE:
        _CACHE["nc"] = build_nc()
    nc = _CACHE["nc"]
    res = run_bass_kernel_spmd(nc, in_maps, core_ids=list(range(NCORE)))
    out = np.empty((S, D), dtype=np.float32)
    for j in range(NCORE):
        out[rows_of[j]] = res.results[j]["out"]
    return out.reshape(1, S, D)


# version 11
# speedup vs baseline: 1.0663x; 1.0663x over previous
import math
import os
from contextlib import ExitStack

import numpy as np
import ml_dtypes

import concourse.bass as bass
import concourse.mybir as mybir
from concourse.bass_utils import run_bass_kernel_spmd

F32 = mybir.dt.float32
BF16 = mybir.dt.bfloat16
F16 = mybir.dt.float16
AF = mybir.ActivationFunctionType
ALU = mybir.AluOpType
AX = mybir.AxisListType.X

NCORE = 8
S = 16384
D = 1024
R = 16
TOK = 2048
HC = 8
IFF = 2816
CI = 22
EPS = 1e-5

LN1E4 = float(np.log(10000.0))
TWO_PI = 2.0 * math.pi
C1 = float(np.float32(int(TWO_PI * 2**9) / 2**9))
C2 = float(np.float32(int((TWO_PI - C1) * 2**20) / 2**20))
C3 = float(TWO_PI - C1 - C2)
PI = float(math.pi)
HALF_PI = float(math.pi / 2)
INV_2PI = float(1.0 / TWO_PI)

ENGS = ("sync", "tensor", "vector", "scalar", "gpsimd")


def build_nc(debug=False):
    nc = bass.Bass("TRN2", target_bir_lowering=False, debug=False,
                   detect_race_conditions=False)

    x_own = nc.dram_tensor("x_own", [R, 128, D], F32, kind="ExternalInput")
    pos_own = nc.dram_tensor("pos_own", [1, TOK], F32, kind="ExternalInput")
    dmask = nc.dram_tensor("dmask", [R, 8, 128, 128], BF16, kind="ExternalInput")
    phi_hi = nc.dram_tensor("phi_hi", [128, 4], F32, kind="ExternalInput")
    phi_lo = nc.dram_tensor("phi_lo", [128, 4], F32, kind="ExternalInput")
    ident_b = nc.dram_tensor("ident_b", [128, 128], BF16, kind="ExternalInput")
    ident_f = nc.dram_tensor("ident_f", [128, 128], F32, kind="ExternalInput")
    clips = nc.dram_tensor("clips", [1, 2], F32, kind="ExternalInput")
    wq_t = nc.dram_tensor("wq_t", [D, D], F32, kind="ExternalInput")
    wk_t = nc.dram_tensor("wk_t", [D, D], F32, kind="ExternalInput")
    wv_t = nc.dram_tensor("wv_t", [D, D], F32, kind="ExternalInput")
    wg_ih = nc.dram_tensor("wg_ih", [IFF, D], F32, kind="ExternalInput")
    wu_ih = nc.dram_tensor("wu_ih", [IFF, D], F32, kind="ExternalInput")
    wd_oi = nc.dram_tensor("wd_oi", [D, IFF], F32, kind="ExternalInput")

    out_t = nc.dram_tensor("out", [TOK, D], F32, kind="ExternalOutput")
    if debug:
        dbg_qr = nc.dram_tensor("dbg_qr", [HC * 128, TOK], F32, kind="ExternalOutput")
        dbg_sml = nc.dram_tensor("dbg_sml", [1, 64], F32, kind="ExternalOutput")
        dbg_negb = nc.dram_tensor("dbg_negb", [128, 1], F32, kind="ExternalOutput")
        dbg_den = nc.dram_tensor("dbg_den", [128, 16], F32, kind="ExternalOutput")
        dbg_osb = nc.dram_tensor("dbg_osb", [128, R * 1024], F32, kind="ExternalOutput")
        dbg_xmid = nc.dram_tensor("dbg_xmid", [128, R * 1024], F32, kind="ExternalOutput")
        dbg_hq = nc.dram_tensor("dbg_hq", [128, CI * 128], F32, kind="ExternalOutput")
        dbg_xn = nc.dram_tensor("dbg_xn", [HC * 128, TOK], F32, kind="ExternalOutput")
        dbg_nrm = nc.dram_tensor("dbg_nrm", [3, 16], F32, kind="ExternalOutput")
        dbg_row = nc.dram_tensor("dbg_row", [1, 64], F32, kind="ExternalOutput")
        dbg_xt = nc.dram_tensor("dbg_xt", [2, 128, 1024], F32, kind="ExternalOutput")
        dbg_xnt = nc.dram_tensor("dbg_xnt", [2, 128, 1024], F32, kind="ExternalOutput")
        dbg_cos = nc.dram_tensor("dbg_cos", [4 * 128, 1024], F32, kind="ExternalOutput")
        dbg_sin = nc.dram_tensor("dbg_sin", [4 * 128, 1024], F32, kind="ExternalOutput")

    kr_in = nc.dram_tensor("kr_in", [D, TOK], BF16)
    kr_out = nc.dram_tensor("kr_out", [NCORE * D, TOK], BF16)
    v_in = nc.dram_tensor("v_in", [TOK, D], BF16)
    v_out = nc.dram_tensor("v_out", [NCORE * TOK, D], BF16)
    nrm_in = nc.dram_tensor("nrm_in", [1, 16], F32)
    nrm_out = nc.dram_tensor("nrm_out", [1, 16], F32)
    m1_in = nc.dram_tensor("m1_in", [1, 16], F32)
    m1_out = nc.dram_tensor("m1_out", [1, 16], F32)
    m2_in = nc.dram_tensor("m2_in", [1, 16], F32)
    m2_out = nc.dram_tensor("m2_out", [1, 16], F32)
    ig_scr = nc.dram_tensor("ig_scr", [HC, 128, IFF], BF16)
    iu_scr = nc.dram_tensor("iu_scr", [HC, 128, IFF], BF16)
    id_scr = nc.dram_tensor("id_scr", [CI, 128, D], BF16)

    prog = {e: [] for e in ENGS}
    DMAS = ("dC", "dX0", "dX1", "dKV", "dV0", "dV1", "dB0", "dB1", "dP0", "dP1",
            "dI0", "dI1", "dIG0", "dIG1", "dID0", "dID1", "dRW", "dWT",
            "dCC", "dOUT", "dDBG")
    n = {k: 0 for k in DMAS}
    n.update({"cc": 0, "pe": 0, "ve": 0, "sc": 0})

    st = ExitStack()
    E = st.enter_context
    sem = {}
    for s_ in list(n.keys()):
        sem[s_] = E(nc.semaphore(s_))

    def em(eng, fn):
        prog[eng].append(fn)

    def dma(eng, out, in_, ctr):
        n[ctr] += 16
        em(eng, lambda e, o=out, i=in_: e.dma_start(out=o, in_=i)
           .then_inc(sem[ctr], 16))
        return n[ctr]

    def wait(eng, ctr, val):
        em(eng, lambda e, c=ctr, v=val: e.wait_ge(sem[c], v))

    def inc(eng, ctr, fn):
        n[ctr] += 1
        em(eng, lambda e, f=fn, c=ctr: f(e).then_inc(sem[c], 1))
        return n[ctr]

    def join():
        marks = {}
        for e_, c_ in (("tensor", "pe"), ("vector", "ve"), ("scalar", "sc")):
            marks[c_] = inc(e_, c_, lambda e: e.drain())
        snap = dict(n)
        for e_ in ENGS:
            for c_, v_ in snap.items():
                wait(e_, c_, v_)

    _PHASE = os.environ.get("K_PHASE", "all")

    class _Trunc(Exception):
        pass

    def gate(tag):
        if _PHASE == tag:
            join()
            raise _Trunc()

    ident_bs = E(nc.sbuf_tensor("ident_bs", [128, 128], BF16))
    ident_fs = E(nc.sbuf_tensor("ident_fs", [128, 128], F32))
    ones_row = E(nc.sbuf_tensor("ones_row", [1, 128], F32))
    ones_col = E(nc.sbuf_tensor("ones_col", [128, 1], BF16))
    negB = E(nc.sbuf_tensor("negB", [128, 1], F32))
    phih_sb = E(nc.sbuf_tensor("phih_sb", [128, 4], F32))
    phil_sb = E(nc.sbuf_tensor("phil_sb", [128, 4], F32))
    clips_sb = E(nc.sbuf_tensor("clips_sb", [1, 2], F32))
    sml = E(nc.sbuf_tensor("sml", [1, 64], F32))
    bc1 = E(nc.sbuf_tensor("bc1", [128, 8], F32))
    stage16 = E(nc.sbuf_tensor("stage16", [1, 16], F32))
    mg_t = E(nc.sbuf_tensor("mg_t", [128, 128], F32))
    mu_t = E(nc.sbuf_tensor("mu_t", [128, 128], F32))
    md_t = E(nc.sbuf_tensor("md_t", [128, 128], F32))
    sg_f = E(nc.sbuf_tensor("sg_f", [128, 22], F32))
    su_f = E(nc.sbuf_tensor("su_f", [128, 22], F32))
    s1sg = E(nc.sbuf_tensor("s1sg", [128, 22], F32))
    s1su = E(nc.sbuf_tensor("s1su", [128, 22], F32))
    lnm = E(nc.sbuf_tensor("lnm", [128, 16], F32))
    lnr = E(nc.sbuf_tensor("lnr", [128, 16], F32))
    hmax = E(nc.sbuf_tensor("hmax", [128, 1], F32))
    pad128 = E(nc.sbuf_tensor("pad128", [128, 128], F32))
    epsb = E(nc.sbuf_tensor("epsb", [128, 1], F32))
    sdrow = E(nc.sbuf_tensor("sdrow", [1, D], F32))
    gath = E(nc.sbuf_tensor("gath", [128, 128], F32))
    snap_sb = E(nc.sbuf_tensor("snap_sb", [1, 64], F32))
    QR = E(nc.sbuf_tensor("QR", [128, HC, TOK], BF16))
    spc = E(nc.sbuf_tensor("spc", [128, 1], F32))


    def vpad():
        em("vector", lambda e: e.tensor_copy(out=spc[:, :], in_=ones_col[:, :]))

    def pmax_to_sml(src_col_ap, dst, tpf):
        vpad()
        cve = inc("vector", "ve", lambda e: e.tensor_copy(
            out=pad128[:, 0:1], in_=src_col_ap))
        wait("tensor", "ve", cve)
        wait("tensor", "sc", n["sc"])
        cpe = inc("tensor", "pe", lambda e: e.transpose(
            tpf[:, :], pad128[:, :], ident_fs[:, :]))
        wait("vector", "pe", cpe)
        r_ = inc("vector", "ve", lambda e, dst=dst: e.reduce_max(
            out=sml[0:1, dst:dst + 1], in_=tpf[0:1, 0:128], axis=AX))
        vpad()
        return r_

    def bcast(src_ap, dstcol, bcp):
        wait("tensor", "sc", n["sc"])
        wait("tensor", "ve", n["ve"])
        cpe = inc("tensor", "pe", lambda e: e.matmul(
            bcp[:, 0:1], ones_row[0:1, :], src_ap, start=True, stop=True,
            skip_group_check=True))
        wait("scalar", "pe", cpe)
        return inc("scalar", "sc", lambda e, dstcol=dstcol: e.copy(
            out=bc1[:, dstcol:dstcol + 1], in_=bcp[:, 0:1]))

    try:
        dma("sync", ident_bs[:, :], ident_b[:, :], "dC")
        dma("sync", ident_fs[:, :], ident_f[:, :], "dC")
        dma("sync", phih_sb[:, :], phi_hi[:, :], "dC")
        dma("sync", phil_sb[:, :], phi_lo[:, :], "dC")
        d_consts = dma("sync", clips_sb[:, :], clips[:, :], "dC")
        wait("tensor", "dC", d_consts)
        wait("scalar", "dC", d_consts)

        wait("vector", "dC", d_consts)
        em("vector", lambda e: e.memset(ones_row[:, :], 1.0))
        em("vector", lambda e: e.memset(ones_col[:, :], 1.0))
        em("vector", lambda e: e.memset(pad128[:, :], 0.0))
        em("vector", lambda e: e.memset(sml[:, :], 0.0))
        em("vector", lambda e: e.memset(mg_t[:, :], 0.0))
        em("vector", lambda e: e.memset(mu_t[:, :], 0.0))
        em("vector", lambda e: e.memset(md_t[:, :], 0.0))
        em("vector", lambda e: e.memset(tmpc_init[0][:, :], 0.0)
           if False else e.memset(bc1[:, :], 0.0))
        em("vector", lambda e: e.memset(epsb[:, :], EPS))
        c_consts = inc("vector", "ve", lambda e: e.memset(stage16[:, :], 0.0))

        p1 = ExitStack()
        E1 = p1.enter_context
        pos_sb = E1(nc.sbuf_tensor("pos_sb", [1, TOK], F32))
        XN = E1(nc.sbuf_tensor("XN", [128, HC, TOK], BF16))
        costab = E1(nc.sbuf_tensor("costab", [128, 4, 1024], F32))
        sintab = E1(nc.sbuf_tensor("sintab", [128, 4, 1024], F32))
        w_bf = E1(nc.sbuf_tensor("w_bf", [128, 2, HC, 1024], BF16))
        xt = E1(nc.sbuf_tensor("xt", [128, 2, 1024], F32))
        xnt = E1(nc.sbuf_tensor("xnt", [128, 1024], BF16))
        sqv = E1(nc.sbuf_tensor("sqv", [128, 2, 1024], BF16))
        t1 = E1(nc.sbuf_tensor("t1", [128, 1024], F32))
        t2 = E1(nc.sbuf_tensor("t2", [128, 1024], F32))
        krs = E1(nc.sbuf_tensor("krs", [128, 2, 1024], BF16))
        vst = E1(nc.sbuf_tensor("vst", [128, 2, 1024], BF16))
        posb = E1(nc.sbuf_tensor("posb", [128, 1024], F32))
        angb = E1(nc.sbuf_tensor("angb", [128, 1024], F32))
        kkb = E1(nc.sbuf_tensor("kkb", [128, 1024], F32))
        redb = E1(nc.sbuf_tensor("redb", [128, 1024], F32))
        stt1 = E1(nc.sbuf_tensor("stt1", [128, 16], F32))
        stt2 = E1(nc.sbuf_tensor("stt2", [128, 16], F32))
        angi = E1(nc.sbuf_tensor("angi", [128, 1024], mybir.dt.int32))
        tmpc = E1(nc.sbuf_tensor("tmpc", [128, 8], F32))

        dma("sync", pos_sb[:, :], pos_own[:, :], "dC")
        for hc in range(HC):
            dma("gpsimd", w_bf[:, 0, hc, :], wq_t[hc * 128:(hc + 1) * 128, :], "dWT")
        for hc in range(HC):
            dma("gpsimd", w_bf[:, 1, hc, :], wk_t[hc * 128:(hc + 1) * 128, :], "dWT")
        d_wqk = n["dWT"]

        ps1a = ExitStack()
        trp = ps1a.enter_context(nc.psum_tensor("trp", [128, 512], BF16))
        trp2 = ps1a.enter_context(nc.psum_tensor("trp2", [128, 512], BF16))
        g_x = [0, 0]
        g_xs = [0, 0]
        g_xn = 0
        g_tr = [0, 0]
        for t in range(R):
            sl = t % 2
            xc = "dX0" if sl == 0 else "dX1"
            wait("sync", "ve", g_x[sl])
            wait("sync", "sc", g_xs[sl])
            d_x = dma("sync", xt[:, sl, :], x_own[t, :, :], xc)
            wait("scalar", xc, d_x)
            if debug and _PHASE == "qkv" and t < 2:
                wait("gpsimd", xc, d_x)
                dX = dma("gpsimd", dbg_xt[t, :, :], xt[:, sl, :], "dDBG")
                wait("sync", "dDBG", dX)
            c_sq = inc("scalar", "sc", lambda e, sl=sl: e.activation(
                out=t1[:, :], in_=xt[:, sl, :], func=AF.Square))
            wait("vector", xc, d_x)
            wait("vector", "sc", c_sq)
            em("vector", lambda e, sl=sl, t=t: e.reduce_sum(
                stt1[:, t:t + 1], xt[:, sl, :], axis=AX))
            em("vector", lambda e, t=t: e.reduce_sum(
                stt2[:, t:t + 1], t1[:, :], axis=AX))
            em("vector", lambda e, t=t: e.tensor_scalar(
                out=stt1[:, t:t + 1], in0=stt1[:, t:t + 1], scalar1=1.0 / D,
                scalar2=None, op0=ALU.mult))
            em("vector", lambda e, t=t: e.tensor_scalar(
                out=stt2[:, t:t + 1], in0=stt2[:, t:t + 1], scalar1=1.0 / D,
                scalar2=None, op0=ALU.mult))
            em("vector", lambda e, t=t: e.scalar_tensor_tensor(
                out=tmpc[:, 0:1], in0=stt1[:, t:t + 1], scalar=-1.0,
                in1=stt1[:, t:t + 1], op0=ALU.mult, op1=ALU.mult))
            vpad()
            c_var = inc("vector", "ve", lambda e, t=t: e.tensor_tensor(
                out=stt2[:, t:t + 1], in0=stt2[:, t:t + 1], in1=tmpc[:, 0:1],
                op=ALU.add))
            wait("scalar", "ve", c_var)
            c_rt = inc("scalar", "sc", lambda e, t=t: e.activation(
                out=tmpc[:, 1:2], in_=stt2[:, t:t + 1], func=AF.Sqrt,
                bias=epsb[:, 0:1]))
            wait("vector", "sc", c_rt)
            em("vector", lambda e: e.reciprocal(tmpc[:, 2:3], tmpc[:, 1:2]))
            em("vector", lambda e, t=t: e.tensor_scalar(
                out=tmpc[:, 3:4], in0=stt2[:, t:t + 1], scalar1=EPS, scalar2=-0.5,
                op0=ALU.add, op1=ALU.mult))
            em("vector", lambda e: e.tensor_tensor(
                out=tmpc[:, 4:5], in0=tmpc[:, 2:3], in1=tmpc[:, 2:3], op=ALU.mult))
            vpad()
            em("vector", lambda e: e.tensor_tensor(
                out=tmpc[:, 4:5], in0=tmpc[:, 4:5], in1=tmpc[:, 3:4], op=ALU.mult))
            vpad()
            em("vector", lambda e: e.tensor_scalar(
                out=tmpc[:, 4:5], in0=tmpc[:, 4:5], scalar1=1.5, scalar2=None,
                op0=ALU.add))
            vpad()
            em("vector", lambda e: e.tensor_tensor(
                out=tmpc[:, 2:3], in0=tmpc[:, 2:3], in1=tmpc[:, 4:5], op=ALU.mult))
            vpad()
            c_bias = inc("vector", "ve", lambda e, t=t: e.scalar_tensor_tensor(
                out=tmpc[:, 5:6], in0=stt1[:, t:t + 1], scalar=-1.0,
                in1=tmpc[:, 2:3], op0=ALU.mult, op1=ALU.mult))
            g_x[sl] = c_bias
            wait("scalar", "ve", c_bias)
            wait("scalar", "pe", g_xn)
            c_xn = inc("scalar", "sc", lambda e, sl=sl: e.activation(
                out=xnt[:, :], in_=xt[:, sl, :], func=AF.Identity,
                bias=tmpc[:, 5:6], scale=tmpc[:, 2:3]))
            g_xs[sl] = c_xn
            wait("tensor", "sc", c_xn)
            if debug and _PHASE == "qkv" and t < 2:
                wait("gpsimd", "sc", c_xn)
                dX = dma("gpsimd", dbg_xnt[t, :, :], xnt[:, :], "dDBG")
                wait("scalar", "dDBG", dX)
            for g in range(2):
                tp = trp if g == 0 else trp2
                wait("tensor", "sc", g_tr[g])
                for k4 in range(4):
                    hc = g * 4 + k4
                    fn = (lambda e, tp=tp, k4=k4, hc=hc: e.transpose(
                        tp[:, k4 * 128:(k4 + 1) * 128],
                        xnt[:, hc * 128:(hc + 1) * 128], ident_bs[:, :]))
                    if k4 == 3:
                        g_xn = inc("tensor", "pe", fn)
                    else:
                        em("tensor", fn)
                wait("scalar", "pe", g_xn)
                g_tr[g] = inc("scalar", "sc", lambda e, tp=tp, g=g, t=t: e.copy(
                    out=XN[:, g * 4:(g + 1) * 4, t * 128:(t + 1) * 128],
                    in_=tp[:, :].rearrange("p (a b) -> p a b", a=4)))
            if os.environ.get("KJOIN", "0") == "1":
                join()
        ps1a.close()
        join()

        ps1n = ExitStack()
        nps = [ps1n.enter_context(nc.psum_tensor(f"nps{i}", [128, 1024], F32))
               for i in range(2)]
        tpfn = ps1n.enter_context(nc.psum_tensor("tpfn", [128, 128], F32))
        g_nev = [0, 0]
        for proj in range(2):
            for t in range(R):
                sl = t % 2
                wait("tensor", "sc", g_nev[sl])
                for hc in range(HC):
                    for k in range(2):
                        fn = (lambda e, sl=sl, hc=hc, k=k, t=t, proj=proj: e.matmul(
                            nps[sl][:, k * 512:(k + 1) * 512],
                            XN[:, hc, t * 128:(t + 1) * 128],
                            w_bf[:, proj, hc, k * 512:(k + 1) * 512],
                            start=(hc == 0), stop=(hc == HC - 1),
                            skip_group_check=True))
                        if hc == HC - 1 and k == 1:
                            c_nmm = inc("tensor", "pe", fn)
                        else:
                            em("tensor", fn)
                wait("scalar", "pe", c_nmm)
                wait("scalar", "ve", n["ve"])
                g_nev[sl] = c_nsq = inc("scalar", "sc", lambda e, sl=sl: e.activation(
                    out=t1[:, :], in_=nps[sl][:, :], func=AF.Square))
                wait("vector", "sc", c_nsq)
                em("vector", lambda e: e.reduce_sum(
                    out=tmpc[:, 6:7], in_=t1[:, :], axis=AX))
                vpad()
                if t == 0:
                    cnv = inc("vector", "ve", lambda e: e.tensor_copy(
                        out=tmpc[:, 7:8], in_=tmpc[:, 6:7]))
                else:
                    cnv = inc("vector", "ve", lambda e: e.tensor_tensor(
                        out=tmpc[:, 7:8], in0=tmpc[:, 7:8], in1=tmpc[:, 6:7],
                        op=ALU.max))
            pmax_to_sml(tmpc[:, 7:8], proj, tpfn)
        ps1n.close()

        wait("tensor", "sc", n["sc"])
        wait("tensor", "ve", n["ve"])
        c_mkst = inc("vector", "ve", lambda e: e.tensor_copy(
            out=stage16[0:1, 0:1], in_=sml[0:1, 1:2]))
        wait("gpsimd", "ve", c_mkst)
        d_nrm = dma("gpsimd", nrm_in[:, :], stage16[:, :], "dCC")
        wait("gpsimd", "dCC", d_nrm)
        n["cc"] += 1
        em("gpsimd", lambda e: e.collective_compute(
            "AllReduce", ALU.max, replica_groups=[list(range(NCORE))],
            ins=[nrm_in.ap()], outs=[nrm_out.ap()]).then_inc(sem["cc"], 1))
        wait("gpsimd", "cc", 1)
        d_nrmo = dma("gpsimd", stage16[:, :], nrm_out[:, :], "dCC")
        if debug:
            dma("gpsimd", dbg_nrm[0:1, :], nrm_in[:, :], "dDBG")
            dma("gpsimd", dbg_nrm[1:2, :], nrm_out[:, :], "dDBG")
            wait("gpsimd", "dDBG", n["dDBG"])
        wait("scalar", "dCC", d_nrmo)
        wait("scalar", "ve", n["ve"])
        em("scalar", lambda e: e.activation(
            out=sml[0:1, 2:3], in_=sml[0:1, 0:1], func=AF.Sqrt))
        c_sq2 = inc("scalar", "sc", lambda e: e.activation(
            out=sml[0:1, 3:4], in_=stage16[0:1, 0:1], func=AF.Sqrt))
        wait("vector", "sc", c_sq2)
        c_nb = inc("vector", "ve", lambda e: e.scalar_tensor_tensor(
            out=sml[0:1, 4:5], in0=sml[0:1, 2:3], scalar=-1.02 / 32.0,
            in1=sml[0:1, 3:4], op0=ALU.mult, op1=ALU.mult))
        if debug:
            em("vector", lambda e: e.memset(snap_sb[:, :], 0.0))
            em("vector", lambda e: e.tensor_copy(out=snap_sb[0:1, 0:16],
                                                 in_=sml[0:1, 0:16]))
            c_snap = inc("vector", "ve", lambda e: e.tensor_copy(
                out=snap_sb[0:1, 16:32], in_=stage16[0:1, 0:16]))
            wait("gpsimd", "ve", c_snap)
            dma("gpsimd", dbg_row[:, :], snap_sb[0:1, :], "dDBG")
        wait("tensor", "ve", c_nb)
        with nc.psum_tensor("bcp0", [128, 16], F32) as bcp0:
            c_nbm = inc("tensor", "pe", lambda e: e.matmul(
                bcp0[:, 0:1], ones_row[0:1, :], sml[0:1, 4:5], start=True,
                stop=True, skip_group_check=True))
            wait("scalar", "pe", c_nbm)
            c_negB = inc("scalar", "sc", lambda e: e.copy(
                out=negB[:, :], in_=bcp0[:, 0:1]))
        wait("tensor", "sc", c_negB)
        if debug and _PHASE == "negb":
            wait("gpsimd", "sc", c_negB)
            dma("gpsimd", dbg_sml[:, :], sml[:, :], "dDBG")
            dma("gpsimd", dbg_negb[:, :], negB[:, :], "dDBG")
            wait("gpsimd", "dDBG", n["dDBG"])
        gate("negb")


        ps1b = ExitStack()
        qe_ps = ps1b.enter_context(nc.psum_tensor("qe_ps", [128, 1024], F32))
        qo_ps = ps1b.enter_context(nc.psum_tensor("qo_ps", [128, 1024], F32))

        n_kr_dma = 0

        for half in range(2):
            for k in range(2):
                c_bcmm = inc("tensor", "pe", lambda e, k=k, half=half: e.matmul(
                    qe_ps[:, k * 512:(k + 1) * 512], ones_row[0:1, :],
                    pos_sb[0:1, half * 1024 + k * 512:half * 1024 + (k + 1) * 512],
                    start=True, stop=True, skip_group_check=True))
            wait("vector", "pe", c_bcmm)
            c_posb = inc("vector", "ve", lambda e: e.tensor_copy(
                out=posb[:, :], in_=qe_ps[:, :]))
            for pc in range(4):
                em("vector", lambda e, pc=pc: e.tensor_scalar(
                    out=redb[:, :], in0=posb[:, :], scalar1=phil_sb[:, pc:pc + 1],
                    scalar2=None, op0=ALU.mult))
                em("vector", lambda e, pc=pc: e.tensor_scalar(
                    out=angb[:, :], in0=posb[:, :], scalar1=phih_sb[:, pc:pc + 1],
                    scalar2=None, op0=ALU.mult))
                em("vector", lambda e: e.tensor_tensor(
                    out=angb[:, :], in0=angb[:, :], in1=redb[:, :], op=ALU.add))
                em("vector", lambda e: e.tensor_scalar(
                    out=kkb[:, :], in0=angb[:, :], scalar1=INV_2PI, scalar2=None,
                    op0=ALU.mult))
                em("vector", lambda e: e.tensor_copy(out=angi[:, :], in_=kkb[:, :]))
                em("vector", lambda e: e.tensor_copy(out=kkb[:, :], in_=angi[:, :]))
                em("vector", lambda e: e.tensor_scalar(
                    out=kkb[:, :], in0=kkb[:, :], scalar1=-1.0, scalar2=None,
                    op0=ALU.mult))
                em("vector", lambda e: e.scalar_tensor_tensor(
                    out=redb[:, :], in0=kkb[:, :], scalar=C1, in1=angb[:, :],
                    op0=ALU.mult, op1=ALU.add))
                em("vector", lambda e: e.scalar_tensor_tensor(
                    out=redb[:, :], in0=kkb[:, :], scalar=C2, in1=redb[:, :],
                    op0=ALU.mult, op1=ALU.add))
                em("vector", lambda e: e.scalar_tensor_tensor(
                    out=redb[:, :], in0=kkb[:, :], scalar=C3, in1=redb[:, :],
                    op0=ALU.mult, op1=ALU.add))
                em("vector", lambda e: e.tensor_scalar(
                    out=t1[:, :], in0=redb[:, :], scalar1=PI, scalar2=None,
                    op0=ALU.is_gt))
                em("vector", lambda e: e.scalar_tensor_tensor(
                    out=redb[:, :], in0=t1[:, :], scalar=-TWO_PI, in1=redb[:, :],
                    op0=ALU.mult, op1=ALU.add))
                em("vector", lambda e: e.tensor_scalar(
                    out=t1[:, :], in0=redb[:, :], scalar1=-PI, scalar2=None,
                    op0=ALU.is_lt))
                em("vector", lambda e: e.scalar_tensor_tensor(
                    out=t1[:, :], in0=t1[:, :], scalar=TWO_PI, in1=redb[:, :],
                    op0=ALU.mult, op1=ALU.add))
                em("vector", lambda e: e.tensor_scalar(
                    out=redb[:, :], in0=t1[:, :], scalar1=HALF_PI, scalar2=None,
                    op0=ALU.add))
                em("vector", lambda e: e.tensor_scalar(
                    out=t2[:, :], in0=redb[:, :], scalar1=PI, scalar2=None,
                    op0=ALU.is_gt))
                c_wr = inc("vector", "ve", lambda e: e.scalar_tensor_tensor(
                    out=t2[:, :], in0=t2[:, :], scalar=-TWO_PI, in1=redb[:, :],
                    op0=ALU.mult, op1=ALU.add))
                wait("scalar", "ve", c_wr)
                em("scalar", lambda e, pc=pc: e.activation(
                    out=sintab[:, pc, :], in_=t1[:, :], func=AF.Sin))
                c_tab = inc("scalar", "sc", lambda e, pc=pc: e.activation(
                    out=costab[:, pc, :], in_=t2[:, :], func=AF.Sin))
                wait("vector", "sc", c_tab)

            wait("tensor", "ve", c_posb)
            for proj in range(2):
                is_q = (proj == 0)
                if proj == 1:
                    wait("tensor", "dWT", d_wqk)
                for pc in range(4):
                    for (ps, dcx) in ((qe_ps, pc), (qo_ps, pc + 4)):
                        for hc in range(HC):
                            for k in range(2):
                                fn = (lambda e, ps=ps, dcx=dcx, hc=hc, k=k,
                                      proj=proj, half=half: e.matmul(
                                          ps[:, k * 512:(k + 1) * 512],
                                          w_bf[:, proj, hc,
                                               dcx * 128:(dcx + 1) * 128],
                                          XN[:, hc, half * 1024 + k * 512:
                                             half * 1024 + (k + 1) * 512],
                                          start=(hc == 0), stop=(hc == HC - 1),
                                          skip_group_check=True))
                                if (ps is qo_ps) and hc == HC - 1 and k == 1:
                                    c_qk = inc("tensor", "pe", fn)
                                else:
                                    em("tensor", fn)
                    wait("vector", "pe", c_qk)
                    if not is_q:
                        wait("vector", "dKV", n_kr_dma)
                        wait("vector", "sc", n["sc"])
                    em("vector", lambda e, pc=pc: e.tensor_tensor(
                        out=t1[:, :], in0=qe_ps[:, :], in1=costab[:, pc, :],
                        op=ALU.mult))
                    em("vector", lambda e, pc=pc: e.tensor_tensor(
                        out=t2[:, :], in0=qo_ps[:, :], in1=sintab[:, pc, :],
                        op=ALU.mult))
                    if is_q:
                        em("vector", lambda e, pc=pc, half=half: e.tensor_tensor(
                            out=QR[:, pc, half * 1024:(half + 1) * 1024],
                            in0=t1[:, :], in1=t2[:, :], op=ALU.add))
                    else:
                        em("vector", lambda e: e.tensor_tensor(
                            out=krs[:, 0, :], in0=t1[:, :], in1=t2[:, :], op=ALU.add))
                    em("vector", lambda e, pc=pc: e.tensor_tensor(
                        out=t1[:, :], in0=qo_ps[:, :], in1=costab[:, pc, :],
                        op=ALU.mult))
                    em("vector", lambda e, pc=pc: e.tensor_tensor(
                        out=t2[:, :], in0=qe_ps[:, :], in1=sintab[:, pc, :],
                        op=ALU.mult))
                    if is_q:
                        c_rp = inc("vector", "ve", lambda e, pc=pc, half=half:
                                   e.tensor_tensor(
                                       out=QR[:, pc + 4,
                                              half * 1024:(half + 1) * 1024],
                                       in0=t1[:, :], in1=t2[:, :], op=ALU.subtract))
                    else:
                        c_rp = inc("vector", "ve", lambda e: e.tensor_tensor(
                            out=krs[:, 1, :], in0=t1[:, :], in1=t2[:, :],
                            op=ALU.subtract))
                    wait("tensor", "ve", c_rp)
                    if not is_q:
                        for pi, dcx in ((0, pc), (1, pc + 4)):
                            wait("sync", "ve", c_rp)
                            n_kr_dma = dma(
                                "sync",
                                kr_in[dcx * 128:(dcx + 1) * 128,
                                      half * 1024:(half + 1) * 1024],
                                krs[:, pi, :], "dKV")

        ps1b.close()
        wait("tensor", "ve", n["ve"])
        wait("gpsimd", "dKV", n["dKV"])
        n["cc"] += 1
        em("gpsimd", lambda e: e.collective_compute(
            "AllGather", ALU.bypass, replica_groups=[list(range(NCORE))],
            ins=[kr_in.ap()], outs=[kr_out.ap()]).then_inc(sem["cc"], 1))

        wait("gpsimd", "pe", n["pe"])
        for hc in range(HC):
            d_wv = dma("gpsimd", w_bf[:, 0, hc, :],
                       wv_t[hc * 128:(hc + 1) * 128, :], "dWT")

        ps1c = ExitStack()
        vps = [ps1c.enter_context(nc.psum_tensor(f"vps{i}", [128, 1024], F32))
               for i in range(2)]
        g_vev = [0, 0]
        g_vdm = [0, 0]
        wait("tensor", "dWT", d_wv)
        n_v_dma = n_kr_dma
        for t in range(R):
            sl = t % 2
            wait("tensor", "sc", g_vev[sl])
            for hc in range(HC):
                for k in range(2):
                    fn = (lambda e, sl=sl, hc=hc, k=k, t=t: e.matmul(
                        vps[sl][:, k * 512:(k + 1) * 512],
                        XN[:, hc, t * 128:(t + 1) * 128],
                        w_bf[:, 0, hc, k * 512:(k + 1) * 512],
                        start=(hc == 0), stop=(hc == HC - 1),
                        skip_group_check=True))
                    if hc == HC - 1 and k == 1:
                        c_vmm = inc("tensor", "pe", fn)
                    else:
                        em("tensor", fn)
            wait("scalar", "pe", c_vmm)
            wait("scalar", "dV0" if sl == 0 else "dV1", g_vdm[sl])
            g_vev[sl] = inc("scalar", "sc", lambda e, sl=sl: e.copy(
                out=vst[:, sl, :], in_=vps[sl][:, :]))
            wait("sync", "sc", g_vev[sl])
            g_vdm[sl] = n_v_dma = dma(
                "sync", v_in[t * 128:(t + 1) * 128, :], vst[:, sl, :],
                "dV0" if sl == 0 else "dV1")
        ps1c.close()
        wait("gpsimd", "dV0", n["dV0"])
        wait("gpsimd", "dV1", n["dV1"])
        n["cc"] += 1
        em("gpsimd", lambda e: e.collective_compute(
            "AllGather", ALU.bypass, replica_groups=[list(range(NCORE))],
            ins=[v_in.ap()], outs=[v_out.ap()]).then_inc(sem["cc"], 1))

        if debug:
            wait("gpsimd", "ve", n["ve"])
            wait("gpsimd", "sc", n["sc"])
            for hc in range(HC):
                dma("gpsimd", dbg_xn[hc * 128:(hc + 1) * 128, :], XN[:, hc, :], "dDBG")
            for pc in range(4):
                dma("gpsimd", dbg_cos[pc * 128:(pc + 1) * 128, :], costab[:, pc, :], "dDBG")
                dma("gpsimd", dbg_sin[pc * 128:(pc + 1) * 128, :], sintab[:, pc, :], "dDBG")
            wait("gpsimd", "dDBG", n["dDBG"])
        if debug and _PHASE == "qkv":
            with nc.sbuf_tensor("dbq0", [128, TOK], F32) as dbq0:
                for dc in range(HC):
                    c_d = inc("vector", "ve", lambda e, dc=dc: e.tensor_copy(
                        out=dbq0[:, :], in_=QR[:, dc, :]))
                    wait("gpsimd", "ve", c_d)
                    dD = dma("gpsimd", dbg_qr[dc * 128:(dc + 1) * 128, :],
                             dbq0[:, :], "dDBG")
                    wait("vector", "dDBG", dD)
        gate("qkv")


        p1.close()
        join()

        scope_o = ExitStack()
        Ea = scope_o.enter_context
        O_sb = Ea(nc.sbuf_tensor("O_sb", [128, R, 1024], F32))
        den_sb = Ea(nc.sbuf_tensor("den_sb", [128, 16], F32))
        rden = Ea(nc.sbuf_tensor("rden", [128, 16], F32))

        p5gu = ExitStack()
        E5g = p5gu.enter_context
        wsl = E5g(nc.sbuf_tensor("wsl", [128, 2, D], F32))
        vvb = E5g(nc.sbuf_tensor("vvb", [128, D], F32))
        ffb = E5g(nc.sbuf_tensor("ffb", [128, D], F32))
        intsl = E5g(nc.sbuf_tensor("intsl", [128, 2, D], BF16))
        stg5 = E5g(nc.sbuf_tensor("stg5", [128, 2, HC, 128], BF16))
        wqi = E5g(nc.sbuf_tensor("wqi", [128, D], mybir.dt.int32))
        tmpq = E5g(nc.sbuf_tensor("tmpq", [128, 4], F32))
        tps_gu = E5g(nc.psum_tensor("tps_gu", [128, 512], BF16))

        g5_w = [0, 0]
        g5_int = [0, 0]
        g5_stW = [0, 0]
        g5_tp = [0]
        u_cq = {}

        def p5_quant(mat, dst, c, uid):
            slw = uid % 2
            pc_ = "dP0" if slw == 0 else "dP1"
            wait("sync", "ve", g5_w[slw])
            d_w = dma("sync", wsl[:, slw, :],
                      mat[c * 128:(c + 1) * 128, :], pc_)
            wait("vector", pc_, d_w)
            em("vector", lambda e, dst=dst, c=c, slw=slw: e.reduce_max(
                out=dst[:, c:c + 1], in_=wsl[:, slw, :],
                axis=AX, apply_absolute_value=True))
            vpad()
            em("vector", lambda e, dst=dst, c=c, slw=slw: e.tensor_scalar(
                out=tmpq[:, 2 * slw:2 * slw + 1], in0=dst[:, c:c + 1],
                scalar1=1.0 / 127.0, scalar2=None, op0=ALU.mult))
            vpad()
            em("vector", lambda e, slw=slw: e.reciprocal(
                tmpq[:, 2 * slw + 1:2 * slw + 2],
                tmpq[:, 2 * slw:2 * slw + 1]))
            vpad()
            g5_w[slw] = inc("vector", "ve", lambda e, slw=slw: e.tensor_scalar(
                out=vvb[:, :], in0=wsl[:, slw, :],
                scalar1=tmpq[:, 2 * slw + 1:2 * slw + 2],
                scalar2=None, op0=ALU.mult))
            em("vector", lambda e: e.tensor_copy(out=wqi[:, :], in_=vvb[:, :]))
            em("vector", lambda e: e.tensor_copy(out=ffb[:, :], in_=wqi[:, :]))
            em("vector", lambda e: e.tensor_tensor(
                out=vvb[:, :], in0=vvb[:, :], in1=ffb[:, :], op=ALU.subtract))
            em("vector", lambda e: e.scalar_tensor_tensor(
                out=ffb[:, :], in0=vvb[:, :], scalar=0.5, in1=ffb[:, :],
                op0=ALU.is_ge, op1=ALU.add))
            em("vector", lambda e: e.tensor_scalar(
                out=vvb[:, :], in0=vvb[:, :], scalar1=-0.5, scalar2=None,
                op0=ALU.is_lt))
            wait("vector", "pe", g5_int[slw])
            u_cq[uid] = inc("vector", "ve", lambda e, slw=slw: e.tensor_tensor(
                out=intsl[:, slw, :], in0=ffb[:, :], in1=vvb[:, :],
                op=ALU.subtract))

        def p5_store(scr, c, uid):
            slw = uid % 2
            ic_ = "dI0" if slw == 0 else "dI1"
            wait("tensor", "ve", u_cq[uid])
            first_cp = True
            for g0 in range(0, HC, 4):
                wait("tensor", "sc", g5_tp[0])
                for k in range(4):
                    fn = (lambda e, k=k, g0=g0, slw=slw: e.transpose(
                        tps_gu[:, k * 128:(k + 1) * 128],
                        intsl[:, slw, (g0 + k) * 128:(g0 + k + 1) * 128],
                        ident_bs[:, :]))
                    if k == 3:
                        c_tp = inc("tensor", "pe", fn)
                    else:
                        em("tensor", fn)
                if g0 + 4 >= HC:
                    g5_int[slw] = c_tp
                wait("scalar", "pe", c_tp)
                if first_cp:
                    wait("scalar", ic_, g5_stW[slw])
                    first_cp = False
                g5_tp[0] = inc("scalar", "sc",
                               lambda e, g0=g0, slw=slw:
                               e.copy(out=stg5[:, slw, g0:g0 + 4, :],
                                      in_=tps_gu[:, :].rearrange(
                                          "p (a b) -> p a b", a=4)))
            wait("sync", "sc", g5_tp[0])
            dstap = bass.AP(scr, c * 128,
                            [[IFF, 128], [128 * IFF, HC], [1, 128]])
            g5_stW[slw] = dma("sync", dstap, stg5[:, slw, 0:HC, :], ic_)

        p5q = []
        for uid in range(44):
            mat, dst, scr = ((wg_ih, mg_t, ig_scr) if uid < 22
                             else (wu_ih, mu_t, iu_scr))
            c = uid % 22
            p5q.append((lambda mat=mat, dst=dst, c=c, uid=uid:
                        p5_quant(mat, dst, c, uid),
                        lambda scr=scr, c=c, uid=uid: p5_store(scr, c, uid)))
        p5_next = [0, 0]

        def p5_drain(n_units):
            for _ in range(n_units):
                if p5_next[0] < len(p5q):
                    p5q[p5_next[0]][0]()
                    p5_next[0] += 1
                lagged = (p5_next[0] - 1 if p5_next[0] < len(p5q)
                          else len(p5q))
                if p5_next[1] < lagged:
                    p5q[p5_next[1]][1]()
                    p5_next[1] += 1

        p3 = ExitStack()
        E3 = p3.enter_context
        kr_sb = E3(nc.sbuf_tensor("kr_sb", [128, 2, HC, 8, 128], BF16))
        v_sb = E3(nc.sbuf_tensor("v_sb", [128, 2, 8, 1024], BF16))
        msk_sb = E3(nc.sbuf_tensor("msk_sb", [128, 2, 8, 128], BF16))
        P_sb = E3(nc.sbuf_tensor("P_sb", [128, 2, 512], BF16))

        s_ps = [E3(nc.psum_tensor(f"s_ps{i}", [128, 512], F32)) for i in range(2)]
        o_ps = [E3(nc.psum_tensor(f"o_ps{i}", [128, 1024], F32)) for i in range(2)]
        den_ps = E3(nc.psum_tensor("den_ps", [128, 16], F32))

        def load_krm(b):
            sl = b % 2
            bc_ = "dB0" if sl == 0 else "dB1"
            for m in range(8):
                src = bass.AP(kr_out, 128 * b + m * (1024 * TOK),
                              [[TOK, 128], [128 * TOK, 8], [1, 128]])
                dma("sync", kr_sb[:, sl, :, m, :], src, bc_)
            src = bass.AP(dmask, b * (8 * 128 * 128),
                          [[128, 128], [128 * 128, 8], [1, 128]])
            return dma("sync", msk_sb[:, sl, :, :], src, bc_)

        def load_v(b):
            sl = b % 2
            bc_ = "dB0" if sl == 0 else "dB1"
            src = bass.AP(v_out, 128 * b * D,
                          [[D, 128], [TOK * D, 8], [1, D]])
            return dma("sync", v_sb[:, sl, :, :], src, bc_)

        def load_block(b):
            d_krm[b] = load_krm(b)
            d_blk[b] = load_v(b)

        d_krm = {}
        d_blk = {}
        wait("sync", "cc", 2)
        d_krm[0] = load_krm(0)
        d_krm[1] = load_krm(1)
        wait("sync", "cc", 3)
        d_blk[0] = load_v(0)
        d_blk[1] = load_v(1)
        pe_blk_end = {}
        ve_blk_end = {}

        c_evac_prev = 0
        last_exp = {0: 0, 1: 0}
        last_pv = {0: 0, 1: 0}
        for b in range(16):
            sl = b % 2
            ntile = 16 - b
            qlo = 128 * b
            chunks = []
            q = qlo
            if ntile % 2 == 1:
                chunks.append((q, 128))
                q += 128
            while q < TOK:
                chunks.append((q, 256))
                q += 256

            wait("tensor", "dB0" if b % 2 == 0 else "dB1", d_krm[b])
            v_waited = False
            for (qoff, qw) in chunks:
                nqs = qw // 128
                diag = (qoff == qlo)
                for pair in range(4):
                    par = pair % 2
                    sps = s_ps[par]
                    wait("tensor", "sc", last_exp[par])
                    for kk in range(2):
                        ks = 2 * pair + kk
                        for dc in range(HC):
                            fn = (lambda e, sps=sps, kk=kk, ks=ks, dc=dc, sl=sl,
                                  qoff=qoff, qw=qw: e.matmul(
                                      sps[:, kk * qw:(kk + 1) * qw],
                                      kr_sb[:, sl, dc, ks, :],
                                      QR[:, dc, qoff:qoff + qw],
                                      start=(dc == 0), stop=(dc == HC - 1),
                                      skip_group_check=True))
                            if kk == 1 and dc == HC - 1:
                                c_s = inc("tensor", "pe", fn)
                            else:
                                em("tensor", fn)
                    wait("scalar", "pe", c_s)
                    wait("scalar", "pe", last_pv[par])
                    last_exp[par] = inc(
                        "scalar", "sc",
                        lambda e, sps=sps, par=par, qw=qw: e.activation(
                            out=P_sb[:, par, 0:2 * qw], in_=sps[:, 0:2 * qw],
                            func=AF.Exp, bias=negB[:, 0:1], scale=1.0 / 32.0))
                    mask_mark = None
                    if diag:
                        wait("vector", "sc", last_exp[par])
                        for kk in range(2):
                            ks = 2 * pair + kk
                            fn = (lambda e, par=par, kk=kk, ks=ks, sl=sl, qw=qw:
                                  e.tensor_tensor(
                                      out=P_sb[:, par, kk * qw:kk * qw + 128],
                                      in0=P_sb[:, par, kk * qw:kk * qw + 128],
                                      in1=msk_sb[:, sl, ks, :], op=ALU.mult))
                            if kk == 1:
                                mask_mark = inc("vector", "ve", fn)
                            else:
                                em("vector", fn)
                    if not v_waited:
                        wait("tensor", "dB0" if b % 2 == 0 else "dB1", d_blk[b])
                        v_waited = True
                    wait("tensor", "sc", last_exp[par])
                    if mask_mark is not None:
                        wait("tensor", "ve", mask_mark)
                    if pair == 0:
                        wait("tensor", "ve", c_evac_prev)
                    for kk in range(2):
                        ks = 2 * pair + kk
                        for qs in range(nqs):
                            lo = kk * qw + qs * 128
                            st0 = (pair == 0 and kk == 0)
                            for dh in range(2):
                                fn = (lambda e, par=par, ks=ks, qs=qs,
                                      lo=lo, dh=dh, sl=sl, st0=st0: e.matmul(
                                          o_ps[qs][:, dh * 512:(dh + 1) * 512],
                                          P_sb[:, par, lo:lo + 128],
                                          v_sb[:, sl, ks, dh * 512:(dh + 1) * 512],
                                          start=st0, stop=True,
                                          skip_group_check=True))
                                em("tensor", fn)
                            fn = (lambda e, par=par, qs=qs, lo=lo,
                                  st0=st0: e.matmul(
                                      den_ps[:, qs:qs + 1],
                                      P_sb[:, par, lo:lo + 128],
                                      ones_col[:, 0:1],
                                      start=(st0 and qs == 0), stop=True,
                                      skip_group_check=True))
                            if kk == 1 and qs == nqs - 1:
                                last_pv[par] = c_pv_last = inc("tensor", "pe", fn)
                            else:
                                em("tensor", fn)
                wait("vector", "pe", c_pv_last)
                for qs in range(nqs):
                    tidx = qoff // 128 + qs
                    if b == 0:
                        em("vector", lambda e, qs=qs, tidx=tidx: e.tensor_copy(
                            out=O_sb[:, tidx, :], in_=o_ps[qs][:, :]))
                        fn = (lambda e, qs=qs, tidx=tidx: e.tensor_copy(
                            out=den_sb[:, tidx:tidx + 1], in_=den_ps[:, qs:qs + 1]))
                    else:
                        em("vector", lambda e, qs=qs, tidx=tidx: e.tensor_tensor(
                            out=O_sb[:, tidx, :], in0=O_sb[:, tidx, :],
                            in1=o_ps[qs][:, :], op=ALU.add))
                        fn = (lambda e, qs=qs, tidx=tidx: e.tensor_tensor(
                            out=den_sb[:, tidx:tidx + 1],
                            in0=den_sb[:, tidx:tidx + 1],
                            in1=den_ps[:, qs:qs + 1], op=ALU.add))
                    if qs == nqs - 1:
                        c_evac_prev = inc("vector", "ve", fn)
                    else:
                        em("vector", fn)
            p5_drain(2 if b < 4 else 4)
            pe_blk_end[b] = n["pe"]
            ve_blk_end[b] = n["ve"]
            if b + 2 <= 15:
                wait("sync", "pe", pe_blk_end[b])
                wait("sync", "ve", ve_blk_end[b])
                load_block(b + 2)

        while p5_next[1] < len(p5q):
            p5_drain(4)
        em("vector", lambda e: e.tensor_scalar(
            out=sg_f[:, :], in0=mg_t[:, 0:22], scalar1=1.0 / 127.0,
            scalar2=None, op0=ALU.mult))
        c_fold = inc("vector", "ve", lambda e: e.tensor_scalar(
            out=su_f[:, :], in0=mu_t[:, 0:22], scalar1=1.0 / 127.0,
            scalar2=None, op0=ALU.mult))
        p3.close()
        p5gu.close()
        join()
        if debug and _PHASE == "attn":
            dma("gpsimd", dbg_qr[:, :],
                v_out[0:2048, :].rearrange("(a b) d -> a (b d)", b=2), "dDBG")
            dma("gpsimd", dbg_xn[:, :],
                v_in[:, :].rearrange("(a b) d -> a (b d)", b=2), "dDBG")
            with nc.sbuf_tensor("dbo0", [128, 1024], F32) as dbo0:
                dma("gpsimd", dbg_den[:, :], den_sb[:, :], "dDBG")
                for t in range(R):
                    c_d = inc("vector", "ve", lambda e, t=t: e.tensor_copy(
                        out=dbo0[:, :], in_=O_sb[:, t, :]))
                    wait("gpsimd", "ve", c_d)
                    dD = dma("gpsimd", dbg_osb[:, t * 1024:(t + 1) * 1024],
                             dbo0[:, :], "dDBG")
                    wait("vector", "dDBG", dD)
        gate("attn")

        xmid = E(nc.sbuf_tensor("xmid", [128, R, 1024], F32, side="right"))
        p4 = ExitStack()
        E4 = p4.enter_context
        xrl = E4(nc.sbuf_tensor("xrl", [128, 2, 1024], F32))
        t3 = E4(nc.sbuf_tensor("t3", [128, 2, 1024], F32))

        c_rd = inc("vector", "ve", lambda e: e.reciprocal(rden[:, :], den_sb[:, :]))
        wait("scalar", "ve", c_rd)
        g_fx = [0, 0]
        g_t3 = [0, 0]
        for t in range(R):
            sl = t % 2
            wait("sync", "ve", g_fx[sl])
            xc = "dX0" if sl == 0 else "dX1"
            d_x = dma("sync", xrl[:, sl, :], x_own[t, :, :], xc)
            wait("scalar", "ve", g_t3[sl])
            c_sc = inc("scalar", "sc", lambda e, t=t, sl=sl: e.activation(
                out=t3[:, sl, :], in_=O_sb[:, t, :], func=AF.Identity,
                scale=rden[:, t:t + 1]))
            wait("vector", xc, d_x)
            wait("vector", "sc", c_sc)
            g_t3[sl] = g_fx[sl] = inc(
                "vector", "ve", lambda e, t=t, sl=sl: e.tensor_tensor(
                    out=xmid[:, t, :], in0=t3[:, sl, :], in1=xrl[:, sl, :],
                    op=ALU.add))
        if debug:
            with nc.sbuf_tensor("dbq", [128, TOK], F32) as dbq, \
                 nc.sbuf_tensor("dbo", [128, 1024], F32) as dbo:
                for dc in range(HC):
                    c_d = inc("vector", "ve", lambda e, dc=dc: e.tensor_copy(
                        out=dbq[:, :], in_=QR[:, dc, :]))
                    wait("gpsimd", "ve", c_d)
                    dD = dma("gpsimd", dbg_qr[dc * 128:(dc + 1) * 128, :],
                             dbq[:, :], "dDBG")
                    wait("vector", "dDBG", dD)
                dma("gpsimd", dbg_sml[:, :], sml[:, :], "dDBG")
                dma("gpsimd", dbg_negb[:, :], negB[:, :], "dDBG")
                dD = dma("gpsimd", dbg_den[:, :], den_sb[:, :], "dDBG")
                for t in range(R):
                    c_d = inc("vector", "ve", lambda e, t=t: e.tensor_copy(
                        out=dbo[:, :], in_=O_sb[:, t, :]))
                    wait("gpsimd", "ve", c_d)
                    dD = dma("gpsimd", dbg_osb[:, t * 1024:(t + 1) * 1024],
                             dbo[:, :], "dDBG")
                    wait("vector", "dDBG", dD)
                dD = dma("gpsimd", dbg_xmid[:, :],
                         xmid[:, :, :].rearrange("p a b -> p (a b)"), "dDBG")
                wait("gpsimd", "dDBG", dD)

        p4.close()
        scope_o.close()
        join()
        gate("xmid")

        p5 = ExitStack()
        E5 = p5.enter_context
        wsd = E5(nc.sbuf_tensor("wsd", [128, 2, IFF], F32))
        vvd = E5(nc.sbuf_tensor("vvd", [128, IFF], F32))
        ffd = E5(nc.sbuf_tensor("ffd", [128, IFF], F32))
        intd = E5(nc.sbuf_tensor("intd", [128, 2, IFF], BF16))
        stgd = E5(nc.sbuf_tensor("stgd", [128, 2, CI, 128], BF16))
        srow = E5(nc.sbuf_tensor("srow", [1, IFF], F32))
        wqd = E5(nc.sbuf_tensor("wqd", [128, IFF], mybir.dt.int32))
        tmpd = E5(nc.sbuf_tensor("tmpd", [128, 4], F32))

        ps5 = ExitStack()
        tps = [ps5.enter_context(nc.psum_tensor(f"tps{i}", [128, 512], BF16))
               for i in range(2)]
        tpf = ps5.enter_context(nc.psum_tensor("tpf", [128, 128], F32))

        g_w = [0, 0]
        g_int = [0, 0]
        g_stW = [0, 0]
        g_tp = [0, 0]
        for c in range(8):
            slw = c % 2
            pc_ = "dP0" if slw == 0 else "dP1"
            ic_ = "dI0" if slw == 0 else "dI1"
            wait("sync", "ve", g_w[slw])
            d_w = dma("sync", wsd[:, slw, :],
                      wd_oi[c * 128:(c + 1) * 128, :], pc_)
            wait("vector", pc_, d_w)
            em("vector", lambda e, c=c, slw=slw: e.reduce_max(
                out=md_t[:, c:c + 1], in_=wsd[:, slw, :],
                axis=AX, apply_absolute_value=True))
            vpad()
            em("vector", lambda e, c=c, slw=slw: e.tensor_scalar(
                out=tmpd[:, 2 * slw:2 * slw + 1], in0=md_t[:, c:c + 1],
                scalar1=1.0 / 127.0, scalar2=None, op0=ALU.mult))
            vpad()
            em("vector", lambda e, slw=slw: e.reciprocal(
                tmpd[:, 2 * slw + 1:2 * slw + 2],
                tmpd[:, 2 * slw:2 * slw + 1]))
            vpad()
            g_w[slw] = inc("vector", "ve", lambda e, slw=slw: e.tensor_scalar(
                out=vvd[:, :], in0=wsd[:, slw, :],
                scalar1=tmpd[:, 2 * slw + 1:2 * slw + 2],
                scalar2=None, op0=ALU.mult))
            em("vector", lambda e: e.tensor_copy(out=wqd[:, :], in_=vvd[:, :]))
            em("vector", lambda e: e.tensor_copy(out=ffd[:, :], in_=wqd[:, :]))
            em("vector", lambda e: e.tensor_tensor(
                out=vvd[:, :], in0=vvd[:, :], in1=ffd[:, :], op=ALU.subtract))
            em("vector", lambda e: e.scalar_tensor_tensor(
                out=ffd[:, :], in0=vvd[:, :], scalar=0.5, in1=ffd[:, :],
                op0=ALU.is_ge, op1=ALU.add))
            em("vector", lambda e: e.tensor_scalar(
                out=vvd[:, :], in0=vvd[:, :], scalar1=-0.5, scalar2=None,
                op0=ALU.is_lt))
            wait("vector", "pe", g_int[slw])
            c_q = inc("vector", "ve", lambda e, slw=slw: e.tensor_tensor(
                out=intd[:, slw, :], in0=ffd[:, :], in1=vvd[:, :],
                op=ALU.subtract))
            wait("tensor", "ve", c_q)
            first_cp = True
            for g0 in range(0, CI, 4):
                gw = min(4, CI - g0)
                pb = (g0 // 4) % 2
                wait("tensor", "sc", g_tp[pb])
                for k in range(gw):
                    fn = (lambda e, pb=pb, k=k, g0=g0, slw=slw: e.transpose(
                        tps[pb][:, k * 128:(k + 1) * 128],
                        intd[:, slw, (g0 + k) * 128:(g0 + k + 1) * 128],
                        ident_bs[:, :]))
                    if k == gw - 1:
                        c_tp = inc("tensor", "pe", fn)
                    else:
                        em("tensor", fn)
                if g0 + 4 >= CI:
                    g_int[slw] = c_tp
                wait("scalar", "pe", c_tp)
                if first_cp:
                    wait("scalar", ic_, g_stW[slw])
                    first_cp = False
                g_tp[pb] = inc("scalar", "sc",
                               lambda e, pb=pb, g0=g0, gw=gw, slw=slw:
                               e.copy(out=stgd[:, slw, g0:g0 + gw, :],
                                      in_=tps[pb][:, 0:gw * 128].rearrange(
                                          "p (a b) -> p a b", a=gw)))
            wait("sync", "sc", g_tp[(max(0, CI - 4) // 4) % 2])
            dstap = bass.AP(id_scr, c * 128,
                            [[D, 128], [128 * D, CI], [1, 128]])
            g_stW[slw] = dma("sync", dstap, stgd[:, slw, 0:CI, :], ic_)

        wait("tensor", "sc", n["sc"])
        wait("tensor", "ve", n["ve"])
        cpe = inc("tensor", "pe", lambda e: e.transpose(
            tpf[:, :], md_t[:, :], ident_fs[:, :]))
        wait("scalar", "pe", cpe)
        c_gth = inc("scalar", "sc", lambda e: e.copy(
            out=gath[:, :], in_=tpf[:, :]))
        wait("sync", "sc", c_gth)
        d_row = dma("sync", srow[0:1, 0:1024], gath[0:8, 0:128], "dRW")
        wait("vector", "dRW", d_row)
        c_sdr = inc("vector", "ve", lambda e: e.tensor_scalar(
            out=sdrow[0:1, 0:D], in0=srow[0:1, 0:D], scalar1=1.0 / 127.0,
            scalar2=None, op0=ALU.mult))
        ps5.close()
        p5.close()
        join()
        gate("wprep")

        p6 = ExitStack()
        E6 = p6.enter_context
        hq = E6(nc.sbuf_tensor("hq", [128, CI, TOK], F16))
        s6ab = ExitStack()
        tq1 = s6ab.enter_context(nc.sbuf_tensor("tq1", [128, 1024], F32))
        tq2 = s6ab.enter_context(nc.sbuf_tensor("tq2", [128, 1024], F32))
        xqt = s6ab.enter_context(nc.sbuf_tensor("xqt", [128, 1024], BF16))
        xqi = s6ab.enter_context(nc.sbuf_tensor("xqi", [128, 1024], mybir.dt.int32))

        def hqb(ci, lo, hi):
            return hq[:, ci, lo:hi].bitcast(BF16)

        ps6a = ExitStack()
        tpf6 = ps6a.enter_context(nc.psum_tensor("tpf6", [128, 128], F32))
        for t in range(R):
            c_sq = inc("scalar", "sc", lambda e, t=t: e.activation(
                out=tq1[:, :], in_=xmid[:, t, :], func=AF.Square))
            wait("vector", "sc", c_sq)
            em("vector", lambda e, t=t: e.reduce_sum(
                lnm[:, t:t + 1], xmid[:, t, :], axis=AX))
            em("vector", lambda e, t=t: e.reduce_sum(
                lnr[:, t:t + 1], tq1[:, :], axis=AX))
            em("vector", lambda e, t=t: e.tensor_scalar(
                out=lnm[:, t:t + 1], in0=lnm[:, t:t + 1], scalar1=1.0 / D,
                scalar2=None, op0=ALU.mult))
            em("vector", lambda e, t=t: e.tensor_scalar(
                out=lnr[:, t:t + 1], in0=lnr[:, t:t + 1], scalar1=1.0 / D,
                scalar2=None, op0=ALU.mult))
            em("vector", lambda e, t=t: e.scalar_tensor_tensor(
                out=bc1[:, 4:5], in0=lnm[:, t:t + 1], scalar=-1.0,
                in1=lnm[:, t:t + 1], op0=ALU.mult, op1=ALU.mult))
            vpad()
            c_var = inc("vector", "ve", lambda e, t=t: e.tensor_tensor(
                out=lnr[:, t:t + 1], in0=lnr[:, t:t + 1], in1=bc1[:, 4:5],
                op=ALU.add))
            wait("scalar", "ve", c_var)
            c_rt = inc("scalar", "sc", lambda e, t=t: e.activation(
                out=bc1[:, 5:6], in_=lnr[:, t:t + 1], func=AF.Sqrt,
                bias=epsb[:, 0:1]))
            wait("vector", "sc", c_rt)
            em("vector", lambda e: e.reciprocal(bc1[:, 6:7], bc1[:, 5:6]))
            em("vector", lambda e, t=t: e.tensor_scalar(
                out=bc1[:, 7:8], in0=lnr[:, t:t + 1], scalar1=EPS, scalar2=-0.5,
                op0=ALU.add, op1=ALU.mult))
            em("vector", lambda e: e.tensor_tensor(
                out=bc1[:, 4:5], in0=bc1[:, 6:7], in1=bc1[:, 6:7], op=ALU.mult))
            vpad()
            em("vector", lambda e: e.tensor_tensor(
                out=bc1[:, 4:5], in0=bc1[:, 4:5], in1=bc1[:, 7:8], op=ALU.mult))
            vpad()
            em("vector", lambda e: e.tensor_scalar(
                out=bc1[:, 4:5], in0=bc1[:, 4:5], scalar1=1.5, scalar2=None,
                op0=ALU.add))
            vpad()
            em("vector", lambda e, t=t: e.tensor_tensor(
                out=lnr[:, t:t + 1], in0=bc1[:, 6:7], in1=bc1[:, 4:5],
                op=ALU.mult))
            em("vector", lambda e, t=t: e.tensor_reduce(
                out=bc1[:, 4:5], in_=xmid[:, t, :], axis=AX, op=ALU.max))
            em("vector", lambda e, t=t: e.tensor_reduce(
                out=bc1[:, 5:6], in_=xmid[:, t, :], axis=AX, op=ALU.min))
            em("vector", lambda e, t=t: e.tensor_tensor(
                out=bc1[:, 4:5], in0=bc1[:, 4:5], in1=lnm[:, t:t + 1],
                op=ALU.subtract))
            em("vector", lambda e, t=t: e.tensor_tensor(
                out=bc1[:, 5:6], in0=lnm[:, t:t + 1], in1=bc1[:, 5:6],
                op=ALU.subtract))
            vpad()
            em("vector", lambda e: e.tensor_tensor(
                out=bc1[:, 4:5], in0=bc1[:, 4:5], in1=bc1[:, 5:6], op=ALU.max))
            vpad()
            em("vector", lambda e, t=t: e.tensor_tensor(
                out=bc1[:, 4:5], in0=bc1[:, 4:5], in1=lnr[:, t:t + 1],
                op=ALU.mult))
            vpad()
            if t == 0:
                em("vector", lambda e: e.tensor_copy(out=hmax[:, :],
                                                     in_=bc1[:, 4:5]))
            else:
                em("vector", lambda e: e.tensor_tensor(
                    out=hmax[:, :], in0=hmax[:, :], in1=bc1[:, 4:5], op=ALU.max))
        pmax_to_sml(hmax[:, 0:1], 11, tpf6)
        c_m1 = inc("vector", "ve", lambda e: e.tensor_copy(
            out=stage16[0:1, 0:1], in_=sml[0:1, 11:12]))
        wait("gpsimd", "ve", c_m1)
        d_m1 = dma("gpsimd", m1_in[:, :], stage16[:, :], "dCC")
        wait("gpsimd", "dCC", d_m1)
        n["cc"] += 1
        em("gpsimd", lambda e: e.collective_compute(
            "AllReduce", ALU.max, replica_groups=[list(range(NCORE))],
            ins=[m1_in.ap()], outs=[m1_out.ap()]).then_inc(sem["cc"], 1))
        wait("gpsimd", "cc", n["cc"])
        d_m1o = dma("gpsimd", stage16[:, :], m1_out[:, :], "dCC")
        wait("scalar", "dCC", d_m1o)
        c_sg1 = inc("scalar", "sc", lambda e: e.activation(
            out=sml[0:1, 10:11], in_=clips_sb[0:1, 0:1], func=AF.Sigmoid))
        wait("vector", "sc", c_sg1)
        em("vector", lambda e: e.scalar_tensor_tensor(
            out=sml[0:1, 5:6], in0=sml[0:1, 10:11], scalar=1.0 / 127.0,
            in1=stage16[0:1, 0:1], op0=ALU.mult, op1=ALU.mult))
        vpad()
        c_s1 = inc("vector", "ve", lambda e: e.reciprocal(
            sml[0:1, 6:7], sml[0:1, 5:6]))
        wait("tensor", "ve", c_s1)
        bcast(sml[0:1, 6:7], 0, tpf6)
        c_bc1 = bcast(sml[0:1, 5:6], 1, tpf6)
        wait("vector", "sc", c_bc1)
        em("vector", lambda e: e.tensor_scalar(
            out=s1sg[:, :], in0=sg_f[:, :], scalar1=bc1[:, 1:2], scalar2=None,
            op0=ALU.mult))
        c_s1f = inc("vector", "ve", lambda e: e.tensor_scalar(
            out=s1su[:, :], in0=su_f[:, :], scalar1=bc1[:, 1:2], scalar2=None,
            op0=ALU.mult))
        ps6a.close()

        ps6b = ExitStack()
        trb = ps6b.enter_context(nc.psum_tensor("trb", [128, 512], BF16))
        trb2 = ps6b.enter_context(nc.psum_tensor("trb2", [128, 512], BF16))
        g_tr6 = [0, 0]
        g_xq = 0
        for t in range(R):
            em("vector", lambda e, t=t: e.tensor_scalar(
                out=bc1[:, 4:5], in0=lnr[:, t:t + 1], scalar1=bc1[:, 0:1],
                scalar2=None, op0=ALU.mult))
            vpad()
            c_ab = inc("vector", "ve", lambda e, t=t: e.scalar_tensor_tensor(
                out=bc1[:, 5:6], in0=lnm[:, t:t + 1], scalar=-1.0,
                in1=bc1[:, 4:5], op0=ALU.mult, op1=ALU.mult))
            wait("scalar", "ve", c_ab)
            c_v = inc("scalar", "sc", lambda e, t=t: e.activation(
                out=tq1[:, :], in_=xmid[:, t, :], func=AF.Identity,
                bias=bc1[:, 5:6], scale=bc1[:, 4:5]))
            wait("vector", "sc", c_v)
            em("vector", lambda e: e.tensor_copy(out=xqi[:, :], in_=tq1[:, :]))
            em("vector", lambda e: e.tensor_copy(out=tq2[:, :], in_=xqi[:, :]))
            em("vector", lambda e: e.tensor_tensor(
                out=tq1[:, :], in0=tq1[:, :], in1=tq2[:, :], op=ALU.subtract))
            em("vector", lambda e: e.scalar_tensor_tensor(
                out=tq2[:, :], in0=tq1[:, :], scalar=0.5, in1=tq2[:, :],
                op0=ALU.is_ge, op1=ALU.add))
            em("vector", lambda e: e.tensor_scalar(
                out=tq1[:, :], in0=tq1[:, :], scalar1=-0.5, scalar2=None,
                op0=ALU.is_lt))
            em("vector", lambda e: e.tensor_tensor(
                out=tq1[:, :], in0=tq2[:, :], in1=tq1[:, :], op=ALU.subtract))
            wait("vector", "pe", g_xq)
            c_xq = inc("vector", "ve", lambda e: e.tensor_scalar(
                out=xqt[:, :], in0=tq1[:, :], scalar1=127.0, scalar2=-128.0,
                op0=ALU.min, op1=ALU.max))
            wait("tensor", "ve", c_xq)
            for g in range(2):
                tp = trb if g == 0 else trb2
                wait("tensor", "sc", g_tr6[g])
                for k4 in range(4):
                    hc = g * 4 + k4
                    fn = (lambda e, tp=tp, k4=k4, hc=hc: e.transpose(
                        tp[:, k4 * 128:(k4 + 1) * 128],
                        xqt[:, hc * 128:(hc + 1) * 128], ident_bs[:, :]))
                    if k4 == 3:
                        g_xq = inc("tensor", "pe", fn)
                    else:
                        em("tensor", fn)
                wait("scalar", "pe", g_xq)
                g_tr6[g] = inc("scalar", "sc", lambda e, tp=tp, g=g, t=t: e.copy(
                    out=QR[:, g * 4:(g + 1) * 4, t * 128:(t + 1) * 128],
                    in_=tp[:, :].rearrange("p (a b) -> p a b", a=4)))
        ps6b.close()
        s6ab.close()
        join()
        if debug and _PHASE == "xq":
            with nc.sbuf_tensor("dbq6", [128, TOK], F32) as dbq6:
                dma("gpsimd", dbg_sml[:, :], sml[:, :], "dDBG")
                for dc in range(HC):
                    c_d = inc("vector", "ve", lambda e, dc=dc: e.tensor_copy(
                        out=dbq6[:, :], in_=QR[:, dc, :]))
                    wait("gpsimd", "ve", c_d)
                    dD = dma("gpsimd", dbg_qr[dc * 128:(dc + 1) * 128, :],
                             dbq6[:, :], "dDBG")
                    wait("vector", "dDBG", dD)
        gate("xq")

        ps6c = ExitStack()
        g_psm = [ps6c.enter_context(nc.psum_tensor(f"g_ps{i}", [128, 512], F32))
                 for i in range(2)]
        u_psm = [ps6c.enter_context(nc.psum_tensor(f"u_ps{i}", [128, 512], F32))
                 for i in range(2)]
        igt = ps6c.enter_context(nc.sbuf_tensor("igt", [128, 2, HC, 128], BF16))
        iut = ps6c.enter_context(nc.sbuf_tensor("iut", [128, 2, HC, 128], BF16))
        sgs = ps6c.enter_context(nc.sbuf_tensor("sgs", [128, 2, 512], F32))

        g_slab = [0, 0]
        g_gu = [0, 0]
        g_sgs = [0, 0]
        wait("scalar", "ve", c_s1f)
        ii = 0
        for ci in range(CI):
            sl2 = ci % 2
            wait("sync", "pe", g_slab[sl2])
            src = bass.AP(ig_scr, ci * 128,
                          [[IFF, 128], [128 * IFF, HC], [1, 128]])
            gc_ = "dIG0" if sl2 == 0 else "dIG1"
            dma("sync", igt[:, sl2, :, :], src, gc_)
            src = bass.AP(iu_scr, ci * 128,
                          [[IFF, 128], [128 * IFF, HC], [1, 128]])
            d_slab = dma("sync", iut[:, sl2, :, :], src, gc_)
            wait("tensor", gc_, d_slab)
            wait("tensor", "ve", c_s1f)
            for tt4 in range(4):
                sl = ii % 2
                ii += 1
                wait("tensor", "ve", g_gu[sl])
                for (psm, wt) in ((g_psm, igt), (u_psm, iut)):
                    for hc in range(HC):
                        fn = (lambda e, psm=psm, wt=wt, hc=hc, sl=sl, sl2=sl2,
                              tt4=tt4: e.matmul(
                                  psm[sl][:, :], wt[:, sl2, hc, :],
                                  QR[:, hc, tt4 * 512:(tt4 + 1) * 512],
                                  start=(hc == 0), stop=(hc == HC - 1),
                                  skip_group_check=True))
                        if psm is u_psm and hc == HC - 1:
                            c_gu = inc("tensor", "pe", fn)
                        else:
                            em("tensor", fn)
                if tt4 == 3:
                    g_slab[sl2] = c_gu
                wait("scalar", "pe", c_gu)
                wait("scalar", "ve", g_sgs[sl])
                c_silu = inc("scalar", "sc", lambda e, sl=sl, ci=ci: e.activation(
                    out=sgs[:, sl, :], in_=g_psm[sl][:, :], func=AF.Silu,
                    scale=s1sg[:, ci:ci + 1]))
                wait("vector", "sc", c_silu)
                em("vector", lambda e, sl=sl, ci=ci, tt4=tt4: e.scalar_tensor_tensor(
                    out=hq[:, ci, tt4 * 512:(tt4 + 1) * 512], in0=u_psm[sl][:, :],
                    scalar=s1su[:, ci:ci + 1], in1=sgs[:, sl, :], op0=ALU.mult,
                    op1=ALU.mult))
                g_sgs[sl] = g_gu[sl] = c_h = inc(
                    "vector", "ve",
                    lambda e, ci=ci, tt4=tt4: e.reduce_max(
                        out=bc1[:, 4:5], in_=hq[:, ci, tt4 * 512:(tt4 + 1) * 512],
                        axis=AX, apply_absolute_value=True))
                vpad()
                if ci == 0 and tt4 == 0:
                    em("vector", lambda e: e.tensor_copy(out=hmax[:, :],
                                                         in_=bc1[:, 4:5]))
                else:
                    em("vector", lambda e: e.tensor_tensor(
                        out=hmax[:, :], in0=hmax[:, :], in1=bc1[:, 4:5],
                        op=ALU.max))
        ps6c.close()

        s6d = ExitStack()
        tq1 = s6d.enter_context(nc.sbuf_tensor("tq1b", [128, 1024], F32))
        tq2 = s6d.enter_context(nc.sbuf_tensor("tq2b", [128, 1024], F32))
        hqi = s6d.enter_context(nc.sbuf_tensor("hqi", [128, 1024], mybir.dt.int32))
        ps6d = ExitStack()
        tpf7 = ps6d.enter_context(nc.psum_tensor("tpf7", [128, 128], F32))
        bcp7 = ps6d.enter_context(nc.psum_tensor("bcp7", [128, 512], F32))
        pmax_to_sml(hmax[:, 0:1], 11, tpf7)
        c_m2 = inc("vector", "ve", lambda e: e.tensor_copy(
            out=stage16[0:1, 0:1], in_=sml[0:1, 11:12]))
        wait("gpsimd", "ve", c_m2)
        d_m2 = dma("gpsimd", m2_in[:, :], stage16[:, :], "dCC")
        wait("gpsimd", "dCC", d_m2)
        n["cc"] += 1
        em("gpsimd", lambda e: e.collective_compute(
            "AllReduce", ALU.max, replica_groups=[list(range(NCORE))],
            ins=[m2_in.ap()], outs=[m2_out.ap()]).then_inc(sem["cc"], 1))
        wait("gpsimd", "cc", n["cc"])
        d_m2o = dma("gpsimd", stage16[:, :], m2_out[:, :], "dCC")
        wait("scalar", "dCC", d_m2o)
        c_sg2 = inc("scalar", "sc", lambda e: e.activation(
            out=sml[0:1, 10:11], in_=clips_sb[0:1, 1:2], func=AF.Sigmoid))
        wait("vector", "sc", c_sg2)
        em("vector", lambda e: e.scalar_tensor_tensor(
            out=sml[0:1, 8:9], in0=sml[0:1, 10:11], scalar=1.0 / 127.0,
            in1=stage16[0:1, 0:1], op0=ALU.mult, op1=ALU.mult))
        vpad()
        c_s2 = inc("vector", "ve", lambda e: e.reciprocal(
            sml[0:1, 9:10], sml[0:1, 8:9]))
        wait("tensor", "ve", c_s2)
        bcast(sml[0:1, 9:10], 2, tpf7)
        c_sd1 = inc("vector", "ve", lambda e: e.tensor_scalar(
            out=sdrow[0:1, :], in0=sdrow[0:1, :], scalar1=sml[0:1, 8:9],
            scalar2=None, op0=ALU.mult))
        ps6d.close()

        wait("vector", "sc", n["sc"])
        for ci in range(CI):
            for th in range(2):
                lo, hi = th * 1024, (th + 1) * 1024
                em("vector", lambda e, ci=ci, lo=lo, hi=hi: e.tensor_scalar(
                    out=tq1[:, :], in0=hq[:, ci, lo:hi], scalar1=bc1[:, 2:3],
                    scalar2=None, op0=ALU.mult))
                em("vector", lambda e: e.tensor_copy(out=hqi[:, :], in_=tq1[:, :]))
                em("vector", lambda e: e.tensor_copy(out=tq2[:, :], in_=hqi[:, :]))
                em("vector", lambda e: e.tensor_tensor(
                    out=tq1[:, :], in0=tq1[:, :], in1=tq2[:, :], op=ALU.subtract))
                em("vector", lambda e: e.scalar_tensor_tensor(
                    out=tq2[:, :], in0=tq1[:, :], scalar=0.5, in1=tq2[:, :],
                    op0=ALU.is_ge, op1=ALU.add))
                em("vector", lambda e: e.tensor_scalar(
                    out=tq1[:, :], in0=tq1[:, :], scalar1=-0.5, scalar2=None,
                    op0=ALU.is_lt))
                em("vector", lambda e: e.tensor_tensor(
                    out=tq1[:, :], in0=tq2[:, :], in1=tq1[:, :], op=ALU.subtract))
                em("vector", lambda e, ci=ci, lo=lo, hi=hi: e.tensor_scalar(
                    out=hqb(ci, lo, hi), in0=tq1[:, :], scalar1=127.0,
                    scalar2=-128.0, op0=ALU.min, op1=ALU.max))
        if debug:
            for g0 in range(0, CI, 8):
                g1 = min(g0 + 8, CI)
                for ci in range(g0, g1):
                    em("vector", lambda e, ci=ci, g0=g0: e.tensor_copy(
                        out=tq1[:, (ci - g0) * 128:(ci - g0 + 1) * 128],
                        in_=hqb(ci, 0, 128)))
                c_d = inc("vector", "ve", lambda e: e.engine_nop())
                wait("gpsimd", "ve", c_d)
                dD = dma("gpsimd", dbg_hq[:, g0 * 128:g1 * 128],
                         tq1[:, 0:(g1 - g0) * 128], "dDBG")
                wait("vector", "dDBG", dD)
        c_hq = inc("vector", "ve", lambda e: e.engine_nop())
        s6d.close()
        join()
        gate("hq")

        sdb = E6(nc.sbuf_tensor("sdb", [128, 1024], F32))
        t6e = E6(nc.sbuf_tensor("t6e", [128, 1024], F32))
        idt = E6(nc.sbuf_tensor("idt", [128, 2, 1024], BF16))
        wait("tensor", "ve", c_sd1)
        with nc.psum_tensor("bcpE", [128, 512], F32) as bcpE:
            for k in range(2):
                c_bs = inc("tensor", "pe", lambda e, k=k: e.matmul(
                    bcpE[:, :], ones_row[0:1, :],
                    sdrow[0:1, k * 512:(k + 1) * 512],
                    start=True, stop=True, skip_group_check=True))
                wait("vector", "pe", c_bs)
                c_bv = inc("vector", "ve", lambda e, k=k: e.tensor_copy(
                    out=sdb[:, k * 512:(k + 1) * 512], in_=bcpE[:, :]))
                wait("tensor", "ve", c_bv)
        ps6e = ExitStack()
        o4 = [ps6e.enter_context(nc.psum_tensor(f"o4_{i}", [128, 1024], F32))
              for i in range(4)]
        g_id = [0, 0]
        d_out = 0
        for tg in range(4):
            for ci in range(CI):
                sl = ci % 2
                wait("sync", "pe", g_id[sl])
                dc_ = "dID0" if sl == 0 else "dID1"
                d_id = dma("sync", idt[:, sl, :], id_scr[ci, :, :], dc_)
                wait("tensor", dc_, d_id)
                if ci == 0:
                    wait("tensor", "ve", n["ve"])
                for tt in range(4):
                    t = tg * 4 + tt
                    for dh in range(2):
                        fn = (lambda e, tt=tt, t=t, ci=ci, sl=sl, dh=dh: e.matmul(
                            o4[tt][:, dh * 512:(dh + 1) * 512],
                            hqb(ci, t * 128, (t + 1) * 128),
                            idt[:, sl, dh * 512:(dh + 1) * 512],
                            start=(ci == 0), stop=True,
                            skip_group_check=True))
                        if tt == 3 and dh == 1:
                            c_dn = inc("tensor", "pe", fn)
                        else:
                            em("tensor", fn)
                g_id[sl] = c_dn
            wait("vector", "pe", c_dn)
            for tt in range(4):
                t = tg * 4 + tt
                em("vector", lambda e, tt=tt: e.tensor_tensor(
                    out=t6e[:, :], in0=o4[tt][:, :], in1=sdb[:, :],
                    op=ALU.mult))
                c_fin = inc("vector", "ve", lambda e, t=t: e.tensor_tensor(
                    out=xmid[:, t, :], in0=xmid[:, t, :], in1=t6e[:, :],
                    op=ALU.add))
                wait("gpsimd", "ve", c_fin)
                d_out = dma("gpsimd", out_t[t * 128:(t + 1) * 128, :],
                            xmid[:, t, :], "dOUT")
            wait("tensor", "ve", n["ve"])
        ps6e.close()
        p6.close()

        wait("gpsimd", "dOUT", d_out)
        wait("sync", "dOUT", d_out)


    except _Trunc:
        st.pop_all()

    with nc.Block() as block:
        clr = range(min(s_.num for s_ in sem.values()), 256)
        nc.gpsimd.dma_reset(clr)
        nc.gpsimd.sem_clear(clr)
        nc.all_engine_barrier()

        @block.sync
        def _(eng):
            for fn in prog["sync"]:
                fn(eng)

        @block.tensor
        def _(eng):
            for fn in prog["tensor"]:
                fn(eng)

        @block.vector
        def _(eng):
            for fn in prog["vector"]:
                fn(eng)

        @block.scalar
        def _(eng):
            for fn in prog["scalar"]:
                fn(eng)

        @block.gpsimd
        def _(eng):
            for fn in prog["gpsimd"]:
                fn(eng)

    st.close()
    global _LAST_N, _LAST_SEM
    _LAST_N = dict(n)
    _LAST_SEM = {k: v for k, v in sem.items()}
    return nc



_CACHE = {}


def _host_prep(L0_attn_x, L0_attn_wq, L0_attn_wk, L0_attn_wv, L0_ffn_clip_x,
               L0_ffn_clip_down, L0_ffn_w_gate, L0_ffn_w_up, L0_ffn_w_down):
    x = np.ascontiguousarray(np.asarray(L0_attn_x, dtype=np.float32)
                             .reshape(S, D))
    wq = np.asarray(L0_attn_wq, dtype=np.float32)
    wk = np.asarray(L0_attn_wk, dtype=np.float32)
    wv = np.asarray(L0_attn_wv, dtype=np.float32)
    wg = np.asarray(L0_ffn_w_gate, dtype=np.float32)
    wu = np.asarray(L0_ffn_w_up, dtype=np.float32)
    wd = np.asarray(L0_ffn_w_down, dtype=np.float32)

    perm = np.concatenate([np.arange(0, D, 2), np.arange(1, D, 2)])
    wq_t = np.ascontiguousarray(wq[perm].T)
    wk_t = np.ascontiguousarray(wk[perm].T)
    wv_t = np.ascontiguousarray(wv.T)

    ident_b = np.eye(128, dtype=ml_dtypes.bfloat16)
    ident_f = np.eye(128, dtype=np.float32)
    i_ = np.arange(512, dtype=np.float64)
    phi64 = np.power(10000.0, -2.0 * (i_ - 1.0) / 1024.0)
    phi_hi = phi64.astype(np.float32)
    phi_lo = (phi64 - phi_hi.astype(np.float64)).astype(np.float32)
    phi_hi = np.ascontiguousarray(phi_hi.reshape(4, 128).T)
    phi_lo = np.ascontiguousarray(phi_lo.reshape(4, 128).T)
    clips = np.array([[np.float32(np.asarray(L0_ffn_clip_x).ravel()[0]),
                       np.float32(np.asarray(L0_ffn_clip_down).ravel()[0])]],
                     dtype=np.float32)

    in_maps = []
    rows_of = []
    for j in range(NCORE):
        rows = np.concatenate([
            np.arange(1024 * (r - 1) + 128 * ((j + r) % 8),
                      1024 * (r - 1) + 128 * ((j + r) % 8) + 128)
            for r in range(1, R + 1)])
        rows_of.append(rows)
        x_own = np.ascontiguousarray(x[rows]).reshape(R, 128, D)
        pos_own = rows.astype(np.float32).reshape(1, TOK)
        dm = np.zeros((R, 8, 128, 128), dtype=np.float32)
        tri = np.triu(np.ones((128, 128), dtype=np.float32))
        for r in range(1, R + 1):
            o = (j + r) % 8
            for m in range(8):
                op = (m + r) % 8
                if op < o:
                    dm[r - 1, m] = 1.0
                elif op == o:
                    dm[r - 1, m] = tri
        in_maps.append({
            "x_own": x_own, "pos_own": pos_own,
            "dmask": dm.astype(ml_dtypes.bfloat16),
            "phi_hi": phi_hi, "phi_lo": phi_lo,
            "ident_b": ident_b, "ident_f": ident_f,
            "clips": clips, "wq_t": wq_t, "wk_t": wk_t, "wv_t": wv_t,
            "wg_ih": wg, "wu_ih": wu, "wd_oi": wd,
        })
    return in_maps, rows_of


def kernel(L0_attn_x, L0_attn_wq, L0_attn_wk, L0_attn_wv, L0_ffn_clip_x,
           L0_ffn_clip_down, L0_ffn_w_gate, L0_ffn_w_up, L0_ffn_w_down):
    in_maps, rows_of = _host_prep(
        L0_attn_x, L0_attn_wq, L0_attn_wk, L0_attn_wv, L0_ffn_clip_x,
        L0_ffn_clip_down, L0_ffn_w_gate, L0_ffn_w_up, L0_ffn_w_down)
    if "nc" not in _CACHE:
        _CACHE["nc"] = build_nc()
    nc = _CACHE["nc"]
    res = run_bass_kernel_spmd(nc, in_maps, core_ids=list(range(NCORE)))
    out = np.empty((S, D), dtype=np.float32)
    for j in range(NCORE):
        out[rows_of[j]] = res.results[j]["out"]
    return out.reshape(1, S, D)


# version 12
# speedup vs baseline: 1.1128x; 1.0437x over previous
import math
import os
from contextlib import ExitStack

import numpy as np
import ml_dtypes

import concourse.bass as bass
import concourse.mybir as mybir
from concourse.bass_utils import run_bass_kernel_spmd

F32 = mybir.dt.float32
BF16 = mybir.dt.bfloat16
F16 = mybir.dt.float16
AF = mybir.ActivationFunctionType
ALU = mybir.AluOpType
AX = mybir.AxisListType.X

NCORE = 8
S = 16384
D = 1024
R = 16
TOK = 2048
HC = 8
IFF = 2816
CI = 22
EPS = 1e-5

LN1E4 = float(np.log(10000.0))
TWO_PI = 2.0 * math.pi
C1 = float(np.float32(int(TWO_PI * 2**9) / 2**9))
C2 = float(np.float32(int((TWO_PI - C1) * 2**20) / 2**20))
C3 = float(TWO_PI - C1 - C2)
PI = float(math.pi)
HALF_PI = float(math.pi / 2)
INV_2PI = float(1.0 / TWO_PI)

ENGS = ("sync", "tensor", "vector", "scalar", "gpsimd")


def build_nc(debug=False):
    nc = bass.Bass("TRN2", target_bir_lowering=False, debug=False,
                   detect_race_conditions=False)

    x_own = nc.dram_tensor("x_own", [R, 128, D], F32, kind="ExternalInput")
    pos_own = nc.dram_tensor("pos_own", [1, TOK], F32, kind="ExternalInput")
    dmask = nc.dram_tensor("dmask", [R, 8, 128, 128], BF16, kind="ExternalInput")
    phi_hi = nc.dram_tensor("phi_hi", [128, 4], F32, kind="ExternalInput")
    phi_lo = nc.dram_tensor("phi_lo", [128, 4], F32, kind="ExternalInput")
    ident_b = nc.dram_tensor("ident_b", [128, 128], BF16, kind="ExternalInput")
    ident_f = nc.dram_tensor("ident_f", [128, 128], F32, kind="ExternalInput")
    clips = nc.dram_tensor("clips", [1, 2], F32, kind="ExternalInput")
    wq_t = nc.dram_tensor("wq_t", [D, D], F32, kind="ExternalInput")
    wk_t = nc.dram_tensor("wk_t", [D, D], F32, kind="ExternalInput")
    wv_t = nc.dram_tensor("wv_t", [D, D], F32, kind="ExternalInput")
    wg_ih = nc.dram_tensor("wg_ih", [IFF, D], F32, kind="ExternalInput")
    wu_ih = nc.dram_tensor("wu_ih", [IFF, D], F32, kind="ExternalInput")
    wd_oi = nc.dram_tensor("wd_oi", [D, IFF], F32, kind="ExternalInput")

    out_t = nc.dram_tensor("out", [TOK, D], F32, kind="ExternalOutput")
    if debug:
        dbg_qr = nc.dram_tensor("dbg_qr", [HC * 128, TOK], F32, kind="ExternalOutput")
        dbg_sml = nc.dram_tensor("dbg_sml", [1, 64], F32, kind="ExternalOutput")
        dbg_negb = nc.dram_tensor("dbg_negb", [128, 1], F32, kind="ExternalOutput")
        dbg_den = nc.dram_tensor("dbg_den", [128, 16], F32, kind="ExternalOutput")
        dbg_osb = nc.dram_tensor("dbg_osb", [128, R * 1024], F32, kind="ExternalOutput")
        dbg_xmid = nc.dram_tensor("dbg_xmid", [128, R * 1024], F32, kind="ExternalOutput")
        dbg_hq = nc.dram_tensor("dbg_hq", [128, CI * 128], F32, kind="ExternalOutput")
        dbg_xn = nc.dram_tensor("dbg_xn", [HC * 128, TOK], F32, kind="ExternalOutput")
        dbg_nrm = nc.dram_tensor("dbg_nrm", [3, 16], F32, kind="ExternalOutput")
        dbg_row = nc.dram_tensor("dbg_row", [1, 64], F32, kind="ExternalOutput")
        dbg_xt = nc.dram_tensor("dbg_xt", [2, 128, 1024], F32, kind="ExternalOutput")
        dbg_xnt = nc.dram_tensor("dbg_xnt", [2, 128, 1024], F32, kind="ExternalOutput")
        dbg_cos = nc.dram_tensor("dbg_cos", [4 * 128, 1024], F32, kind="ExternalOutput")
        dbg_sin = nc.dram_tensor("dbg_sin", [4 * 128, 1024], F32, kind="ExternalOutput")

    kr_in_h = [nc.dram_tensor(f"kr_in_h{h}", [D, TOK // 2], BF16)
               for h in range(2)]
    kr_out_h = [nc.dram_tensor(f"kr_out_h{h}", [NCORE * D, TOK // 2], BF16)
                for h in range(2)]
    v_in = nc.dram_tensor("v_in", [TOK, D], BF16)
    v_out = nc.dram_tensor("v_out", [NCORE * TOK, D], BF16)
    nrm_in = nc.dram_tensor("nrm_in", [1, 16], F32)
    nrm_out = nc.dram_tensor("nrm_out", [1, 16], F32)
    m1_in = nc.dram_tensor("m1_in", [1, 16], F32)
    m1_out = nc.dram_tensor("m1_out", [1, 16], F32)
    m2_in = nc.dram_tensor("m2_in", [1, 16], F32)
    m2_out = nc.dram_tensor("m2_out", [1, 16], F32)
    ig_scr = nc.dram_tensor("ig_scr", [HC, 128, IFF], BF16)
    iu_scr = nc.dram_tensor("iu_scr", [HC, 128, IFF], BF16)
    id_scr = nc.dram_tensor("id_scr", [CI, 128, D], BF16)

    prog = {e: [] for e in ENGS}
    DMAS = ("dC", "dX0", "dX1", "dKV", "dV0", "dV1", "dB0", "dB1", "dP0", "dP1",
            "dI0", "dI1", "dIG0", "dIG1", "dID0", "dID1", "dRW", "dWT",
            "dCC", "dOUT", "dDBG")
    n = {k: 0 for k in DMAS}
    n.update({"cc": 0, "pe": 0, "ve": 0, "sc": 0})

    st = ExitStack()
    E = st.enter_context
    sem = {}
    for s_ in list(n.keys()):
        sem[s_] = E(nc.semaphore(s_))

    def em(eng, fn):
        prog[eng].append(fn)

    def dma(eng, out, in_, ctr):
        n[ctr] += 16
        em(eng, lambda e, o=out, i=in_: e.dma_start(out=o, in_=i)
           .then_inc(sem[ctr], 16))
        return n[ctr]

    def wait(eng, ctr, val):
        em(eng, lambda e, c=ctr, v=val: e.wait_ge(sem[c], v))

    def inc(eng, ctr, fn):
        n[ctr] += 1
        em(eng, lambda e, f=fn, c=ctr: f(e).then_inc(sem[c], 1))
        return n[ctr]

    def join():
        marks = {}
        for e_, c_ in (("tensor", "pe"), ("vector", "ve"), ("scalar", "sc")):
            marks[c_] = inc(e_, c_, lambda e: e.drain())
        snap = dict(n)
        for e_ in ENGS:
            for c_, v_ in snap.items():
                wait(e_, c_, v_)

    _PHASE = os.environ.get("K_PHASE", "all")

    class _Trunc(Exception):
        pass

    def gate(tag):
        if _PHASE == tag:
            join()
            raise _Trunc()

    ident_bs = E(nc.sbuf_tensor("ident_bs", [128, 128], BF16))
    ident_fs = E(nc.sbuf_tensor("ident_fs", [128, 128], F32))
    ones_row = E(nc.sbuf_tensor("ones_row", [1, 128], F32))
    ones_col = E(nc.sbuf_tensor("ones_col", [128, 1], BF16))
    negB = E(nc.sbuf_tensor("negB", [128, 1], F32))
    phih_sb = E(nc.sbuf_tensor("phih_sb", [128, 4], F32))
    phil_sb = E(nc.sbuf_tensor("phil_sb", [128, 4], F32))
    clips_sb = E(nc.sbuf_tensor("clips_sb", [1, 2], F32))
    sml = E(nc.sbuf_tensor("sml", [1, 64], F32))
    bc1 = E(nc.sbuf_tensor("bc1", [128, 8], F32))
    stage16 = E(nc.sbuf_tensor("stage16", [1, 16], F32))
    mg_t = E(nc.sbuf_tensor("mg_t", [128, 128], F32))
    mu_t = E(nc.sbuf_tensor("mu_t", [128, 128], F32))
    md_t = E(nc.sbuf_tensor("md_t", [128, 128], F32))
    sg_f = E(nc.sbuf_tensor("sg_f", [128, 22], F32))
    su_f = E(nc.sbuf_tensor("su_f", [128, 22], F32))
    s1sg = E(nc.sbuf_tensor("s1sg", [128, 22], F32))
    s1su = E(nc.sbuf_tensor("s1su", [128, 22], F32))
    lnm = E(nc.sbuf_tensor("lnm", [128, 16], F32))
    lnr = E(nc.sbuf_tensor("lnr", [128, 16], F32))
    hmax = E(nc.sbuf_tensor("hmax", [128, 1], F32))
    pad128 = E(nc.sbuf_tensor("pad128", [128, 128], F32))
    epsb = E(nc.sbuf_tensor("epsb", [128, 1], F32))
    sdrow = E(nc.sbuf_tensor("sdrow", [1, D], F32))
    gath = E(nc.sbuf_tensor("gath", [128, 128], F32))
    snap_sb = E(nc.sbuf_tensor("snap_sb", [1, 64], F32))
    QR = E(nc.sbuf_tensor("QR", [128, HC, TOK], BF16))
    spc = E(nc.sbuf_tensor("spc", [128, 1], F32))


    def vpad():
        em("vector", lambda e: e.tensor_copy(out=spc[:, :], in_=ones_col[:, :]))

    def pmax_to_sml(src_col_ap, dst, tpf):
        vpad()
        cve = inc("vector", "ve", lambda e: e.tensor_copy(
            out=pad128[:, 0:1], in_=src_col_ap))
        wait("tensor", "ve", cve)
        wait("tensor", "sc", n["sc"])
        cpe = inc("tensor", "pe", lambda e: e.transpose(
            tpf[:, :], pad128[:, :], ident_fs[:, :]))
        wait("vector", "pe", cpe)
        r_ = inc("vector", "ve", lambda e, dst=dst: e.reduce_max(
            out=sml[0:1, dst:dst + 1], in_=tpf[0:1, 0:128], axis=AX))
        vpad()
        return r_

    def bcast(src_ap, dstcol, bcp):
        wait("tensor", "sc", n["sc"])
        wait("tensor", "ve", n["ve"])
        cpe = inc("tensor", "pe", lambda e: e.matmul(
            bcp[:, 0:1], ones_row[0:1, :], src_ap, start=True, stop=True,
            skip_group_check=True))
        wait("scalar", "pe", cpe)
        return inc("scalar", "sc", lambda e, dstcol=dstcol: e.copy(
            out=bc1[:, dstcol:dstcol + 1], in_=bcp[:, 0:1]))

    try:
        dma("sync", ident_bs[:, :], ident_b[:, :], "dC")
        dma("sync", ident_fs[:, :], ident_f[:, :], "dC")
        dma("sync", phih_sb[:, :], phi_hi[:, :], "dC")
        dma("sync", phil_sb[:, :], phi_lo[:, :], "dC")
        d_consts = dma("sync", clips_sb[:, :], clips[:, :], "dC")
        wait("tensor", "dC", d_consts)
        wait("scalar", "dC", d_consts)

        wait("vector", "dC", d_consts)
        em("vector", lambda e: e.memset(ones_row[:, :], 1.0))
        em("vector", lambda e: e.memset(ones_col[:, :], 1.0))
        em("vector", lambda e: e.memset(pad128[:, :], 0.0))
        em("vector", lambda e: e.memset(sml[:, :], 0.0))
        em("vector", lambda e: e.memset(mg_t[:, :], 0.0))
        em("vector", lambda e: e.memset(mu_t[:, :], 0.0))
        em("vector", lambda e: e.memset(md_t[:, :], 0.0))
        em("vector", lambda e: e.memset(tmpc_init[0][:, :], 0.0)
           if False else e.memset(bc1[:, :], 0.0))
        em("vector", lambda e: e.memset(epsb[:, :], EPS))
        c_consts = inc("vector", "ve", lambda e: e.memset(stage16[:, :], 0.0))

        p1 = ExitStack()
        E1 = p1.enter_context
        pos_sb = E1(nc.sbuf_tensor("pos_sb", [1, TOK], F32))
        XN = E1(nc.sbuf_tensor("XN", [128, HC, TOK], BF16))
        costab = E1(nc.sbuf_tensor("costab", [128, 4, 1024], F32))
        sintab = E1(nc.sbuf_tensor("sintab", [128, 4, 1024], F32))
        w_bf = E1(nc.sbuf_tensor("w_bf", [128, 2, HC, 1024], BF16))
        xt = E1(nc.sbuf_tensor("xt", [128, 2, 1024], F32))
        xnt = E1(nc.sbuf_tensor("xnt", [128, 1024], BF16))
        sqv = E1(nc.sbuf_tensor("sqv", [128, 2, 1024], BF16))
        t1 = E1(nc.sbuf_tensor("t1", [128, 1024], F32))
        t2 = E1(nc.sbuf_tensor("t2", [128, 1024], F32))
        krs = E1(nc.sbuf_tensor("krs", [128, 2, 1024], BF16))
        vst = E1(nc.sbuf_tensor("vst", [128, 2, 1024], BF16))
        posb = E1(nc.sbuf_tensor("posb", [128, 1024], F32))
        angb = E1(nc.sbuf_tensor("angb", [128, 1024], F32))
        kkb = E1(nc.sbuf_tensor("kkb", [128, 1024], F32))
        redb = E1(nc.sbuf_tensor("redb", [128, 1024], F32))
        stt1 = E1(nc.sbuf_tensor("stt1", [128, 16], F32))
        stt2 = E1(nc.sbuf_tensor("stt2", [128, 16], F32))
        angi = E1(nc.sbuf_tensor("angi", [128, 1024], mybir.dt.int32))
        tmpc = E1(nc.sbuf_tensor("tmpc", [128, 8], F32))

        dma("sync", pos_sb[:, :], pos_own[:, :], "dC")
        for hc in range(HC):
            dma("gpsimd", w_bf[:, 0, hc, :], wv_t[hc * 128:(hc + 1) * 128, :], "dWT")
        for hc in range(HC):
            dma("gpsimd", w_bf[:, 1, hc, :], wk_t[hc * 128:(hc + 1) * 128, :], "dWT")
        d_wvk = n["dWT"]

        ps1a = ExitStack()
        trp = ps1a.enter_context(nc.psum_tensor("trp", [128, 512], BF16))
        trp2 = ps1a.enter_context(nc.psum_tensor("trp2", [128, 512], BF16))
        g_x = [0, 0]
        g_xs = [0, 0]
        g_xn = 0
        g_tr = [0, 0]
        for t in range(R):
            sl = t % 2
            xc = "dX0" if sl == 0 else "dX1"
            wait("sync", "ve", g_x[sl])
            wait("sync", "sc", g_xs[sl])
            d_x = dma("sync", xt[:, sl, :], x_own[t, :, :], xc)
            wait("scalar", xc, d_x)
            if debug and _PHASE == "qkv" and t < 2:
                wait("gpsimd", xc, d_x)
                dX = dma("gpsimd", dbg_xt[t, :, :], xt[:, sl, :], "dDBG")
                wait("sync", "dDBG", dX)
            c_sq = inc("scalar", "sc", lambda e, sl=sl: e.activation(
                out=t1[:, :], in_=xt[:, sl, :], func=AF.Square))
            wait("vector", xc, d_x)
            wait("vector", "sc", c_sq)
            em("vector", lambda e, sl=sl, t=t: e.reduce_sum(
                stt1[:, t:t + 1], xt[:, sl, :], axis=AX))
            em("vector", lambda e, t=t: e.reduce_sum(
                stt2[:, t:t + 1], t1[:, :], axis=AX))
            em("vector", lambda e, t=t: e.tensor_scalar(
                out=stt1[:, t:t + 1], in0=stt1[:, t:t + 1], scalar1=1.0 / D,
                scalar2=None, op0=ALU.mult))
            em("vector", lambda e, t=t: e.tensor_scalar(
                out=stt2[:, t:t + 1], in0=stt2[:, t:t + 1], scalar1=1.0 / D,
                scalar2=None, op0=ALU.mult))
            em("vector", lambda e, t=t: e.scalar_tensor_tensor(
                out=tmpc[:, 0:1], in0=stt1[:, t:t + 1], scalar=-1.0,
                in1=stt1[:, t:t + 1], op0=ALU.mult, op1=ALU.mult))
            vpad()
            c_var = inc("vector", "ve", lambda e, t=t: e.tensor_tensor(
                out=stt2[:, t:t + 1], in0=stt2[:, t:t + 1], in1=tmpc[:, 0:1],
                op=ALU.add))
            wait("scalar", "ve", c_var)
            c_rt = inc("scalar", "sc", lambda e, t=t: e.activation(
                out=tmpc[:, 1:2], in_=stt2[:, t:t + 1], func=AF.Sqrt,
                bias=epsb[:, 0:1]))
            wait("vector", "sc", c_rt)
            em("vector", lambda e: e.reciprocal(tmpc[:, 2:3], tmpc[:, 1:2]))
            em("vector", lambda e, t=t: e.tensor_scalar(
                out=tmpc[:, 3:4], in0=stt2[:, t:t + 1], scalar1=EPS, scalar2=-0.5,
                op0=ALU.add, op1=ALU.mult))
            em("vector", lambda e: e.tensor_tensor(
                out=tmpc[:, 4:5], in0=tmpc[:, 2:3], in1=tmpc[:, 2:3], op=ALU.mult))
            vpad()
            em("vector", lambda e: e.tensor_tensor(
                out=tmpc[:, 4:5], in0=tmpc[:, 4:5], in1=tmpc[:, 3:4], op=ALU.mult))
            vpad()
            em("vector", lambda e: e.tensor_scalar(
                out=tmpc[:, 4:5], in0=tmpc[:, 4:5], scalar1=1.5, scalar2=None,
                op0=ALU.add))
            vpad()
            em("vector", lambda e: e.tensor_tensor(
                out=tmpc[:, 2:3], in0=tmpc[:, 2:3], in1=tmpc[:, 4:5], op=ALU.mult))
            vpad()
            c_bias = inc("vector", "ve", lambda e, t=t: e.scalar_tensor_tensor(
                out=tmpc[:, 5:6], in0=stt1[:, t:t + 1], scalar=-1.0,
                in1=tmpc[:, 2:3], op0=ALU.mult, op1=ALU.mult))
            g_x[sl] = c_bias
            wait("scalar", "ve", c_bias)
            wait("scalar", "pe", g_xn)
            c_xn = inc("scalar", "sc", lambda e, sl=sl: e.activation(
                out=xnt[:, :], in_=xt[:, sl, :], func=AF.Identity,
                bias=tmpc[:, 5:6], scale=tmpc[:, 2:3]))
            g_xs[sl] = c_xn
            wait("tensor", "sc", c_xn)
            if debug and _PHASE == "qkv" and t < 2:
                wait("gpsimd", "sc", c_xn)
                dX = dma("gpsimd", dbg_xnt[t, :, :], xnt[:, :], "dDBG")
                wait("scalar", "dDBG", dX)
            for g in range(2):
                tp = trp if g == 0 else trp2
                wait("tensor", "sc", g_tr[g])
                for k4 in range(4):
                    hc = g * 4 + k4
                    fn = (lambda e, tp=tp, k4=k4, hc=hc: e.transpose(
                        tp[:, k4 * 128:(k4 + 1) * 128],
                        xnt[:, hc * 128:(hc + 1) * 128], ident_bs[:, :]))
                    if k4 == 3:
                        g_xn = inc("tensor", "pe", fn)
                    else:
                        em("tensor", fn)
                wait("scalar", "pe", g_xn)
                g_tr[g] = inc("scalar", "sc", lambda e, tp=tp, g=g, t=t: e.copy(
                    out=XN[:, g * 4:(g + 1) * 4, t * 128:(t + 1) * 128],
                    in_=tp[:, :].rearrange("p (a b) -> p a b", a=4)))
            if os.environ.get("KJOIN", "0") == "1":
                join()
        ps1a.close()
        join()

        ps1c = ExitStack()
        vps = [ps1c.enter_context(nc.psum_tensor(f"vps{i}", [128, 1024], F32))
               for i in range(2)]
        g_vev = [0, 0]
        g_vdm = [0, 0]
        wait("tensor", "dWT", d_wvk)
        n_v_dma = 0
        for t in range(R):
            sl = t % 2
            wait("tensor", "sc", g_vev[sl])
            for hc in range(HC):
                for k in range(2):
                    fn = (lambda e, sl=sl, hc=hc, k=k, t=t: e.matmul(
                        vps[sl][:, k * 512:(k + 1) * 512],
                        XN[:, hc, t * 128:(t + 1) * 128],
                        w_bf[:, 0, hc, k * 512:(k + 1) * 512],
                        start=(hc == 0), stop=(hc == HC - 1),
                        skip_group_check=True))
                    if hc == HC - 1 and k == 1:
                        c_vmm = inc("tensor", "pe", fn)
                    else:
                        em("tensor", fn)
            wait("scalar", "pe", c_vmm)
            wait("scalar", "dV0" if sl == 0 else "dV1", g_vdm[sl])
            g_vev[sl] = inc("scalar", "sc", lambda e, sl=sl: e.copy(
                out=vst[:, sl, :], in_=vps[sl][:, :]))
            wait("sync", "sc", g_vev[sl])
            g_vdm[sl] = n_v_dma = dma(
                "sync", v_in[t * 128:(t + 1) * 128, :], vst[:, sl, :],
                "dV0" if sl == 0 else "dV1")
        ps1c.close()
        wait("gpsimd", "dV0", n["dV0"])
        wait("gpsimd", "dV1", n["dV1"])
        n["cc"] += 1
        em("gpsimd", lambda e: e.collective_compute(
            "AllGather", ALU.bypass, replica_groups=[list(range(NCORE))],
            ins=[v_in.ap()], outs=[v_out.ap()]).then_inc(sem["cc"], 1))


        wait("gpsimd", "pe", n["pe"])
        for hc in range(HC):
            d_wq2 = dma("gpsimd", w_bf[:, 0, hc, :],
                        wq_t[hc * 128:(hc + 1) * 128, :], "dWT")

        ps1n = ExitStack()
        wait("tensor", "dWT", d_wq2)
        nps = [ps1n.enter_context(nc.psum_tensor(f"nps{i}", [128, 1024], F32))
               for i in range(2)]
        tpfn = ps1n.enter_context(nc.psum_tensor("tpfn", [128, 128], F32))
        g_nev = [0, 0]
        for proj in range(2):
            for t in range(R):
                sl = t % 2
                wait("tensor", "sc", g_nev[sl])
                for hc in range(HC):
                    for k in range(2):
                        fn = (lambda e, sl=sl, hc=hc, k=k, t=t, proj=proj: e.matmul(
                            nps[sl][:, k * 512:(k + 1) * 512],
                            XN[:, hc, t * 128:(t + 1) * 128],
                            w_bf[:, proj, hc, k * 512:(k + 1) * 512],
                            start=(hc == 0), stop=(hc == HC - 1),
                            skip_group_check=True))
                        if hc == HC - 1 and k == 1:
                            c_nmm = inc("tensor", "pe", fn)
                        else:
                            em("tensor", fn)
                wait("scalar", "pe", c_nmm)
                wait("scalar", "ve", n["ve"])
                g_nev[sl] = c_nsq = inc("scalar", "sc", lambda e, sl=sl: e.activation(
                    out=t1[:, :], in_=nps[sl][:, :], func=AF.Square))
                wait("vector", "sc", c_nsq)
                em("vector", lambda e: e.reduce_sum(
                    out=tmpc[:, 6:7], in_=t1[:, :], axis=AX))
                vpad()
                if t == 0:
                    cnv = inc("vector", "ve", lambda e: e.tensor_copy(
                        out=tmpc[:, 7:8], in_=tmpc[:, 6:7]))
                else:
                    cnv = inc("vector", "ve", lambda e: e.tensor_tensor(
                        out=tmpc[:, 7:8], in0=tmpc[:, 7:8], in1=tmpc[:, 6:7],
                        op=ALU.max))
            pmax_to_sml(tmpc[:, 7:8], proj, tpfn)
        ps1n.close()

        wait("tensor", "sc", n["sc"])
        wait("tensor", "ve", n["ve"])
        c_mkst = inc("vector", "ve", lambda e: e.tensor_copy(
            out=stage16[0:1, 0:1], in_=sml[0:1, 1:2]))
        wait("gpsimd", "ve", c_mkst)
        d_nrm = dma("gpsimd", nrm_in[:, :], stage16[:, :], "dCC")
        wait("gpsimd", "dCC", d_nrm)
        n["cc"] += 1
        em("gpsimd", lambda e: e.collective_compute(
            "AllReduce", ALU.max, replica_groups=[list(range(NCORE))],
            ins=[nrm_in.ap()], outs=[nrm_out.ap()]).then_inc(sem["cc"], 1))
        wait("gpsimd", "cc", n["cc"])
        d_nrmo = dma("gpsimd", stage16[:, :], nrm_out[:, :], "dCC")
        if debug:
            dma("gpsimd", dbg_nrm[0:1, :], nrm_in[:, :], "dDBG")
            dma("gpsimd", dbg_nrm[1:2, :], nrm_out[:, :], "dDBG")
            wait("gpsimd", "dDBG", n["dDBG"])
        wait("scalar", "dCC", d_nrmo)
        wait("scalar", "ve", n["ve"])
        em("scalar", lambda e: e.activation(
            out=sml[0:1, 2:3], in_=sml[0:1, 0:1], func=AF.Sqrt))
        c_sq2 = inc("scalar", "sc", lambda e: e.activation(
            out=sml[0:1, 3:4], in_=stage16[0:1, 0:1], func=AF.Sqrt))
        wait("vector", "sc", c_sq2)
        c_nb = inc("vector", "ve", lambda e: e.scalar_tensor_tensor(
            out=sml[0:1, 4:5], in0=sml[0:1, 2:3], scalar=-1.02 / 32.0,
            in1=sml[0:1, 3:4], op0=ALU.mult, op1=ALU.mult))
        if debug:
            em("vector", lambda e: e.memset(snap_sb[:, :], 0.0))
            em("vector", lambda e: e.tensor_copy(out=snap_sb[0:1, 0:16],
                                                 in_=sml[0:1, 0:16]))
            c_snap = inc("vector", "ve", lambda e: e.tensor_copy(
                out=snap_sb[0:1, 16:32], in_=stage16[0:1, 0:16]))
            wait("gpsimd", "ve", c_snap)
            dma("gpsimd", dbg_row[:, :], snap_sb[0:1, :], "dDBG")
        wait("tensor", "ve", c_nb)
        with nc.psum_tensor("bcp0", [128, 16], F32) as bcp0:
            c_nbm = inc("tensor", "pe", lambda e: e.matmul(
                bcp0[:, 0:1], ones_row[0:1, :], sml[0:1, 4:5], start=True,
                stop=True, skip_group_check=True))
            wait("scalar", "pe", c_nbm)
            c_negB = inc("scalar", "sc", lambda e: e.copy(
                out=negB[:, :], in_=bcp0[:, 0:1]))
        wait("tensor", "sc", c_negB)
        if debug and _PHASE == "negb":
            wait("gpsimd", "sc", c_negB)
            dma("gpsimd", dbg_sml[:, :], sml[:, :], "dDBG")
            dma("gpsimd", dbg_negb[:, :], negB[:, :], "dDBG")
            wait("gpsimd", "dDBG", n["dDBG"])
        gate("negb")


        ps1b = ExitStack()
        qe_ps = ps1b.enter_context(nc.psum_tensor("qe_ps", [128, 1024], F32))
        qo_ps = ps1b.enter_context(nc.psum_tensor("qo_ps", [128, 1024], F32))

        n_kr_dma = 0

        for half in range(2):
            for k in range(2):
                c_bcmm = inc("tensor", "pe", lambda e, k=k, half=half: e.matmul(
                    qe_ps[:, k * 512:(k + 1) * 512], ones_row[0:1, :],
                    pos_sb[0:1, half * 1024 + k * 512:half * 1024 + (k + 1) * 512],
                    start=True, stop=True, skip_group_check=True))
            wait("vector", "pe", c_bcmm)
            c_posb = inc("vector", "ve", lambda e: e.tensor_copy(
                out=posb[:, :], in_=qe_ps[:, :]))
            for pc in range(4):
                em("vector", lambda e, pc=pc: e.tensor_scalar(
                    out=redb[:, :], in0=posb[:, :], scalar1=phil_sb[:, pc:pc + 1],
                    scalar2=None, op0=ALU.mult))
                em("vector", lambda e, pc=pc: e.tensor_scalar(
                    out=angb[:, :], in0=posb[:, :], scalar1=phih_sb[:, pc:pc + 1],
                    scalar2=None, op0=ALU.mult))
                em("vector", lambda e: e.tensor_tensor(
                    out=angb[:, :], in0=angb[:, :], in1=redb[:, :], op=ALU.add))
                em("vector", lambda e: e.tensor_scalar(
                    out=kkb[:, :], in0=angb[:, :], scalar1=INV_2PI, scalar2=None,
                    op0=ALU.mult))
                em("vector", lambda e: e.tensor_copy(out=angi[:, :], in_=kkb[:, :]))
                em("vector", lambda e: e.tensor_copy(out=kkb[:, :], in_=angi[:, :]))
                em("vector", lambda e: e.tensor_scalar(
                    out=kkb[:, :], in0=kkb[:, :], scalar1=-1.0, scalar2=None,
                    op0=ALU.mult))
                em("vector", lambda e: e.scalar_tensor_tensor(
                    out=redb[:, :], in0=kkb[:, :], scalar=C1, in1=angb[:, :],
                    op0=ALU.mult, op1=ALU.add))
                em("vector", lambda e: e.scalar_tensor_tensor(
                    out=redb[:, :], in0=kkb[:, :], scalar=C2, in1=redb[:, :],
                    op0=ALU.mult, op1=ALU.add))
                em("vector", lambda e: e.scalar_tensor_tensor(
                    out=redb[:, :], in0=kkb[:, :], scalar=C3, in1=redb[:, :],
                    op0=ALU.mult, op1=ALU.add))
                em("vector", lambda e: e.tensor_scalar(
                    out=t1[:, :], in0=redb[:, :], scalar1=PI, scalar2=None,
                    op0=ALU.is_gt))
                em("vector", lambda e: e.scalar_tensor_tensor(
                    out=redb[:, :], in0=t1[:, :], scalar=-TWO_PI, in1=redb[:, :],
                    op0=ALU.mult, op1=ALU.add))
                em("vector", lambda e: e.tensor_scalar(
                    out=t1[:, :], in0=redb[:, :], scalar1=-PI, scalar2=None,
                    op0=ALU.is_lt))
                em("vector", lambda e: e.scalar_tensor_tensor(
                    out=t1[:, :], in0=t1[:, :], scalar=TWO_PI, in1=redb[:, :],
                    op0=ALU.mult, op1=ALU.add))
                em("vector", lambda e: e.tensor_scalar(
                    out=redb[:, :], in0=t1[:, :], scalar1=HALF_PI, scalar2=None,
                    op0=ALU.add))
                em("vector", lambda e: e.tensor_scalar(
                    out=t2[:, :], in0=redb[:, :], scalar1=PI, scalar2=None,
                    op0=ALU.is_gt))
                c_wr = inc("vector", "ve", lambda e: e.scalar_tensor_tensor(
                    out=t2[:, :], in0=t2[:, :], scalar=-TWO_PI, in1=redb[:, :],
                    op0=ALU.mult, op1=ALU.add))
                wait("scalar", "ve", c_wr)
                em("scalar", lambda e, pc=pc: e.activation(
                    out=sintab[:, pc, :], in_=t1[:, :], func=AF.Sin))
                c_tab = inc("scalar", "sc", lambda e, pc=pc: e.activation(
                    out=costab[:, pc, :], in_=t2[:, :], func=AF.Sin))
                wait("vector", "sc", c_tab)

            wait("tensor", "ve", c_posb)
            for proj in range(2):
                is_q = (proj == 0)
                for pc in range(4):
                    for (ps, dcx) in ((qe_ps, pc), (qo_ps, pc + 4)):
                        for hc in range(HC):
                            for k in range(2):
                                fn = (lambda e, ps=ps, dcx=dcx, hc=hc, k=k,
                                      proj=proj, half=half: e.matmul(
                                          ps[:, k * 512:(k + 1) * 512],
                                          w_bf[:, proj, hc,
                                               dcx * 128:(dcx + 1) * 128],
                                          XN[:, hc, half * 1024 + k * 512:
                                             half * 1024 + (k + 1) * 512],
                                          start=(hc == 0), stop=(hc == HC - 1),
                                          skip_group_check=True))
                                if (ps is qo_ps) and hc == HC - 1 and k == 1:
                                    c_qk = inc("tensor", "pe", fn)
                                else:
                                    em("tensor", fn)
                    wait("vector", "pe", c_qk)
                    if not is_q:
                        wait("vector", "dKV", n_kr_dma)
                        wait("vector", "sc", n["sc"])
                    em("vector", lambda e, pc=pc: e.tensor_tensor(
                        out=t1[:, :], in0=qe_ps[:, :], in1=costab[:, pc, :],
                        op=ALU.mult))
                    em("vector", lambda e, pc=pc: e.tensor_tensor(
                        out=t2[:, :], in0=qo_ps[:, :], in1=sintab[:, pc, :],
                        op=ALU.mult))
                    if is_q:
                        em("vector", lambda e, pc=pc, half=half: e.tensor_tensor(
                            out=QR[:, pc, half * 1024:(half + 1) * 1024],
                            in0=t1[:, :], in1=t2[:, :], op=ALU.add))
                    else:
                        em("vector", lambda e: e.tensor_tensor(
                            out=krs[:, 0, :], in0=t1[:, :], in1=t2[:, :], op=ALU.add))
                    em("vector", lambda e, pc=pc: e.tensor_tensor(
                        out=t1[:, :], in0=qo_ps[:, :], in1=costab[:, pc, :],
                        op=ALU.mult))
                    em("vector", lambda e, pc=pc: e.tensor_tensor(
                        out=t2[:, :], in0=qe_ps[:, :], in1=sintab[:, pc, :],
                        op=ALU.mult))
                    if is_q:
                        c_rp = inc("vector", "ve", lambda e, pc=pc, half=half:
                                   e.tensor_tensor(
                                       out=QR[:, pc + 4,
                                              half * 1024:(half + 1) * 1024],
                                       in0=t1[:, :], in1=t2[:, :], op=ALU.subtract))
                    else:
                        c_rp = inc("vector", "ve", lambda e: e.tensor_tensor(
                            out=krs[:, 1, :], in0=t1[:, :], in1=t2[:, :],
                            op=ALU.subtract))
                    wait("tensor", "ve", c_rp)
                    if not is_q:
                        for pi, dcx in ((0, pc), (1, pc + 4)):
                            wait("sync", "ve", c_rp)
                            n_kr_dma = dma(
                                "sync",
                                kr_in_h[half][dcx * 128:(dcx + 1) * 128, :],
                                krs[:, pi, :], "dKV")
            wait("gpsimd", "dKV", n["dKV"])
            n["cc"] += 1
            em("gpsimd", lambda e, half=half: e.collective_compute(
                "AllGather", ALU.bypass, replica_groups=[list(range(NCORE))],
                ins=[kr_in_h[half].ap()],
                outs=[kr_out_h[half].ap()]).then_inc(sem["cc"], 1))

        ps1b.close()
        wait("tensor", "ve", n["ve"])

        if debug:
            wait("gpsimd", "ve", n["ve"])
            wait("gpsimd", "sc", n["sc"])
            for hc in range(HC):
                dma("gpsimd", dbg_xn[hc * 128:(hc + 1) * 128, :], XN[:, hc, :], "dDBG")
            for pc in range(4):
                dma("gpsimd", dbg_cos[pc * 128:(pc + 1) * 128, :], costab[:, pc, :], "dDBG")
                dma("gpsimd", dbg_sin[pc * 128:(pc + 1) * 128, :], sintab[:, pc, :], "dDBG")
            wait("gpsimd", "dDBG", n["dDBG"])
        if debug and _PHASE == "qkv":
            with nc.sbuf_tensor("dbq0", [128, TOK], F32) as dbq0:
                for dc in range(HC):
                    c_d = inc("vector", "ve", lambda e, dc=dc: e.tensor_copy(
                        out=dbq0[:, :], in_=QR[:, dc, :]))
                    wait("gpsimd", "ve", c_d)
                    dD = dma("gpsimd", dbg_qr[dc * 128:(dc + 1) * 128, :],
                             dbq0[:, :], "dDBG")
                    wait("vector", "dDBG", dD)
        gate("qkv")


        p1.close()
        join()

        scope_o = ExitStack()
        Ea = scope_o.enter_context
        O_sb = Ea(nc.sbuf_tensor("O_sb", [128, R, 1024], F32))
        den_sb = Ea(nc.sbuf_tensor("den_sb", [128, 16], F32))
        rden = Ea(nc.sbuf_tensor("rden", [128, 16], F32))

        p5gu = ExitStack()
        E5g = p5gu.enter_context
        wsl = E5g(nc.sbuf_tensor("wsl", [128, 2, D], F32))
        vvb = E5g(nc.sbuf_tensor("vvb", [128, D], F32))
        ffb = E5g(nc.sbuf_tensor("ffb", [128, D], F32))
        intsl = E5g(nc.sbuf_tensor("intsl", [128, 2, D], BF16))
        stg5 = E5g(nc.sbuf_tensor("stg5", [128, 2, HC, 128], BF16))
        wqi = E5g(nc.sbuf_tensor("wqi", [128, D], mybir.dt.int32))
        tmpq = E5g(nc.sbuf_tensor("tmpq", [128, 4], F32))
        tps_gu = E5g(nc.psum_tensor("tps_gu", [128, 512], BF16))

        g5_w = [0, 0]
        g5_int = [0, 0]
        g5_stW = [0, 0]
        g5_tp = [0]
        u_cq = {}

        def p5_quant(mat, dst, c, uid):
            slw = uid % 2
            pc_ = "dP0" if slw == 0 else "dP1"
            wait("sync", "ve", g5_w[slw])
            d_w = dma("sync", wsl[:, slw, :],
                      mat[c * 128:(c + 1) * 128, :], pc_)
            wait("vector", pc_, d_w)
            em("vector", lambda e, dst=dst, c=c, slw=slw: e.reduce_max(
                out=dst[:, c:c + 1], in_=wsl[:, slw, :],
                axis=AX, apply_absolute_value=True))
            vpad()
            em("vector", lambda e, dst=dst, c=c, slw=slw: e.tensor_scalar(
                out=tmpq[:, 2 * slw:2 * slw + 1], in0=dst[:, c:c + 1],
                scalar1=1.0 / 127.0, scalar2=None, op0=ALU.mult))
            vpad()
            em("vector", lambda e, slw=slw: e.reciprocal(
                tmpq[:, 2 * slw + 1:2 * slw + 2],
                tmpq[:, 2 * slw:2 * slw + 1]))
            vpad()
            g5_w[slw] = inc("vector", "ve", lambda e, slw=slw: e.tensor_scalar(
                out=vvb[:, :], in0=wsl[:, slw, :],
                scalar1=tmpq[:, 2 * slw + 1:2 * slw + 2],
                scalar2=None, op0=ALU.mult))
            em("vector", lambda e: e.tensor_copy(out=wqi[:, :], in_=vvb[:, :]))
            em("vector", lambda e: e.tensor_copy(out=ffb[:, :], in_=wqi[:, :]))
            em("vector", lambda e: e.tensor_tensor(
                out=vvb[:, :], in0=vvb[:, :], in1=ffb[:, :], op=ALU.subtract))
            em("vector", lambda e: e.scalar_tensor_tensor(
                out=ffb[:, :], in0=vvb[:, :], scalar=0.5, in1=ffb[:, :],
                op0=ALU.is_ge, op1=ALU.add))
            em("vector", lambda e: e.tensor_scalar(
                out=vvb[:, :], in0=vvb[:, :], scalar1=-0.5, scalar2=None,
                op0=ALU.is_lt))
            wait("vector", "pe", g5_int[slw])
            u_cq[uid] = inc("vector", "ve", lambda e, slw=slw: e.tensor_tensor(
                out=intsl[:, slw, :], in0=ffb[:, :], in1=vvb[:, :],
                op=ALU.subtract))

        def p5_store(scr, c, uid):
            slw = uid % 2
            ic_ = "dI0" if slw == 0 else "dI1"
            wait("tensor", "ve", u_cq[uid])
            first_cp = True
            for g0 in range(0, HC, 4):
                wait("tensor", "sc", g5_tp[0])
                for k in range(4):
                    fn = (lambda e, k=k, g0=g0, slw=slw: e.transpose(
                        tps_gu[:, k * 128:(k + 1) * 128],
                        intsl[:, slw, (g0 + k) * 128:(g0 + k + 1) * 128],
                        ident_bs[:, :]))
                    if k == 3:
                        c_tp = inc("tensor", "pe", fn)
                    else:
                        em("tensor", fn)
                if g0 + 4 >= HC:
                    g5_int[slw] = c_tp
                wait("scalar", "pe", c_tp)
                if first_cp:
                    wait("scalar", ic_, g5_stW[slw])
                    first_cp = False
                g5_tp[0] = inc("scalar", "sc",
                               lambda e, g0=g0, slw=slw:
                               e.copy(out=stg5[:, slw, g0:g0 + 4, :],
                                      in_=tps_gu[:, :].rearrange(
                                          "p (a b) -> p a b", a=4)))
            wait("sync", "sc", g5_tp[0])
            dstap = bass.AP(scr, c * 128,
                            [[IFF, 128], [128 * IFF, HC], [1, 128]])
            g5_stW[slw] = dma("sync", dstap, stg5[:, slw, 0:HC, :], ic_)

        p5q = []
        for uid in range(44):
            mat, dst, scr = ((wg_ih, mg_t, ig_scr) if uid < 22
                             else (wu_ih, mu_t, iu_scr))
            c = uid % 22
            p5q.append((lambda mat=mat, dst=dst, c=c, uid=uid:
                        p5_quant(mat, dst, c, uid),
                        lambda scr=scr, c=c, uid=uid: p5_store(scr, c, uid)))
        p5_next = [0, 0]

        def p5_drain(n_units):
            for _ in range(n_units):
                if p5_next[0] < len(p5q):
                    p5q[p5_next[0]][0]()
                    p5_next[0] += 1
                lagged = (p5_next[0] - 1 if p5_next[0] < len(p5q)
                          else len(p5q))
                if p5_next[1] < lagged:
                    p5q[p5_next[1]][1]()
                    p5_next[1] += 1

        p3 = ExitStack()
        E3 = p3.enter_context
        kr_sb = E3(nc.sbuf_tensor("kr_sb", [128, 2, HC, 8, 128], BF16))
        v_sb = E3(nc.sbuf_tensor("v_sb", [128, 2, 8, 1024], BF16))
        msk_sb = E3(nc.sbuf_tensor("msk_sb", [128, 2, 8, 128], BF16))
        P_sb = E3(nc.sbuf_tensor("P_sb", [128, 2, 512], BF16))

        s_ps = [E3(nc.psum_tensor(f"s_ps{i}", [128, 512], F32)) for i in range(2)]
        o_ps = [E3(nc.psum_tensor(f"o_ps{i}", [128, 1024], F32)) for i in range(2)]
        den_ps = E3(nc.psum_tensor("den_ps", [128, 16], F32))

        def load_krm(b):
            sl = b % 2
            bc_ = "dB0" if sl == 0 else "dB1"
            half = b // 8
            HT = TOK // 2
            for m in range(8):
                src = bass.AP(kr_out_h[half], 128 * (b % 8) + m * (1024 * HT),
                              [[HT, 128], [128 * HT, 8], [1, 128]])
                dma("sync", kr_sb[:, sl, :, m, :], src, bc_)
            src = bass.AP(dmask, b * (8 * 128 * 128),
                          [[128, 128], [128 * 128, 8], [1, 128]])
            return dma("sync", msk_sb[:, sl, :, :], src, bc_)

        def load_v(b):
            sl = b % 2
            bc_ = "dB0" if sl == 0 else "dB1"
            src = bass.AP(v_out, 128 * b * D,
                          [[D, 128], [TOK * D, 8], [1, D]])
            return dma("sync", v_sb[:, sl, :, :], src, bc_)

        def load_block(b):
            d_krm[b] = load_krm(b)
            d_blk[b] = load_v(b)

        d_krm = {}
        d_blk = {}
        wait("sync", "cc", 1)
        d_blk[0] = load_v(0)
        d_blk[1] = load_v(1)
        wait("sync", "cc", 3)
        d_krm[0] = load_krm(0)
        d_krm[1] = load_krm(1)
        pe_blk_end = {}
        ve_blk_end = {}

        c_evac_prev = 0
        last_exp = {0: 0, 1: 0}
        last_pv = {0: 0, 1: 0}
        for b in range(16):
            sl = b % 2
            ntile = 16 - b
            qlo = 128 * b
            chunks = []
            q = qlo
            if ntile % 2 == 1:
                chunks.append((q, 128))
                q += 128
            while q < TOK:
                chunks.append((q, 256))
                q += 256

            wait("tensor", "dB0" if b % 2 == 0 else "dB1", d_krm[b])
            v_waited = False
            for (qoff, qw) in chunks:
                nqs = qw // 128
                diag = (qoff == qlo)
                for pair in range(4):
                    par = pair % 2
                    sps = s_ps[par]
                    wait("tensor", "sc", last_exp[par])
                    for kk in range(2):
                        ks = 2 * pair + kk
                        for dc in range(HC):
                            fn = (lambda e, sps=sps, kk=kk, ks=ks, dc=dc, sl=sl,
                                  qoff=qoff, qw=qw: e.matmul(
                                      sps[:, kk * qw:(kk + 1) * qw],
                                      kr_sb[:, sl, dc, ks, :],
                                      QR[:, dc, qoff:qoff + qw],
                                      start=(dc == 0), stop=(dc == HC - 1),
                                      skip_group_check=True))
                            if kk == 1 and dc == HC - 1:
                                c_s = inc("tensor", "pe", fn)
                            else:
                                em("tensor", fn)
                    wait("scalar", "pe", c_s)
                    wait("scalar", "pe", last_pv[par])
                    last_exp[par] = inc(
                        "scalar", "sc",
                        lambda e, sps=sps, par=par, qw=qw: e.activation(
                            out=P_sb[:, par, 0:2 * qw], in_=sps[:, 0:2 * qw],
                            func=AF.Exp, bias=negB[:, 0:1], scale=1.0 / 32.0))
                    mask_mark = None
                    if diag:
                        wait("vector", "sc", last_exp[par])
                        for kk in range(2):
                            ks = 2 * pair + kk
                            fn = (lambda e, par=par, kk=kk, ks=ks, sl=sl, qw=qw:
                                  e.tensor_tensor(
                                      out=P_sb[:, par, kk * qw:kk * qw + 128],
                                      in0=P_sb[:, par, kk * qw:kk * qw + 128],
                                      in1=msk_sb[:, sl, ks, :], op=ALU.mult))
                            if kk == 1:
                                mask_mark = inc("vector", "ve", fn)
                            else:
                                em("vector", fn)
                    if not v_waited:
                        wait("tensor", "dB0" if b % 2 == 0 else "dB1", d_blk[b])
                        v_waited = True
                    wait("tensor", "sc", last_exp[par])
                    if mask_mark is not None:
                        wait("tensor", "ve", mask_mark)
                    if pair == 0:
                        wait("tensor", "ve", c_evac_prev)
                    for kk in range(2):
                        ks = 2 * pair + kk
                        for qs in range(nqs):
                            lo = kk * qw + qs * 128
                            st0 = (pair == 0 and kk == 0)
                            for dh in range(2):
                                fn = (lambda e, par=par, ks=ks, qs=qs,
                                      lo=lo, dh=dh, sl=sl, st0=st0: e.matmul(
                                          o_ps[qs][:, dh * 512:(dh + 1) * 512],
                                          P_sb[:, par, lo:lo + 128],
                                          v_sb[:, sl, ks, dh * 512:(dh + 1) * 512],
                                          start=st0, stop=True,
                                          skip_group_check=True))
                                em("tensor", fn)
                            fn = (lambda e, par=par, qs=qs, lo=lo,
                                  st0=st0: e.matmul(
                                      den_ps[:, qs:qs + 1],
                                      P_sb[:, par, lo:lo + 128],
                                      ones_col[:, 0:1],
                                      start=(st0 and qs == 0), stop=True,
                                      skip_group_check=True))
                            if kk == 1 and qs == nqs - 1:
                                last_pv[par] = c_pv_last = inc("tensor", "pe", fn)
                            else:
                                em("tensor", fn)
                wait("vector", "pe", c_pv_last)
                for qs in range(nqs):
                    tidx = qoff // 128 + qs
                    if b == 0:
                        em("vector", lambda e, qs=qs, tidx=tidx: e.tensor_copy(
                            out=O_sb[:, tidx, :], in_=o_ps[qs][:, :]))
                        fn = (lambda e, qs=qs, tidx=tidx: e.tensor_copy(
                            out=den_sb[:, tidx:tidx + 1], in_=den_ps[:, qs:qs + 1]))
                    else:
                        em("vector", lambda e, qs=qs, tidx=tidx: e.tensor_tensor(
                            out=O_sb[:, tidx, :], in0=O_sb[:, tidx, :],
                            in1=o_ps[qs][:, :], op=ALU.add))
                        fn = (lambda e, qs=qs, tidx=tidx: e.tensor_tensor(
                            out=den_sb[:, tidx:tidx + 1],
                            in0=den_sb[:, tidx:tidx + 1],
                            in1=den_ps[:, qs:qs + 1], op=ALU.add))
                    if qs == nqs - 1:
                        c_evac_prev = inc("vector", "ve", fn)
                    else:
                        em("vector", fn)
            p5_drain(2 if b < 4 else 4)
            pe_blk_end[b] = n["pe"]
            ve_blk_end[b] = n["ve"]
            if b + 2 <= 15:
                wait("sync", "pe", pe_blk_end[b])
                wait("sync", "ve", ve_blk_end[b])
                if b + 2 == 8:
                    wait("sync", "cc", 4)
                load_block(b + 2)

        while p5_next[1] < len(p5q):
            p5_drain(4)
        em("vector", lambda e: e.tensor_scalar(
            out=sg_f[:, :], in0=mg_t[:, 0:22], scalar1=1.0 / 127.0,
            scalar2=None, op0=ALU.mult))
        c_fold = inc("vector", "ve", lambda e: e.tensor_scalar(
            out=su_f[:, :], in0=mu_t[:, 0:22], scalar1=1.0 / 127.0,
            scalar2=None, op0=ALU.mult))
        p3.close()
        p5gu.close()
        join()
        if debug and _PHASE == "attn":
            dma("gpsimd", dbg_qr[:, :],
                v_out[0:2048, :].rearrange("(a b) d -> a (b d)", b=2), "dDBG")
            dma("gpsimd", dbg_xn[:, :],
                v_in[:, :].rearrange("(a b) d -> a (b d)", b=2), "dDBG")
            with nc.sbuf_tensor("dbo0", [128, 1024], F32) as dbo0:
                dma("gpsimd", dbg_den[:, :], den_sb[:, :], "dDBG")
                for t in range(R):
                    c_d = inc("vector", "ve", lambda e, t=t: e.tensor_copy(
                        out=dbo0[:, :], in_=O_sb[:, t, :]))
                    wait("gpsimd", "ve", c_d)
                    dD = dma("gpsimd", dbg_osb[:, t * 1024:(t + 1) * 1024],
                             dbo0[:, :], "dDBG")
                    wait("vector", "dDBG", dD)
        gate("attn")

        xmid = E(nc.sbuf_tensor("xmid", [128, R, 1024], F32, side="right"))
        p4 = ExitStack()
        E4 = p4.enter_context
        xrl = E4(nc.sbuf_tensor("xrl", [128, 2, 1024], F32))
        t3 = E4(nc.sbuf_tensor("t3", [128, 2, 1024], F32))

        c_rd = inc("vector", "ve", lambda e: e.reciprocal(rden[:, :], den_sb[:, :]))
        wait("scalar", "ve", c_rd)
        g_fx = [0, 0]
        g_t3 = [0, 0]
        for t in range(R):
            sl = t % 2
            wait("sync", "ve", g_fx[sl])
            xc = "dX0" if sl == 0 else "dX1"
            d_x = dma("sync", xrl[:, sl, :], x_own[t, :, :], xc)
            wait("scalar", "ve", g_t3[sl])
            c_sc = inc("scalar", "sc", lambda e, t=t, sl=sl: e.activation(
                out=t3[:, sl, :], in_=O_sb[:, t, :], func=AF.Identity,
                scale=rden[:, t:t + 1]))
            wait("vector", xc, d_x)
            wait("vector", "sc", c_sc)
            g_t3[sl] = g_fx[sl] = inc(
                "vector", "ve", lambda e, t=t, sl=sl: e.tensor_tensor(
                    out=xmid[:, t, :], in0=t3[:, sl, :], in1=xrl[:, sl, :],
                    op=ALU.add))
        if debug:
            with nc.sbuf_tensor("dbq", [128, TOK], F32) as dbq, \
                 nc.sbuf_tensor("dbo", [128, 1024], F32) as dbo:
                for dc in range(HC):
                    c_d = inc("vector", "ve", lambda e, dc=dc: e.tensor_copy(
                        out=dbq[:, :], in_=QR[:, dc, :]))
                    wait("gpsimd", "ve", c_d)
                    dD = dma("gpsimd", dbg_qr[dc * 128:(dc + 1) * 128, :],
                             dbq[:, :], "dDBG")
                    wait("vector", "dDBG", dD)
                dma("gpsimd", dbg_sml[:, :], sml[:, :], "dDBG")
                dma("gpsimd", dbg_negb[:, :], negB[:, :], "dDBG")
                dD = dma("gpsimd", dbg_den[:, :], den_sb[:, :], "dDBG")
                for t in range(R):
                    c_d = inc("vector", "ve", lambda e, t=t: e.tensor_copy(
                        out=dbo[:, :], in_=O_sb[:, t, :]))
                    wait("gpsimd", "ve", c_d)
                    dD = dma("gpsimd", dbg_osb[:, t * 1024:(t + 1) * 1024],
                             dbo[:, :], "dDBG")
                    wait("vector", "dDBG", dD)
                dD = dma("gpsimd", dbg_xmid[:, :],
                         xmid[:, :, :].rearrange("p a b -> p (a b)"), "dDBG")
                wait("gpsimd", "dDBG", dD)

        p4.close()
        scope_o.close()
        join()
        gate("xmid")

        p5 = ExitStack()
        E5 = p5.enter_context
        wsd = E5(nc.sbuf_tensor("wsd", [128, 2, IFF], F32))
        vvd = E5(nc.sbuf_tensor("vvd", [128, IFF], F32))
        ffd = E5(nc.sbuf_tensor("ffd", [128, IFF], F32))
        intd = E5(nc.sbuf_tensor("intd", [128, 2, IFF], BF16))
        stgd = E5(nc.sbuf_tensor("stgd", [128, 2, CI, 128], BF16))
        srow = E5(nc.sbuf_tensor("srow", [1, IFF], F32))
        wqd = E5(nc.sbuf_tensor("wqd", [128, IFF], mybir.dt.int32))
        tmpd = E5(nc.sbuf_tensor("tmpd", [128, 4], F32))

        ps5 = ExitStack()
        tps = [ps5.enter_context(nc.psum_tensor(f"tps{i}", [128, 512], BF16))
               for i in range(2)]
        tpf = ps5.enter_context(nc.psum_tensor("tpf", [128, 128], F32))

        g_w = [0, 0]
        g_int = [0, 0]
        g_stW = [0, 0]
        g_tp = [0, 0]
        for c in range(8):
            slw = c % 2
            pc_ = "dP0" if slw == 0 else "dP1"
            ic_ = "dI0" if slw == 0 else "dI1"
            wait("sync", "ve", g_w[slw])
            d_w = dma("sync", wsd[:, slw, :],
                      wd_oi[c * 128:(c + 1) * 128, :], pc_)
            wait("vector", pc_, d_w)
            em("vector", lambda e, c=c, slw=slw: e.reduce_max(
                out=md_t[:, c:c + 1], in_=wsd[:, slw, :],
                axis=AX, apply_absolute_value=True))
            vpad()
            em("vector", lambda e, c=c, slw=slw: e.tensor_scalar(
                out=tmpd[:, 2 * slw:2 * slw + 1], in0=md_t[:, c:c + 1],
                scalar1=1.0 / 127.0, scalar2=None, op0=ALU.mult))
            vpad()
            em("vector", lambda e, slw=slw: e.reciprocal(
                tmpd[:, 2 * slw + 1:2 * slw + 2],
                tmpd[:, 2 * slw:2 * slw + 1]))
            vpad()
            g_w[slw] = inc("vector", "ve", lambda e, slw=slw: e.tensor_scalar(
                out=vvd[:, :], in0=wsd[:, slw, :],
                scalar1=tmpd[:, 2 * slw + 1:2 * slw + 2],
                scalar2=None, op0=ALU.mult))
            em("vector", lambda e: e.tensor_copy(out=wqd[:, :], in_=vvd[:, :]))
            em("vector", lambda e: e.tensor_copy(out=ffd[:, :], in_=wqd[:, :]))
            em("vector", lambda e: e.tensor_tensor(
                out=vvd[:, :], in0=vvd[:, :], in1=ffd[:, :], op=ALU.subtract))
            em("vector", lambda e: e.scalar_tensor_tensor(
                out=ffd[:, :], in0=vvd[:, :], scalar=0.5, in1=ffd[:, :],
                op0=ALU.is_ge, op1=ALU.add))
            em("vector", lambda e: e.tensor_scalar(
                out=vvd[:, :], in0=vvd[:, :], scalar1=-0.5, scalar2=None,
                op0=ALU.is_lt))
            wait("vector", "pe", g_int[slw])
            c_q = inc("vector", "ve", lambda e, slw=slw: e.tensor_tensor(
                out=intd[:, slw, :], in0=ffd[:, :], in1=vvd[:, :],
                op=ALU.subtract))
            wait("tensor", "ve", c_q)
            first_cp = True
            for g0 in range(0, CI, 4):
                gw = min(4, CI - g0)
                pb = (g0 // 4) % 2
                wait("tensor", "sc", g_tp[pb])
                for k in range(gw):
                    fn = (lambda e, pb=pb, k=k, g0=g0, slw=slw: e.transpose(
                        tps[pb][:, k * 128:(k + 1) * 128],
                        intd[:, slw, (g0 + k) * 128:(g0 + k + 1) * 128],
                        ident_bs[:, :]))
                    if k == gw - 1:
                        c_tp = inc("tensor", "pe", fn)
                    else:
                        em("tensor", fn)
                if g0 + 4 >= CI:
                    g_int[slw] = c_tp
                wait("scalar", "pe", c_tp)
                if first_cp:
                    wait("scalar", ic_, g_stW[slw])
                    first_cp = False
                g_tp[pb] = inc("scalar", "sc",
                               lambda e, pb=pb, g0=g0, gw=gw, slw=slw:
                               e.copy(out=stgd[:, slw, g0:g0 + gw, :],
                                      in_=tps[pb][:, 0:gw * 128].rearrange(
                                          "p (a b) -> p a b", a=gw)))
            wait("sync", "sc", g_tp[(max(0, CI - 4) // 4) % 2])
            dstap = bass.AP(id_scr, c * 128,
                            [[D, 128], [128 * D, CI], [1, 128]])
            g_stW[slw] = dma("sync", dstap, stgd[:, slw, 0:CI, :], ic_)

        wait("tensor", "sc", n["sc"])
        wait("tensor", "ve", n["ve"])
        cpe = inc("tensor", "pe", lambda e: e.transpose(
            tpf[:, :], md_t[:, :], ident_fs[:, :]))
        wait("scalar", "pe", cpe)
        c_gth = inc("scalar", "sc", lambda e: e.copy(
            out=gath[:, :], in_=tpf[:, :]))
        wait("sync", "sc", c_gth)
        d_row = dma("sync", srow[0:1, 0:1024], gath[0:8, 0:128], "dRW")
        wait("vector", "dRW", d_row)
        c_sdr = inc("vector", "ve", lambda e: e.tensor_scalar(
            out=sdrow[0:1, 0:D], in0=srow[0:1, 0:D], scalar1=1.0 / 127.0,
            scalar2=None, op0=ALU.mult))
        ps5.close()
        p5.close()
        join()
        gate("wprep")

        p6 = ExitStack()
        E6 = p6.enter_context
        hq = E6(nc.sbuf_tensor("hq", [128, CI, TOK], F16))
        s6ab = ExitStack()
        tq1 = s6ab.enter_context(nc.sbuf_tensor("tq1", [128, 1024], F32))
        tq2 = s6ab.enter_context(nc.sbuf_tensor("tq2", [128, 1024], F32))
        xqt = s6ab.enter_context(nc.sbuf_tensor("xqt", [128, 1024], BF16))
        xqi = s6ab.enter_context(nc.sbuf_tensor("xqi", [128, 1024], mybir.dt.int32))

        def hqb(ci, lo, hi):
            return hq[:, ci, lo:hi].bitcast(BF16)

        ps6a = ExitStack()
        tpf6 = ps6a.enter_context(nc.psum_tensor("tpf6", [128, 128], F32))
        for t in range(R):
            c_sq = inc("scalar", "sc", lambda e, t=t: e.activation(
                out=tq1[:, :], in_=xmid[:, t, :], func=AF.Square))
            wait("vector", "sc", c_sq)
            em("vector", lambda e, t=t: e.reduce_sum(
                lnm[:, t:t + 1], xmid[:, t, :], axis=AX))
            em("vector", lambda e, t=t: e.reduce_sum(
                lnr[:, t:t + 1], tq1[:, :], axis=AX))
            em("vector", lambda e, t=t: e.tensor_scalar(
                out=lnm[:, t:t + 1], in0=lnm[:, t:t + 1], scalar1=1.0 / D,
                scalar2=None, op0=ALU.mult))
            em("vector", lambda e, t=t: e.tensor_scalar(
                out=lnr[:, t:t + 1], in0=lnr[:, t:t + 1], scalar1=1.0 / D,
                scalar2=None, op0=ALU.mult))
            em("vector", lambda e, t=t: e.scalar_tensor_tensor(
                out=bc1[:, 4:5], in0=lnm[:, t:t + 1], scalar=-1.0,
                in1=lnm[:, t:t + 1], op0=ALU.mult, op1=ALU.mult))
            vpad()
            c_var = inc("vector", "ve", lambda e, t=t: e.tensor_tensor(
                out=lnr[:, t:t + 1], in0=lnr[:, t:t + 1], in1=bc1[:, 4:5],
                op=ALU.add))
            wait("scalar", "ve", c_var)
            c_rt = inc("scalar", "sc", lambda e, t=t: e.activation(
                out=bc1[:, 5:6], in_=lnr[:, t:t + 1], func=AF.Sqrt,
                bias=epsb[:, 0:1]))
            wait("vector", "sc", c_rt)
            em("vector", lambda e: e.reciprocal(bc1[:, 6:7], bc1[:, 5:6]))
            em("vector", lambda e, t=t: e.tensor_scalar(
                out=bc1[:, 7:8], in0=lnr[:, t:t + 1], scalar1=EPS, scalar2=-0.5,
                op0=ALU.add, op1=ALU.mult))
            em("vector", lambda e: e.tensor_tensor(
                out=bc1[:, 4:5], in0=bc1[:, 6:7], in1=bc1[:, 6:7], op=ALU.mult))
            vpad()
            em("vector", lambda e: e.tensor_tensor(
                out=bc1[:, 4:5], in0=bc1[:, 4:5], in1=bc1[:, 7:8], op=ALU.mult))
            vpad()
            em("vector", lambda e: e.tensor_scalar(
                out=bc1[:, 4:5], in0=bc1[:, 4:5], scalar1=1.5, scalar2=None,
                op0=ALU.add))
            vpad()
            em("vector", lambda e, t=t: e.tensor_tensor(
                out=lnr[:, t:t + 1], in0=bc1[:, 6:7], in1=bc1[:, 4:5],
                op=ALU.mult))
            em("vector", lambda e, t=t: e.tensor_reduce(
                out=bc1[:, 4:5], in_=xmid[:, t, :], axis=AX, op=ALU.max))
            em("vector", lambda e, t=t: e.tensor_reduce(
                out=bc1[:, 5:6], in_=xmid[:, t, :], axis=AX, op=ALU.min))
            em("vector", lambda e, t=t: e.tensor_tensor(
                out=bc1[:, 4:5], in0=bc1[:, 4:5], in1=lnm[:, t:t + 1],
                op=ALU.subtract))
            em("vector", lambda e, t=t: e.tensor_tensor(
                out=bc1[:, 5:6], in0=lnm[:, t:t + 1], in1=bc1[:, 5:6],
                op=ALU.subtract))
            vpad()
            em("vector", lambda e: e.tensor_tensor(
                out=bc1[:, 4:5], in0=bc1[:, 4:5], in1=bc1[:, 5:6], op=ALU.max))
            vpad()
            em("vector", lambda e, t=t: e.tensor_tensor(
                out=bc1[:, 4:5], in0=bc1[:, 4:5], in1=lnr[:, t:t + 1],
                op=ALU.mult))
            vpad()
            if t == 0:
                em("vector", lambda e: e.tensor_copy(out=hmax[:, :],
                                                     in_=bc1[:, 4:5]))
            else:
                em("vector", lambda e: e.tensor_tensor(
                    out=hmax[:, :], in0=hmax[:, :], in1=bc1[:, 4:5], op=ALU.max))
        pmax_to_sml(hmax[:, 0:1], 11, tpf6)
        c_m1 = inc("vector", "ve", lambda e: e.tensor_copy(
            out=stage16[0:1, 0:1], in_=sml[0:1, 11:12]))
        wait("gpsimd", "ve", c_m1)
        d_m1 = dma("gpsimd", m1_in[:, :], stage16[:, :], "dCC")
        wait("gpsimd", "dCC", d_m1)
        n["cc"] += 1
        em("gpsimd", lambda e: e.collective_compute(
            "AllReduce", ALU.max, replica_groups=[list(range(NCORE))],
            ins=[m1_in.ap()], outs=[m1_out.ap()]).then_inc(sem["cc"], 1))
        wait("gpsimd", "cc", n["cc"])
        d_m1o = dma("gpsimd", stage16[:, :], m1_out[:, :], "dCC")
        wait("scalar", "dCC", d_m1o)
        c_sg1 = inc("scalar", "sc", lambda e: e.activation(
            out=sml[0:1, 10:11], in_=clips_sb[0:1, 0:1], func=AF.Sigmoid))
        wait("vector", "sc", c_sg1)
        em("vector", lambda e: e.scalar_tensor_tensor(
            out=sml[0:1, 5:6], in0=sml[0:1, 10:11], scalar=1.0 / 127.0,
            in1=stage16[0:1, 0:1], op0=ALU.mult, op1=ALU.mult))
        vpad()
        c_s1 = inc("vector", "ve", lambda e: e.reciprocal(
            sml[0:1, 6:7], sml[0:1, 5:6]))
        wait("tensor", "ve", c_s1)
        bcast(sml[0:1, 6:7], 0, tpf6)
        c_bc1 = bcast(sml[0:1, 5:6], 1, tpf6)
        wait("vector", "sc", c_bc1)
        em("vector", lambda e: e.tensor_scalar(
            out=s1sg[:, :], in0=sg_f[:, :], scalar1=bc1[:, 1:2], scalar2=None,
            op0=ALU.mult))
        c_s1f = inc("vector", "ve", lambda e: e.tensor_scalar(
            out=s1su[:, :], in0=su_f[:, :], scalar1=bc1[:, 1:2], scalar2=None,
            op0=ALU.mult))
        ps6a.close()

        ps6b = ExitStack()
        trb = ps6b.enter_context(nc.psum_tensor("trb", [128, 512], BF16))
        trb2 = ps6b.enter_context(nc.psum_tensor("trb2", [128, 512], BF16))
        g_tr6 = [0, 0]
        g_xq = 0
        for t in range(R):
            em("vector", lambda e, t=t: e.tensor_scalar(
                out=bc1[:, 4:5], in0=lnr[:, t:t + 1], scalar1=bc1[:, 0:1],
                scalar2=None, op0=ALU.mult))
            vpad()
            c_ab = inc("vector", "ve", lambda e, t=t: e.scalar_tensor_tensor(
                out=bc1[:, 5:6], in0=lnm[:, t:t + 1], scalar=-1.0,
                in1=bc1[:, 4:5], op0=ALU.mult, op1=ALU.mult))
            wait("scalar", "ve", c_ab)
            c_v = inc("scalar", "sc", lambda e, t=t: e.activation(
                out=tq1[:, :], in_=xmid[:, t, :], func=AF.Identity,
                bias=bc1[:, 5:6], scale=bc1[:, 4:5]))
            wait("vector", "sc", c_v)
            em("vector", lambda e: e.tensor_copy(out=xqi[:, :], in_=tq1[:, :]))
            em("vector", lambda e: e.tensor_copy(out=tq2[:, :], in_=xqi[:, :]))
            em("vector", lambda e: e.tensor_tensor(
                out=tq1[:, :], in0=tq1[:, :], in1=tq2[:, :], op=ALU.subtract))
            em("vector", lambda e: e.scalar_tensor_tensor(
                out=tq2[:, :], in0=tq1[:, :], scalar=0.5, in1=tq2[:, :],
                op0=ALU.is_ge, op1=ALU.add))
            em("vector", lambda e: e.tensor_scalar(
                out=tq1[:, :], in0=tq1[:, :], scalar1=-0.5, scalar2=None,
                op0=ALU.is_lt))
            em("vector", lambda e: e.tensor_tensor(
                out=tq1[:, :], in0=tq2[:, :], in1=tq1[:, :], op=ALU.subtract))
            wait("vector", "pe", g_xq)
            c_xq = inc("vector", "ve", lambda e: e.tensor_scalar(
                out=xqt[:, :], in0=tq1[:, :], scalar1=127.0, scalar2=-128.0,
                op0=ALU.min, op1=ALU.max))
            wait("tensor", "ve", c_xq)
            for g in range(2):
                tp = trb if g == 0 else trb2
                wait("tensor", "sc", g_tr6[g])
                for k4 in range(4):
                    hc = g * 4 + k4
                    fn = (lambda e, tp=tp, k4=k4, hc=hc: e.transpose(
                        tp[:, k4 * 128:(k4 + 1) * 128],
                        xqt[:, hc * 128:(hc + 1) * 128], ident_bs[:, :]))
                    if k4 == 3:
                        g_xq = inc("tensor", "pe", fn)
                    else:
                        em("tensor", fn)
                wait("scalar", "pe", g_xq)
                g_tr6[g] = inc("scalar", "sc", lambda e, tp=tp, g=g, t=t: e.copy(
                    out=QR[:, g * 4:(g + 1) * 4, t * 128:(t + 1) * 128],
                    in_=tp[:, :].rearrange("p (a b) -> p a b", a=4)))
        ps6b.close()
        s6ab.close()
        join()
        if debug and _PHASE == "xq":
            with nc.sbuf_tensor("dbq6", [128, TOK], F32) as dbq6:
                dma("gpsimd", dbg_sml[:, :], sml[:, :], "dDBG")
                for dc in range(HC):
                    c_d = inc("vector", "ve", lambda e, dc=dc: e.tensor_copy(
                        out=dbq6[:, :], in_=QR[:, dc, :]))
                    wait("gpsimd", "ve", c_d)
                    dD = dma("gpsimd", dbg_qr[dc * 128:(dc + 1) * 128, :],
                             dbq6[:, :], "dDBG")
                    wait("vector", "dDBG", dD)
        gate("xq")

        ps6c = ExitStack()
        g_psm = [ps6c.enter_context(nc.psum_tensor(f"g_ps{i}", [128, 512], F32))
                 for i in range(2)]
        u_psm = [ps6c.enter_context(nc.psum_tensor(f"u_ps{i}", [128, 512], F32))
                 for i in range(2)]
        igt = ps6c.enter_context(nc.sbuf_tensor("igt", [128, 2, HC, 128], BF16))
        iut = ps6c.enter_context(nc.sbuf_tensor("iut", [128, 2, HC, 128], BF16))
        sgs = ps6c.enter_context(nc.sbuf_tensor("sgs", [128, 2, 512], F32))

        g_slab = [0, 0]
        g_gu = [0, 0]
        g_sgs = [0, 0]
        wait("scalar", "ve", c_s1f)
        ii = 0
        for ci in range(CI):
            sl2 = ci % 2
            wait("sync", "pe", g_slab[sl2])
            src = bass.AP(ig_scr, ci * 128,
                          [[IFF, 128], [128 * IFF, HC], [1, 128]])
            gc_ = "dIG0" if sl2 == 0 else "dIG1"
            dma("sync", igt[:, sl2, :, :], src, gc_)
            src = bass.AP(iu_scr, ci * 128,
                          [[IFF, 128], [128 * IFF, HC], [1, 128]])
            d_slab = dma("sync", iut[:, sl2, :, :], src, gc_)
            wait("tensor", gc_, d_slab)
            wait("tensor", "ve", c_s1f)
            for tt4 in range(4):
                sl = ii % 2
                ii += 1
                wait("tensor", "ve", g_gu[sl])
                for (psm, wt) in ((g_psm, igt), (u_psm, iut)):
                    for hc in range(HC):
                        fn = (lambda e, psm=psm, wt=wt, hc=hc, sl=sl, sl2=sl2,
                              tt4=tt4: e.matmul(
                                  psm[sl][:, :], wt[:, sl2, hc, :],
                                  QR[:, hc, tt4 * 512:(tt4 + 1) * 512],
                                  start=(hc == 0), stop=(hc == HC - 1),
                                  skip_group_check=True))
                        if psm is u_psm and hc == HC - 1:
                            c_gu = inc("tensor", "pe", fn)
                        else:
                            em("tensor", fn)
                if tt4 == 3:
                    g_slab[sl2] = c_gu
                wait("scalar", "pe", c_gu)
                wait("scalar", "ve", g_sgs[sl])
                c_silu = inc("scalar", "sc", lambda e, sl=sl, ci=ci: e.activation(
                    out=sgs[:, sl, :], in_=g_psm[sl][:, :], func=AF.Silu,
                    scale=s1sg[:, ci:ci + 1]))
                wait("vector", "sc", c_silu)
                em("vector", lambda e, sl=sl, ci=ci, tt4=tt4: e.scalar_tensor_tensor(
                    out=hq[:, ci, tt4 * 512:(tt4 + 1) * 512], in0=u_psm[sl][:, :],
                    scalar=s1su[:, ci:ci + 1], in1=sgs[:, sl, :], op0=ALU.mult,
                    op1=ALU.mult))
                g_sgs[sl] = g_gu[sl] = c_h = inc(
                    "vector", "ve",
                    lambda e, ci=ci, tt4=tt4: e.reduce_max(
                        out=bc1[:, 4:5], in_=hq[:, ci, tt4 * 512:(tt4 + 1) * 512],
                        axis=AX, apply_absolute_value=True))
                vpad()
                if ci == 0 and tt4 == 0:
                    em("vector", lambda e: e.tensor_copy(out=hmax[:, :],
                                                         in_=bc1[:, 4:5]))
                else:
                    em("vector", lambda e: e.tensor_tensor(
                        out=hmax[:, :], in0=hmax[:, :], in1=bc1[:, 4:5],
                        op=ALU.max))
        ps6c.close()

        s6d = ExitStack()
        tq1 = s6d.enter_context(nc.sbuf_tensor("tq1b", [128, 1024], F32))
        tq2 = s6d.enter_context(nc.sbuf_tensor("tq2b", [128, 1024], F32))
        hqi = s6d.enter_context(nc.sbuf_tensor("hqi", [128, 1024], mybir.dt.int32))
        ps6d = ExitStack()
        tpf7 = ps6d.enter_context(nc.psum_tensor("tpf7", [128, 128], F32))
        bcp7 = ps6d.enter_context(nc.psum_tensor("bcp7", [128, 512], F32))
        pmax_to_sml(hmax[:, 0:1], 11, tpf7)
        c_m2 = inc("vector", "ve", lambda e: e.tensor_copy(
            out=stage16[0:1, 0:1], in_=sml[0:1, 11:12]))
        wait("gpsimd", "ve", c_m2)
        d_m2 = dma("gpsimd", m2_in[:, :], stage16[:, :], "dCC")
        wait("gpsimd", "dCC", d_m2)
        n["cc"] += 1
        em("gpsimd", lambda e: e.collective_compute(
            "AllReduce", ALU.max, replica_groups=[list(range(NCORE))],
            ins=[m2_in.ap()], outs=[m2_out.ap()]).then_inc(sem["cc"], 1))
        wait("gpsimd", "cc", n["cc"])
        d_m2o = dma("gpsimd", stage16[:, :], m2_out[:, :], "dCC")
        wait("scalar", "dCC", d_m2o)
        c_sg2 = inc("scalar", "sc", lambda e: e.activation(
            out=sml[0:1, 10:11], in_=clips_sb[0:1, 1:2], func=AF.Sigmoid))
        wait("vector", "sc", c_sg2)
        em("vector", lambda e: e.scalar_tensor_tensor(
            out=sml[0:1, 8:9], in0=sml[0:1, 10:11], scalar=1.0 / 127.0,
            in1=stage16[0:1, 0:1], op0=ALU.mult, op1=ALU.mult))
        vpad()
        c_s2 = inc("vector", "ve", lambda e: e.reciprocal(
            sml[0:1, 9:10], sml[0:1, 8:9]))
        wait("tensor", "ve", c_s2)
        bcast(sml[0:1, 9:10], 2, tpf7)
        c_sd1 = inc("vector", "ve", lambda e: e.tensor_scalar(
            out=sdrow[0:1, :], in0=sdrow[0:1, :], scalar1=sml[0:1, 8:9],
            scalar2=None, op0=ALU.mult))
        ps6d.close()

        wait("vector", "sc", n["sc"])
        for ci in range(CI):
            for th in range(2):
                lo, hi = th * 1024, (th + 1) * 1024
                em("vector", lambda e, ci=ci, lo=lo, hi=hi: e.tensor_scalar(
                    out=tq1[:, :], in0=hq[:, ci, lo:hi], scalar1=bc1[:, 2:3],
                    scalar2=None, op0=ALU.mult))
                em("vector", lambda e: e.tensor_copy(out=hqi[:, :], in_=tq1[:, :]))
                em("vector", lambda e: e.tensor_copy(out=tq2[:, :], in_=hqi[:, :]))
                em("vector", lambda e: e.tensor_tensor(
                    out=tq1[:, :], in0=tq1[:, :], in1=tq2[:, :], op=ALU.subtract))
                em("vector", lambda e: e.scalar_tensor_tensor(
                    out=tq2[:, :], in0=tq1[:, :], scalar=0.5, in1=tq2[:, :],
                    op0=ALU.is_ge, op1=ALU.add))
                em("vector", lambda e: e.tensor_scalar(
                    out=tq1[:, :], in0=tq1[:, :], scalar1=-0.5, scalar2=None,
                    op0=ALU.is_lt))
                em("vector", lambda e: e.tensor_tensor(
                    out=tq1[:, :], in0=tq2[:, :], in1=tq1[:, :], op=ALU.subtract))
                em("vector", lambda e, ci=ci, lo=lo, hi=hi: e.tensor_scalar(
                    out=hqb(ci, lo, hi), in0=tq1[:, :], scalar1=127.0,
                    scalar2=-128.0, op0=ALU.min, op1=ALU.max))
        if debug:
            for g0 in range(0, CI, 8):
                g1 = min(g0 + 8, CI)
                for ci in range(g0, g1):
                    em("vector", lambda e, ci=ci, g0=g0: e.tensor_copy(
                        out=tq1[:, (ci - g0) * 128:(ci - g0 + 1) * 128],
                        in_=hqb(ci, 0, 128)))
                c_d = inc("vector", "ve", lambda e: e.engine_nop())
                wait("gpsimd", "ve", c_d)
                dD = dma("gpsimd", dbg_hq[:, g0 * 128:g1 * 128],
                         tq1[:, 0:(g1 - g0) * 128], "dDBG")
                wait("vector", "dDBG", dD)
        c_hq = inc("vector", "ve", lambda e: e.engine_nop())
        s6d.close()
        join()
        gate("hq")

        sdb = E6(nc.sbuf_tensor("sdb", [128, 1024], F32))
        t6e = E6(nc.sbuf_tensor("t6e", [128, 1024], F32))
        idt = E6(nc.sbuf_tensor("idt", [128, 2, 1024], BF16))
        wait("tensor", "ve", c_sd1)
        with nc.psum_tensor("bcpE", [128, 512], F32) as bcpE:
            for k in range(2):
                c_bs = inc("tensor", "pe", lambda e, k=k: e.matmul(
                    bcpE[:, :], ones_row[0:1, :],
                    sdrow[0:1, k * 512:(k + 1) * 512],
                    start=True, stop=True, skip_group_check=True))
                wait("vector", "pe", c_bs)
                c_bv = inc("vector", "ve", lambda e, k=k: e.tensor_copy(
                    out=sdb[:, k * 512:(k + 1) * 512], in_=bcpE[:, :]))
                wait("tensor", "ve", c_bv)
        ps6e = ExitStack()
        o4 = [ps6e.enter_context(nc.psum_tensor(f"o4_{i}", [128, 1024], F32))
              for i in range(4)]
        g_id = [0, 0]
        d_out = 0
        for tg in range(4):
            for ci in range(CI):
                sl = ci % 2
                wait("sync", "pe", g_id[sl])
                dc_ = "dID0" if sl == 0 else "dID1"
                d_id = dma("sync", idt[:, sl, :], id_scr[ci, :, :], dc_)
                wait("tensor", dc_, d_id)
                if ci == 0:
                    wait("tensor", "ve", n["ve"])
                for tt in range(4):
                    t = tg * 4 + tt
                    for dh in range(2):
                        fn = (lambda e, tt=tt, t=t, ci=ci, sl=sl, dh=dh: e.matmul(
                            o4[tt][:, dh * 512:(dh + 1) * 512],
                            hqb(ci, t * 128, (t + 1) * 128),
                            idt[:, sl, dh * 512:(dh + 1) * 512],
                            start=(ci == 0), stop=True,
                            skip_group_check=True))
                        if tt == 3 and dh == 1:
                            c_dn = inc("tensor", "pe", fn)
                        else:
                            em("tensor", fn)
                g_id[sl] = c_dn
            wait("vector", "pe", c_dn)
            for tt in range(4):
                t = tg * 4 + tt
                em("vector", lambda e, tt=tt: e.tensor_tensor(
                    out=t6e[:, :], in0=o4[tt][:, :], in1=sdb[:, :],
                    op=ALU.mult))
                c_fin = inc("vector", "ve", lambda e, t=t: e.tensor_tensor(
                    out=xmid[:, t, :], in0=xmid[:, t, :], in1=t6e[:, :],
                    op=ALU.add))
                wait("gpsimd", "ve", c_fin)
                d_out = dma("gpsimd", out_t[t * 128:(t + 1) * 128, :],
                            xmid[:, t, :], "dOUT")
            wait("tensor", "ve", n["ve"])
        ps6e.close()
        p6.close()

        wait("gpsimd", "dOUT", d_out)
        wait("sync", "dOUT", d_out)


    except _Trunc:
        st.pop_all()

    with nc.Block() as block:
        clr = range(min(s_.num for s_ in sem.values()), 256)
        nc.gpsimd.dma_reset(clr)
        nc.gpsimd.sem_clear(clr)
        nc.all_engine_barrier()

        @block.sync
        def _(eng):
            for fn in prog["sync"]:
                fn(eng)

        @block.tensor
        def _(eng):
            for fn in prog["tensor"]:
                fn(eng)

        @block.vector
        def _(eng):
            for fn in prog["vector"]:
                fn(eng)

        @block.scalar
        def _(eng):
            for fn in prog["scalar"]:
                fn(eng)

        @block.gpsimd
        def _(eng):
            for fn in prog["gpsimd"]:
                fn(eng)

    st.close()
    global _LAST_N, _LAST_SEM
    _LAST_N = dict(n)
    _LAST_SEM = {k: v for k, v in sem.items()}
    return nc



_CACHE = {}


def _host_prep(L0_attn_x, L0_attn_wq, L0_attn_wk, L0_attn_wv, L0_ffn_clip_x,
               L0_ffn_clip_down, L0_ffn_w_gate, L0_ffn_w_up, L0_ffn_w_down):
    x = np.ascontiguousarray(np.asarray(L0_attn_x, dtype=np.float32)
                             .reshape(S, D))
    wq = np.asarray(L0_attn_wq, dtype=np.float32)
    wk = np.asarray(L0_attn_wk, dtype=np.float32)
    wv = np.asarray(L0_attn_wv, dtype=np.float32)
    wg = np.asarray(L0_ffn_w_gate, dtype=np.float32)
    wu = np.asarray(L0_ffn_w_up, dtype=np.float32)
    wd = np.asarray(L0_ffn_w_down, dtype=np.float32)

    perm = np.concatenate([np.arange(0, D, 2), np.arange(1, D, 2)])
    wq_t = np.ascontiguousarray(wq[perm].T)
    wk_t = np.ascontiguousarray(wk[perm].T)
    wv_t = np.ascontiguousarray(wv.T)

    ident_b = np.eye(128, dtype=ml_dtypes.bfloat16)
    ident_f = np.eye(128, dtype=np.float32)
    i_ = np.arange(512, dtype=np.float64)
    phi64 = np.power(10000.0, -2.0 * (i_ - 1.0) / 1024.0)
    phi_hi = phi64.astype(np.float32)
    phi_lo = (phi64 - phi_hi.astype(np.float64)).astype(np.float32)
    phi_hi = np.ascontiguousarray(phi_hi.reshape(4, 128).T)
    phi_lo = np.ascontiguousarray(phi_lo.reshape(4, 128).T)
    clips = np.array([[np.float32(np.asarray(L0_ffn_clip_x).ravel()[0]),
                       np.float32(np.asarray(L0_ffn_clip_down).ravel()[0])]],
                     dtype=np.float32)

    in_maps = []
    rows_of = []
    for j in range(NCORE):
        rows = np.concatenate([
            np.arange(1024 * (r - 1) + 128 * ((j + r) % 8),
                      1024 * (r - 1) + 128 * ((j + r) % 8) + 128)
            for r in range(1, R + 1)])
        rows_of.append(rows)
        x_own = np.ascontiguousarray(x[rows]).reshape(R, 128, D)
        pos_own = rows.astype(np.float32).reshape(1, TOK)
        dm = np.zeros((R, 8, 128, 128), dtype=np.float32)
        tri = np.triu(np.ones((128, 128), dtype=np.float32))
        for r in range(1, R + 1):
            o = (j + r) % 8
            for m in range(8):
                op = (m + r) % 8
                if op < o:
                    dm[r - 1, m] = 1.0
                elif op == o:
                    dm[r - 1, m] = tri
        in_maps.append({
            "x_own": x_own, "pos_own": pos_own,
            "dmask": dm.astype(ml_dtypes.bfloat16),
            "phi_hi": phi_hi, "phi_lo": phi_lo,
            "ident_b": ident_b, "ident_f": ident_f,
            "clips": clips, "wq_t": wq_t, "wk_t": wk_t, "wv_t": wv_t,
            "wg_ih": wg, "wu_ih": wu, "wd_oi": wd,
        })
    return in_maps, rows_of


def kernel(L0_attn_x, L0_attn_wq, L0_attn_wk, L0_attn_wv, L0_ffn_clip_x,
           L0_ffn_clip_down, L0_ffn_w_gate, L0_ffn_w_up, L0_ffn_w_down):
    in_maps, rows_of = _host_prep(
        L0_attn_x, L0_attn_wq, L0_attn_wk, L0_attn_wv, L0_ffn_clip_x,
        L0_ffn_clip_down, L0_ffn_w_gate, L0_ffn_w_up, L0_ffn_w_down)
    if "nc" not in _CACHE:
        _CACHE["nc"] = build_nc()
    nc = _CACHE["nc"]
    res = run_bass_kernel_spmd(nc, in_maps, core_ids=list(range(NCORE)))
    out = np.empty((S, D), dtype=np.float32)
    for j in range(NCORE):
        out[rows_of[j]] = res.results[j]["out"]
    return out.reshape(1, S, D)


# version 14 (final)
# speedup vs baseline: 1.1147x; 1.0016x over previous
import math
import os
from contextlib import ExitStack

import numpy as np
import ml_dtypes

import concourse.bass as bass
import concourse.mybir as mybir
from concourse.bass_utils import run_bass_kernel_spmd

F32 = mybir.dt.float32
BF16 = mybir.dt.bfloat16
F16 = mybir.dt.float16
AF = mybir.ActivationFunctionType
ALU = mybir.AluOpType
AX = mybir.AxisListType.X

NCORE = 8
S = 16384
D = 1024
R = 16
TOK = 2048
HC = 8
IFF = 2816
CI = 22
EPS = 1e-5

LN1E4 = float(np.log(10000.0))
TWO_PI = 2.0 * math.pi
C1 = float(np.float32(int(TWO_PI * 2**9) / 2**9))
C2 = float(np.float32(int((TWO_PI - C1) * 2**20) / 2**20))
C3 = float(TWO_PI - C1 - C2)
PI = float(math.pi)
HALF_PI = float(math.pi / 2)
INV_2PI = float(1.0 / TWO_PI)

ENGS = ("sync", "tensor", "vector", "scalar", "gpsimd")


def build_nc(debug=False):
    nc = bass.Bass("TRN2", target_bir_lowering=False, debug=False,
                   detect_race_conditions=False)

    x_own = nc.dram_tensor("x_own", [R, 128, D], F32, kind="ExternalInput")
    pos_own = nc.dram_tensor("pos_own", [1, TOK], F32, kind="ExternalInput")
    dmask = nc.dram_tensor("dmask", [R, 8, 128, 128], BF16, kind="ExternalInput")
    phi_hi = nc.dram_tensor("phi_hi", [128, 4], F32, kind="ExternalInput")
    phi_lo = nc.dram_tensor("phi_lo", [128, 4], F32, kind="ExternalInput")
    ident_b = nc.dram_tensor("ident_b", [128, 128], BF16, kind="ExternalInput")
    ident_f = nc.dram_tensor("ident_f", [128, 128], F32, kind="ExternalInput")
    clips = nc.dram_tensor("clips", [1, 2], F32, kind="ExternalInput")
    wq_t = nc.dram_tensor("wq_t", [D, D], F32, kind="ExternalInput")
    wk_t = nc.dram_tensor("wk_t", [D, D], F32, kind="ExternalInput")
    wv_t = nc.dram_tensor("wv_t", [D, D], F32, kind="ExternalInput")
    wg_ih = nc.dram_tensor("wg_ih", [IFF, D], F32, kind="ExternalInput")
    wu_ih = nc.dram_tensor("wu_ih", [IFF, D], F32, kind="ExternalInput")
    wd_oi = nc.dram_tensor("wd_oi", [D, IFF], F32, kind="ExternalInput")

    out_t = nc.dram_tensor("out", [TOK, D], F32, kind="ExternalOutput")
    if debug:
        dbg_qr = nc.dram_tensor("dbg_qr", [HC * 128, TOK], F32, kind="ExternalOutput")
        dbg_sml = nc.dram_tensor("dbg_sml", [1, 64], F32, kind="ExternalOutput")
        dbg_negb = nc.dram_tensor("dbg_negb", [128, 1], F32, kind="ExternalOutput")
        dbg_den = nc.dram_tensor("dbg_den", [128, 16], F32, kind="ExternalOutput")
        dbg_osb = nc.dram_tensor("dbg_osb", [128, R * 1024], F32, kind="ExternalOutput")
        dbg_xmid = nc.dram_tensor("dbg_xmid", [128, R * 1024], F32, kind="ExternalOutput")
        dbg_hq = nc.dram_tensor("dbg_hq", [128, CI * 128], F32, kind="ExternalOutput")
        dbg_xn = nc.dram_tensor("dbg_xn", [HC * 128, TOK], F32, kind="ExternalOutput")
        dbg_nrm = nc.dram_tensor("dbg_nrm", [3, 16], F32, kind="ExternalOutput")
        dbg_row = nc.dram_tensor("dbg_row", [1, 64], F32, kind="ExternalOutput")
        dbg_xt = nc.dram_tensor("dbg_xt", [2, 128, 1024], F32, kind="ExternalOutput")
        dbg_xnt = nc.dram_tensor("dbg_xnt", [2, 128, 1024], F32, kind="ExternalOutput")
        dbg_cos = nc.dram_tensor("dbg_cos", [4 * 128, 1024], F32, kind="ExternalOutput")
        dbg_sin = nc.dram_tensor("dbg_sin", [4 * 128, 1024], F32, kind="ExternalOutput")

    kr_in_h = [nc.dram_tensor(f"kr_in_h{h}", [D, TOK // 2], BF16)
               for h in range(2)]
    kr_out_h = [nc.dram_tensor(f"kr_out_h{h}", [NCORE * D, TOK // 2], BF16)
                for h in range(2)]
    v_in = nc.dram_tensor("v_in", [TOK, D], BF16)
    v_out = nc.dram_tensor("v_out", [NCORE * TOK, D], BF16)
    nrm_in = nc.dram_tensor("nrm_in", [1, 16], F32)
    nrm_out = nc.dram_tensor("nrm_out", [1, 16], F32)
    m1_in = nc.dram_tensor("m1_in", [1, 16], F32)
    m1_out = nc.dram_tensor("m1_out", [1, 16], F32)
    m2_in = nc.dram_tensor("m2_in", [1, 16], F32)
    m2_out = nc.dram_tensor("m2_out", [1, 16], F32)
    ig_scr = nc.dram_tensor("ig_scr", [22, 128, HC * 128], BF16)
    iu_scr = nc.dram_tensor("iu_scr", [22, 128, HC * 128], BF16)
    id_scr = nc.dram_tensor("id_scr", [CI, 128, D], BF16)

    prog = {e: [] for e in ENGS}
    DMAS = ("dC", "dX0", "dX1", "dKV", "dV0", "dV1", "dB0", "dB1", "dP0", "dP1",
            "dI0", "dI1", "dIG0", "dIG1", "dID0", "dID1", "dRW", "dWT",
            "dCC", "dOUT", "dDBG")
    n = {k: 0 for k in DMAS}
    n.update({"cc": 0, "pe": 0, "ve": 0, "sc": 0})

    st = ExitStack()
    E = st.enter_context
    sem = {}
    for s_ in list(n.keys()):
        sem[s_] = E(nc.semaphore(s_))

    def em(eng, fn):
        prog[eng].append(fn)

    def dma(eng, out, in_, ctr):
        n[ctr] += 16
        em(eng, lambda e, o=out, i=in_: e.dma_start(out=o, in_=i)
           .then_inc(sem[ctr], 16))
        return n[ctr]

    def wait(eng, ctr, val):
        em(eng, lambda e, c=ctr, v=val: e.wait_ge(sem[c], v))

    def inc(eng, ctr, fn):
        n[ctr] += 1
        em(eng, lambda e, f=fn, c=ctr: f(e).then_inc(sem[c], 1))
        return n[ctr]

    def join():
        marks = {}
        for e_, c_ in (("tensor", "pe"), ("vector", "ve"), ("scalar", "sc")):
            marks[c_] = inc(e_, c_, lambda e: e.drain())
        snap = dict(n)
        for e_ in ENGS:
            for c_, v_ in snap.items():
                wait(e_, c_, v_)

    _PHASE = os.environ.get("K_PHASE", "all")

    class _Trunc(Exception):
        pass

    def gate(tag):
        if _PHASE == tag:
            join()
            raise _Trunc()

    ident_bs = E(nc.sbuf_tensor("ident_bs", [128, 128], BF16))
    ident_fs = E(nc.sbuf_tensor("ident_fs", [128, 128], F32))
    ones_row = E(nc.sbuf_tensor("ones_row", [1, 128], F32))
    ones_col = E(nc.sbuf_tensor("ones_col", [128, 1], BF16))
    negB = E(nc.sbuf_tensor("negB", [128, 1], F32))
    phih_sb = E(nc.sbuf_tensor("phih_sb", [128, 4], F32))
    phil_sb = E(nc.sbuf_tensor("phil_sb", [128, 4], F32))
    clips_sb = E(nc.sbuf_tensor("clips_sb", [1, 2], F32))
    sml = E(nc.sbuf_tensor("sml", [1, 64], F32))
    bc1 = E(nc.sbuf_tensor("bc1", [128, 8], F32))
    stage16 = E(nc.sbuf_tensor("stage16", [1, 16], F32))
    mg_t = E(nc.sbuf_tensor("mg_t", [128, 128], F32))
    mu_t = E(nc.sbuf_tensor("mu_t", [128, 128], F32))
    md_t = E(nc.sbuf_tensor("md_t", [128, 128], F32))
    sg_f = E(nc.sbuf_tensor("sg_f", [128, 22], F32))
    su_f = E(nc.sbuf_tensor("su_f", [128, 22], F32))
    s1sg = E(nc.sbuf_tensor("s1sg", [128, 22], F32))
    s1su = E(nc.sbuf_tensor("s1su", [128, 22], F32))
    lnm = E(nc.sbuf_tensor("lnm", [128, 16], F32))
    lnr = E(nc.sbuf_tensor("lnr", [128, 16], F32))
    hmax = E(nc.sbuf_tensor("hmax", [128, 1], F32))
    pad128 = E(nc.sbuf_tensor("pad128", [128, 128], F32))
    epsb = E(nc.sbuf_tensor("epsb", [128, 1], F32))
    sdrow = E(nc.sbuf_tensor("sdrow", [1, D], F32))
    gath = E(nc.sbuf_tensor("gath", [128, 128], F32))
    snap_sb = E(nc.sbuf_tensor("snap_sb", [1, 64], F32))
    QR = E(nc.sbuf_tensor("QR", [128, HC, TOK], BF16))
    spc = E(nc.sbuf_tensor("spc", [128, 1], F32))


    def vpad():
        em("vector", lambda e: e.tensor_copy(out=spc[:, :], in_=ones_col[:, :]))

    def pmax_to_sml(src_col_ap, dst, tpf):
        vpad()
        cve = inc("vector", "ve", lambda e: e.tensor_copy(
            out=pad128[:, 0:1], in_=src_col_ap))
        wait("tensor", "ve", cve)
        wait("tensor", "sc", n["sc"])
        cpe = inc("tensor", "pe", lambda e: e.transpose(
            tpf[:, :], pad128[:, :], ident_fs[:, :]))
        wait("vector", "pe", cpe)
        r_ = inc("vector", "ve", lambda e, dst=dst: e.reduce_max(
            out=sml[0:1, dst:dst + 1], in_=tpf[0:1, 0:128], axis=AX))
        vpad()
        return r_

    def bcast(src_ap, dstcol, bcp):
        wait("tensor", "sc", n["sc"])
        wait("tensor", "ve", n["ve"])
        cpe = inc("tensor", "pe", lambda e: e.matmul(
            bcp[:, 0:1], ones_row[0:1, :], src_ap, start=True, stop=True,
            skip_group_check=True))
        wait("scalar", "pe", cpe)
        return inc("scalar", "sc", lambda e, dstcol=dstcol: e.copy(
            out=bc1[:, dstcol:dstcol + 1], in_=bcp[:, 0:1]))

    try:
        dma("sync", ident_bs[:, :], ident_b[:, :], "dC")
        dma("sync", ident_fs[:, :], ident_f[:, :], "dC")
        dma("sync", phih_sb[:, :], phi_hi[:, :], "dC")
        dma("sync", phil_sb[:, :], phi_lo[:, :], "dC")
        d_consts = dma("sync", clips_sb[:, :], clips[:, :], "dC")
        wait("tensor", "dC", d_consts)
        wait("scalar", "dC", d_consts)

        wait("vector", "dC", d_consts)
        em("vector", lambda e: e.memset(ones_row[:, :], 1.0))
        em("vector", lambda e: e.memset(ones_col[:, :], 1.0))
        em("vector", lambda e: e.memset(pad128[:, :], 0.0))
        em("vector", lambda e: e.memset(sml[:, :], 0.0))
        em("vector", lambda e: e.memset(mg_t[:, :], 0.0))
        em("vector", lambda e: e.memset(mu_t[:, :], 0.0))
        em("vector", lambda e: e.memset(md_t[:, :], 0.0))
        em("vector", lambda e: e.memset(tmpc_init[0][:, :], 0.0)
           if False else e.memset(bc1[:, :], 0.0))
        em("vector", lambda e: e.memset(epsb[:, :], EPS))
        c_consts = inc("vector", "ve", lambda e: e.memset(stage16[:, :], 0.0))

        p1 = ExitStack()
        E1 = p1.enter_context
        pos_sb = E1(nc.sbuf_tensor("pos_sb", [1, TOK], F32))
        XN = E1(nc.sbuf_tensor("XN", [128, HC, TOK], BF16))
        costab = E1(nc.sbuf_tensor("costab", [128, 4, 1024], F32))
        sintab = E1(nc.sbuf_tensor("sintab", [128, 4, 1024], F32))
        w_bf = E1(nc.sbuf_tensor("w_bf", [128, 2, HC, 1024], BF16))
        xt = E1(nc.sbuf_tensor("xt", [128, 2, 1024], F32))
        xnt = E1(nc.sbuf_tensor("xnt", [128, 1024], BF16))
        sqv = E1(nc.sbuf_tensor("sqv", [128, 2, 1024], BF16))
        t1 = E1(nc.sbuf_tensor("t1", [128, 1024], F32))
        t2 = E1(nc.sbuf_tensor("t2", [128, 1024], F32))
        krs = E1(nc.sbuf_tensor("krs", [128, 2, 1024], BF16))
        vst = E1(nc.sbuf_tensor("vst", [128, 2, 1024], BF16))
        posb = E1(nc.sbuf_tensor("posb", [128, 1024], F32))
        angb = E1(nc.sbuf_tensor("angb", [128, 1024], F32))
        kkb = E1(nc.sbuf_tensor("kkb", [128, 1024], F32))
        redb = E1(nc.sbuf_tensor("redb", [128, 1024], F32))
        stt1 = E1(nc.sbuf_tensor("stt1", [128, 16], F32))
        stt2 = E1(nc.sbuf_tensor("stt2", [128, 16], F32))
        angi = E1(nc.sbuf_tensor("angi", [128, 1024], mybir.dt.int32))
        tmpc = E1(nc.sbuf_tensor("tmpc", [128, 8], F32))

        dma("sync", pos_sb[:, :], pos_own[:, :], "dC")
        for hc in range(HC):
            dma("gpsimd", w_bf[:, 0, hc, :], wv_t[hc * 128:(hc + 1) * 128, :], "dWT")
        for hc in range(HC):
            dma("gpsimd", w_bf[:, 1, hc, :], wk_t[hc * 128:(hc + 1) * 128, :], "dWT")
        d_wvk = n["dWT"]

        ps1a = ExitStack()
        trp = ps1a.enter_context(nc.psum_tensor("trp", [128, 512], BF16))
        trp2 = ps1a.enter_context(nc.psum_tensor("trp2", [128, 512], BF16))
        g_x = [0, 0]
        g_xs = [0, 0]
        g_xn = 0
        g_tr = [0, 0]
        for t in range(R):
            sl = t % 2
            xc = "dX0" if sl == 0 else "dX1"
            wait("sync", "ve", g_x[sl])
            wait("sync", "sc", g_xs[sl])
            d_x = dma("sync", xt[:, sl, :], x_own[t, :, :], xc)
            wait("scalar", xc, d_x)
            if debug and _PHASE == "qkv" and t < 2:
                wait("gpsimd", xc, d_x)
                dX = dma("gpsimd", dbg_xt[t, :, :], xt[:, sl, :], "dDBG")
                wait("sync", "dDBG", dX)
            c_sq = inc("scalar", "sc", lambda e, sl=sl: e.activation(
                out=t1[:, :], in_=xt[:, sl, :], func=AF.Square))
            wait("vector", xc, d_x)
            wait("vector", "sc", c_sq)
            em("vector", lambda e, sl=sl, t=t: e.reduce_sum(
                stt1[:, t:t + 1], xt[:, sl, :], axis=AX))
            em("vector", lambda e, t=t: e.reduce_sum(
                stt2[:, t:t + 1], t1[:, :], axis=AX))
            em("vector", lambda e, t=t: e.tensor_scalar(
                out=stt1[:, t:t + 1], in0=stt1[:, t:t + 1], scalar1=1.0 / D,
                scalar2=None, op0=ALU.mult))
            em("vector", lambda e, t=t: e.tensor_scalar(
                out=stt2[:, t:t + 1], in0=stt2[:, t:t + 1], scalar1=1.0 / D,
                scalar2=None, op0=ALU.mult))
            em("vector", lambda e, t=t: e.scalar_tensor_tensor(
                out=tmpc[:, 0:1], in0=stt1[:, t:t + 1], scalar=-1.0,
                in1=stt1[:, t:t + 1], op0=ALU.mult, op1=ALU.mult))
            vpad()
            c_var = inc("vector", "ve", lambda e, t=t: e.tensor_tensor(
                out=stt2[:, t:t + 1], in0=stt2[:, t:t + 1], in1=tmpc[:, 0:1],
                op=ALU.add))
            wait("scalar", "ve", c_var)
            c_rt = inc("scalar", "sc", lambda e, t=t: e.activation(
                out=tmpc[:, 1:2], in_=stt2[:, t:t + 1], func=AF.Sqrt,
                bias=epsb[:, 0:1]))
            wait("vector", "sc", c_rt)
            em("vector", lambda e: e.reciprocal(tmpc[:, 2:3], tmpc[:, 1:2]))
            em("vector", lambda e, t=t: e.tensor_scalar(
                out=tmpc[:, 3:4], in0=stt2[:, t:t + 1], scalar1=EPS, scalar2=-0.5,
                op0=ALU.add, op1=ALU.mult))
            em("vector", lambda e: e.tensor_tensor(
                out=tmpc[:, 4:5], in0=tmpc[:, 2:3], in1=tmpc[:, 2:3], op=ALU.mult))
            vpad()
            em("vector", lambda e: e.tensor_tensor(
                out=tmpc[:, 4:5], in0=tmpc[:, 4:5], in1=tmpc[:, 3:4], op=ALU.mult))
            vpad()
            em("vector", lambda e: e.tensor_scalar(
                out=tmpc[:, 4:5], in0=tmpc[:, 4:5], scalar1=1.5, scalar2=None,
                op0=ALU.add))
            vpad()
            em("vector", lambda e: e.tensor_tensor(
                out=tmpc[:, 2:3], in0=tmpc[:, 2:3], in1=tmpc[:, 4:5], op=ALU.mult))
            vpad()
            c_bias = inc("vector", "ve", lambda e, t=t: e.scalar_tensor_tensor(
                out=tmpc[:, 5:6], in0=stt1[:, t:t + 1], scalar=-1.0,
                in1=tmpc[:, 2:3], op0=ALU.mult, op1=ALU.mult))
            g_x[sl] = c_bias
            wait("scalar", "ve", c_bias)
            wait("scalar", "pe", g_xn)
            c_xn = inc("scalar", "sc", lambda e, sl=sl: e.activation(
                out=xnt[:, :], in_=xt[:, sl, :], func=AF.Identity,
                bias=tmpc[:, 5:6], scale=tmpc[:, 2:3]))
            g_xs[sl] = c_xn
            wait("tensor", "sc", c_xn)
            if debug and _PHASE == "qkv" and t < 2:
                wait("gpsimd", "sc", c_xn)
                dX = dma("gpsimd", dbg_xnt[t, :, :], xnt[:, :], "dDBG")
                wait("scalar", "dDBG", dX)
            for g in range(2):
                tp = trp if g == 0 else trp2
                wait("tensor", "sc", g_tr[g])
                for k4 in range(4):
                    hc = g * 4 + k4
                    fn = (lambda e, tp=tp, k4=k4, hc=hc: e.transpose(
                        tp[:, k4 * 128:(k4 + 1) * 128],
                        xnt[:, hc * 128:(hc + 1) * 128], ident_bs[:, :]))
                    if k4 == 3:
                        g_xn = inc("tensor", "pe", fn)
                    else:
                        em("tensor", fn)
                wait("scalar", "pe", g_xn)
                g_tr[g] = inc("scalar", "sc", lambda e, tp=tp, g=g, t=t: e.copy(
                    out=XN[:, g * 4:(g + 1) * 4, t * 128:(t + 1) * 128],
                    in_=tp[:, :].rearrange("p (a b) -> p a b", a=4)))
            if os.environ.get("KJOIN", "0") == "1":
                join()
        ps1a.close()
        join()

        ps1c = ExitStack()
        vps = [ps1c.enter_context(nc.psum_tensor(f"vps{i}", [128, 1024], F32))
               for i in range(2)]
        g_vev = [0, 0]
        g_vdm = [0, 0]
        wait("tensor", "dWT", d_wvk)
        n_v_dma = 0
        for t in range(R):
            sl = t % 2
            wait("tensor", "sc", g_vev[sl])
            for hc in range(HC):
                for k in range(2):
                    fn = (lambda e, sl=sl, hc=hc, k=k, t=t: e.matmul(
                        vps[sl][:, k * 512:(k + 1) * 512],
                        XN[:, hc, t * 128:(t + 1) * 128],
                        w_bf[:, 0, hc, k * 512:(k + 1) * 512],
                        start=(hc == 0), stop=(hc == HC - 1),
                        skip_group_check=True))
                    if hc == HC - 1 and k == 1:
                        c_vmm = inc("tensor", "pe", fn)
                    else:
                        em("tensor", fn)
            wait("scalar", "pe", c_vmm)
            wait("scalar", "dV0" if sl == 0 else "dV1", g_vdm[sl])
            g_vev[sl] = inc("scalar", "sc", lambda e, sl=sl: e.copy(
                out=vst[:, sl, :], in_=vps[sl][:, :]))
            wait("sync", "sc", g_vev[sl])
            g_vdm[sl] = n_v_dma = dma(
                "sync", v_in[t * 128:(t + 1) * 128, :], vst[:, sl, :],
                "dV0" if sl == 0 else "dV1")
        ps1c.close()
        wait("gpsimd", "dV0", n["dV0"])
        wait("gpsimd", "dV1", n["dV1"])
        n["cc"] += 1
        em("gpsimd", lambda e: e.collective_compute(
            "AllGather", ALU.bypass, replica_groups=[list(range(NCORE))],
            ins=[v_in.ap()], outs=[v_out.ap()]).then_inc(sem["cc"], 1))


        wait("gpsimd", "pe", n["pe"])
        for hc in range(HC):
            d_wq2 = dma("gpsimd", w_bf[:, 0, hc, :],
                        wq_t[hc * 128:(hc + 1) * 128, :], "dWT")

        ps1n = ExitStack()
        wait("tensor", "dWT", d_wq2)
        nps = [ps1n.enter_context(nc.psum_tensor(f"nps{i}", [128, 1024], F32))
               for i in range(2)]
        tpfn = ps1n.enter_context(nc.psum_tensor("tpfn", [128, 128], F32))
        g_nev = [0, 0]
        for proj in range(2):
            for t in range(R):
                sl = t % 2
                wait("tensor", "sc", g_nev[sl])
                for hc in range(HC):
                    for k in range(2):
                        fn = (lambda e, sl=sl, hc=hc, k=k, t=t, proj=proj: e.matmul(
                            nps[sl][:, k * 512:(k + 1) * 512],
                            XN[:, hc, t * 128:(t + 1) * 128],
                            w_bf[:, proj, hc, k * 512:(k + 1) * 512],
                            start=(hc == 0), stop=(hc == HC - 1),
                            skip_group_check=True))
                        if hc == HC - 1 and k == 1:
                            c_nmm = inc("tensor", "pe", fn)
                        else:
                            em("tensor", fn)
                wait("scalar", "pe", c_nmm)
                wait("scalar", "ve", n["ve"])
                g_nev[sl] = c_nsq = inc("scalar", "sc", lambda e, sl=sl: e.activation(
                    out=t1[:, :], in_=nps[sl][:, :], func=AF.Square))
                wait("vector", "sc", c_nsq)
                em("vector", lambda e: e.reduce_sum(
                    out=tmpc[:, 6:7], in_=t1[:, :], axis=AX))
                vpad()
                if t == 0:
                    cnv = inc("vector", "ve", lambda e: e.tensor_copy(
                        out=tmpc[:, 7:8], in_=tmpc[:, 6:7]))
                else:
                    cnv = inc("vector", "ve", lambda e: e.tensor_tensor(
                        out=tmpc[:, 7:8], in0=tmpc[:, 7:8], in1=tmpc[:, 6:7],
                        op=ALU.max))
            pmax_to_sml(tmpc[:, 7:8], proj, tpfn)
        ps1n.close()

        wait("tensor", "sc", n["sc"])
        wait("tensor", "ve", n["ve"])
        c_mkst = inc("vector", "ve", lambda e: e.tensor_copy(
            out=stage16[0:1, 0:1], in_=sml[0:1, 1:2]))
        wait("gpsimd", "ve", c_mkst)
        d_nrm = dma("gpsimd", nrm_in[:, :], stage16[:, :], "dCC")
        wait("gpsimd", "dCC", d_nrm)
        n["cc"] += 1
        em("gpsimd", lambda e: e.collective_compute(
            "AllReduce", ALU.max, replica_groups=[list(range(NCORE))],
            ins=[nrm_in.ap()], outs=[nrm_out.ap()]).then_inc(sem["cc"], 1))
        wait("gpsimd", "cc", n["cc"])
        d_nrmo = dma("gpsimd", stage16[:, :], nrm_out[:, :], "dCC")
        if debug:
            dma("gpsimd", dbg_nrm[0:1, :], nrm_in[:, :], "dDBG")
            dma("gpsimd", dbg_nrm[1:2, :], nrm_out[:, :], "dDBG")
            wait("gpsimd", "dDBG", n["dDBG"])
        wait("scalar", "dCC", d_nrmo)
        wait("scalar", "ve", n["ve"])
        em("scalar", lambda e: e.activation(
            out=sml[0:1, 2:3], in_=sml[0:1, 0:1], func=AF.Sqrt))
        c_sq2 = inc("scalar", "sc", lambda e: e.activation(
            out=sml[0:1, 3:4], in_=stage16[0:1, 0:1], func=AF.Sqrt))
        wait("vector", "sc", c_sq2)
        c_nb = inc("vector", "ve", lambda e: e.scalar_tensor_tensor(
            out=sml[0:1, 4:5], in0=sml[0:1, 2:3], scalar=-1.02 / 32.0,
            in1=sml[0:1, 3:4], op0=ALU.mult, op1=ALU.mult))
        if debug:
            em("vector", lambda e: e.memset(snap_sb[:, :], 0.0))
            em("vector", lambda e: e.tensor_copy(out=snap_sb[0:1, 0:16],
                                                 in_=sml[0:1, 0:16]))
            c_snap = inc("vector", "ve", lambda e: e.tensor_copy(
                out=snap_sb[0:1, 16:32], in_=stage16[0:1, 0:16]))
            wait("gpsimd", "ve", c_snap)
            dma("gpsimd", dbg_row[:, :], snap_sb[0:1, :], "dDBG")
        wait("tensor", "ve", c_nb)
        with nc.psum_tensor("bcp0", [128, 16], F32) as bcp0:
            c_nbm = inc("tensor", "pe", lambda e: e.matmul(
                bcp0[:, 0:1], ones_row[0:1, :], sml[0:1, 4:5], start=True,
                stop=True, skip_group_check=True))
            wait("scalar", "pe", c_nbm)
            c_negB = inc("scalar", "sc", lambda e: e.copy(
                out=negB[:, :], in_=bcp0[:, 0:1]))
        wait("tensor", "sc", c_negB)
        if debug and _PHASE == "negb":
            wait("gpsimd", "sc", c_negB)
            dma("gpsimd", dbg_sml[:, :], sml[:, :], "dDBG")
            dma("gpsimd", dbg_negb[:, :], negB[:, :], "dDBG")
            wait("gpsimd", "dDBG", n["dDBG"])
        gate("negb")


        ps1b = ExitStack()
        qe_ps = ps1b.enter_context(nc.psum_tensor("qe_ps", [128, 1024], F32))
        qo_ps = ps1b.enter_context(nc.psum_tensor("qo_ps", [128, 1024], F32))

        n_kr_dma = 0

        for half in range(2):
            for k in range(2):
                c_bcmm = inc("tensor", "pe", lambda e, k=k, half=half: e.matmul(
                    qe_ps[:, k * 512:(k + 1) * 512], ones_row[0:1, :],
                    pos_sb[0:1, half * 1024 + k * 512:half * 1024 + (k + 1) * 512],
                    start=True, stop=True, skip_group_check=True))
            wait("vector", "pe", c_bcmm)
            c_posb = inc("vector", "ve", lambda e: e.tensor_copy(
                out=posb[:, :], in_=qe_ps[:, :]))
            for pc in range(4):
                em("vector", lambda e, pc=pc: e.tensor_scalar(
                    out=redb[:, :], in0=posb[:, :], scalar1=phil_sb[:, pc:pc + 1],
                    scalar2=None, op0=ALU.mult))
                em("vector", lambda e, pc=pc: e.tensor_scalar(
                    out=angb[:, :], in0=posb[:, :], scalar1=phih_sb[:, pc:pc + 1],
                    scalar2=None, op0=ALU.mult))
                em("vector", lambda e: e.tensor_tensor(
                    out=angb[:, :], in0=angb[:, :], in1=redb[:, :], op=ALU.add))
                em("vector", lambda e: e.tensor_scalar(
                    out=kkb[:, :], in0=angb[:, :], scalar1=INV_2PI, scalar2=None,
                    op0=ALU.mult))
                em("vector", lambda e: e.tensor_copy(out=angi[:, :], in_=kkb[:, :]))
                em("vector", lambda e: e.tensor_copy(out=kkb[:, :], in_=angi[:, :]))
                em("vector", lambda e: e.tensor_scalar(
                    out=kkb[:, :], in0=kkb[:, :], scalar1=-1.0, scalar2=None,
                    op0=ALU.mult))
                em("vector", lambda e: e.scalar_tensor_tensor(
                    out=redb[:, :], in0=kkb[:, :], scalar=C1, in1=angb[:, :],
                    op0=ALU.mult, op1=ALU.add))
                em("vector", lambda e: e.scalar_tensor_tensor(
                    out=redb[:, :], in0=kkb[:, :], scalar=C2, in1=redb[:, :],
                    op0=ALU.mult, op1=ALU.add))
                em("vector", lambda e: e.scalar_tensor_tensor(
                    out=redb[:, :], in0=kkb[:, :], scalar=C3, in1=redb[:, :],
                    op0=ALU.mult, op1=ALU.add))
                em("vector", lambda e: e.tensor_scalar(
                    out=t1[:, :], in0=redb[:, :], scalar1=PI, scalar2=None,
                    op0=ALU.is_gt))
                em("vector", lambda e: e.scalar_tensor_tensor(
                    out=redb[:, :], in0=t1[:, :], scalar=-TWO_PI, in1=redb[:, :],
                    op0=ALU.mult, op1=ALU.add))
                em("vector", lambda e: e.tensor_scalar(
                    out=t1[:, :], in0=redb[:, :], scalar1=-PI, scalar2=None,
                    op0=ALU.is_lt))
                em("vector", lambda e: e.scalar_tensor_tensor(
                    out=t1[:, :], in0=t1[:, :], scalar=TWO_PI, in1=redb[:, :],
                    op0=ALU.mult, op1=ALU.add))
                em("vector", lambda e: e.tensor_scalar(
                    out=redb[:, :], in0=t1[:, :], scalar1=HALF_PI, scalar2=None,
                    op0=ALU.add))
                em("vector", lambda e: e.tensor_scalar(
                    out=t2[:, :], in0=redb[:, :], scalar1=PI, scalar2=None,
                    op0=ALU.is_gt))
                c_wr = inc("vector", "ve", lambda e: e.scalar_tensor_tensor(
                    out=t2[:, :], in0=t2[:, :], scalar=-TWO_PI, in1=redb[:, :],
                    op0=ALU.mult, op1=ALU.add))
                wait("scalar", "ve", c_wr)
                em("scalar", lambda e, pc=pc: e.activation(
                    out=sintab[:, pc, :], in_=t1[:, :], func=AF.Sin))
                c_tab = inc("scalar", "sc", lambda e, pc=pc: e.activation(
                    out=costab[:, pc, :], in_=t2[:, :], func=AF.Sin))
                wait("vector", "sc", c_tab)

            wait("tensor", "ve", c_posb)
            for proj in range(2):
                is_q = (proj == 0)
                for pc in range(4):
                    for (ps, dcx) in ((qe_ps, pc), (qo_ps, pc + 4)):
                        for hc in range(HC):
                            for k in range(2):
                                fn = (lambda e, ps=ps, dcx=dcx, hc=hc, k=k,
                                      proj=proj, half=half: e.matmul(
                                          ps[:, k * 512:(k + 1) * 512],
                                          w_bf[:, proj, hc,
                                               dcx * 128:(dcx + 1) * 128],
                                          XN[:, hc, half * 1024 + k * 512:
                                             half * 1024 + (k + 1) * 512],
                                          start=(hc == 0), stop=(hc == HC - 1),
                                          skip_group_check=True))
                                if (ps is qo_ps) and hc == HC - 1 and k == 1:
                                    c_qk = inc("tensor", "pe", fn)
                                else:
                                    em("tensor", fn)
                    wait("vector", "pe", c_qk)
                    if not is_q:
                        wait("vector", "dKV", n_kr_dma)
                        wait("vector", "sc", n["sc"])
                    em("vector", lambda e, pc=pc: e.tensor_tensor(
                        out=t1[:, :], in0=qe_ps[:, :], in1=costab[:, pc, :],
                        op=ALU.mult))
                    em("vector", lambda e, pc=pc: e.tensor_tensor(
                        out=t2[:, :], in0=qo_ps[:, :], in1=sintab[:, pc, :],
                        op=ALU.mult))
                    if is_q:
                        em("vector", lambda e, pc=pc, half=half: e.tensor_tensor(
                            out=QR[:, pc, half * 1024:(half + 1) * 1024],
                            in0=t1[:, :], in1=t2[:, :], op=ALU.add))
                    else:
                        em("vector", lambda e: e.tensor_tensor(
                            out=krs[:, 0, :], in0=t1[:, :], in1=t2[:, :], op=ALU.add))
                    em("vector", lambda e, pc=pc: e.tensor_tensor(
                        out=t1[:, :], in0=qo_ps[:, :], in1=costab[:, pc, :],
                        op=ALU.mult))
                    em("vector", lambda e, pc=pc: e.tensor_tensor(
                        out=t2[:, :], in0=qe_ps[:, :], in1=sintab[:, pc, :],
                        op=ALU.mult))
                    if is_q:
                        c_rp = inc("vector", "ve", lambda e, pc=pc, half=half:
                                   e.tensor_tensor(
                                       out=QR[:, pc + 4,
                                              half * 1024:(half + 1) * 1024],
                                       in0=t1[:, :], in1=t2[:, :], op=ALU.subtract))
                    else:
                        c_rp = inc("vector", "ve", lambda e: e.tensor_tensor(
                            out=krs[:, 1, :], in0=t1[:, :], in1=t2[:, :],
                            op=ALU.subtract))
                    wait("tensor", "ve", c_rp)
                    if not is_q:
                        for pi, dcx in ((0, pc), (1, pc + 4)):
                            wait("sync", "ve", c_rp)
                            n_kr_dma = dma(
                                "sync",
                                kr_in_h[half][dcx * 128:(dcx + 1) * 128, :],
                                krs[:, pi, :], "dKV")
            wait("gpsimd", "dKV", n["dKV"])
            n["cc"] += 1
            em("gpsimd", lambda e, half=half: e.collective_compute(
                "AllGather", ALU.bypass, replica_groups=[list(range(NCORE))],
                ins=[kr_in_h[half].ap()],
                outs=[kr_out_h[half].ap()]).then_inc(sem["cc"], 1))

        ps1b.close()
        wait("tensor", "ve", n["ve"])

        if debug:
            wait("gpsimd", "ve", n["ve"])
            wait("gpsimd", "sc", n["sc"])
            for hc in range(HC):
                dma("gpsimd", dbg_xn[hc * 128:(hc + 1) * 128, :], XN[:, hc, :], "dDBG")
            for pc in range(4):
                dma("gpsimd", dbg_cos[pc * 128:(pc + 1) * 128, :], costab[:, pc, :], "dDBG")
                dma("gpsimd", dbg_sin[pc * 128:(pc + 1) * 128, :], sintab[:, pc, :], "dDBG")
            wait("gpsimd", "dDBG", n["dDBG"])
        if debug and _PHASE == "qkv":
            with nc.sbuf_tensor("dbq0", [128, TOK], F32) as dbq0:
                for dc in range(HC):
                    c_d = inc("vector", "ve", lambda e, dc=dc: e.tensor_copy(
                        out=dbq0[:, :], in_=QR[:, dc, :]))
                    wait("gpsimd", "ve", c_d)
                    dD = dma("gpsimd", dbg_qr[dc * 128:(dc + 1) * 128, :],
                             dbq0[:, :], "dDBG")
                    wait("vector", "dDBG", dD)
        gate("qkv")


        p1.close()
        join()

        scope_o = ExitStack()
        Ea = scope_o.enter_context
        O_sb = Ea(nc.sbuf_tensor("O_sb", [128, R, 1024], F32))
        den_sb = Ea(nc.sbuf_tensor("den_sb", [128, 16], F32))
        rden = Ea(nc.sbuf_tensor("rden", [128, 16], F32))

        p5gu = ExitStack()
        E5g = p5gu.enter_context
        wsl = E5g(nc.sbuf_tensor("wsl", [128, 2, D], F32))
        vvb = E5g(nc.sbuf_tensor("vvb", [128, D], F32))
        ffb = E5g(nc.sbuf_tensor("ffb", [128, D], F32))
        intsl = E5g(nc.sbuf_tensor("intsl", [128, 2, D], BF16))
        stg5 = E5g(nc.sbuf_tensor("stg5", [128, 2, HC, 128], BF16))
        wqi = E5g(nc.sbuf_tensor("wqi", [128, D], mybir.dt.int32))
        tmpq = E5g(nc.sbuf_tensor("tmpq", [128, 4], F32))
        tps_gu = E5g(nc.psum_tensor("tps_gu", [128, 512], BF16))

        g5_w = [0, 0]
        g5_int = [0, 0]
        g5_stW = [0, 0]
        g5_tp = [0]
        u_cq = {}

        def p5_quant(mat, dst, c, uid):
            slw = uid % 2
            pc_ = "dP0" if slw == 0 else "dP1"
            wait("sync", "ve", g5_w[slw])
            d_w = dma("sync", wsl[:, slw, :],
                      mat[c * 128:(c + 1) * 128, :], pc_)
            wait("vector", pc_, d_w)
            em("vector", lambda e, dst=dst, c=c, slw=slw: e.reduce_max(
                out=dst[:, c:c + 1], in_=wsl[:, slw, :],
                axis=AX, apply_absolute_value=True))
            vpad()
            em("vector", lambda e, dst=dst, c=c, slw=slw: e.tensor_scalar(
                out=tmpq[:, 2 * slw:2 * slw + 1], in0=dst[:, c:c + 1],
                scalar1=1.0 / 127.0, scalar2=None, op0=ALU.mult))
            vpad()
            em("vector", lambda e, slw=slw: e.reciprocal(
                tmpq[:, 2 * slw + 1:2 * slw + 2],
                tmpq[:, 2 * slw:2 * slw + 1]))
            vpad()
            g5_w[slw] = inc("vector", "ve", lambda e, slw=slw: e.tensor_scalar(
                out=vvb[:, :], in0=wsl[:, slw, :],
                scalar1=tmpq[:, 2 * slw + 1:2 * slw + 2],
                scalar2=None, op0=ALU.mult))
            em("vector", lambda e: e.tensor_copy(out=wqi[:, :], in_=vvb[:, :]))
            em("vector", lambda e: e.tensor_copy(out=ffb[:, :], in_=wqi[:, :]))
            em("vector", lambda e: e.tensor_tensor(
                out=vvb[:, :], in0=vvb[:, :], in1=ffb[:, :], op=ALU.subtract))
            em("vector", lambda e: e.scalar_tensor_tensor(
                out=ffb[:, :], in0=vvb[:, :], scalar=0.5, in1=ffb[:, :],
                op0=ALU.is_ge, op1=ALU.add))
            em("vector", lambda e: e.tensor_scalar(
                out=vvb[:, :], in0=vvb[:, :], scalar1=-0.5, scalar2=None,
                op0=ALU.is_lt))
            wait("vector", "pe", g5_int[slw])
            u_cq[uid] = inc("vector", "ve", lambda e, slw=slw: e.tensor_tensor(
                out=intsl[:, slw, :], in0=ffb[:, :], in1=vvb[:, :],
                op=ALU.subtract))

        def p5_store(scr, c, uid):
            slw = uid % 2
            ic_ = "dI0" if slw == 0 else "dI1"
            wait("tensor", "ve", u_cq[uid])
            first_cp = True
            for g0 in range(0, HC, 4):
                wait("tensor", "sc", g5_tp[0])
                for k in range(4):
                    fn = (lambda e, k=k, g0=g0, slw=slw: e.transpose(
                        tps_gu[:, k * 128:(k + 1) * 128],
                        intsl[:, slw, (g0 + k) * 128:(g0 + k + 1) * 128],
                        ident_bs[:, :]))
                    if k == 3:
                        c_tp = inc("tensor", "pe", fn)
                    else:
                        em("tensor", fn)
                if g0 + 4 >= HC:
                    g5_int[slw] = c_tp
                wait("scalar", "pe", c_tp)
                if first_cp:
                    wait("scalar", ic_, g5_stW[slw])
                    first_cp = False
                g5_tp[0] = inc("scalar", "sc",
                               lambda e, g0=g0, slw=slw:
                               e.copy(out=stg5[:, slw, g0:g0 + 4, :],
                                      in_=tps_gu[:, :].rearrange(
                                          "p (a b) -> p a b", a=4)))
            wait("sync", "sc", g5_tp[0])
            g5_stW[slw] = dma("sync", scr[c, :, :],
                              stg5[:, slw, 0:HC, :].rearrange(
                                  "p a b -> p (a b)"), ic_)

        p5q = []
        for uid in range(44):
            mat, dst, scr = ((wg_ih, mg_t, ig_scr) if uid < 22
                             else (wu_ih, mu_t, iu_scr))
            c = uid % 22
            p5q.append((lambda mat=mat, dst=dst, c=c, uid=uid:
                        p5_quant(mat, dst, c, uid),
                        lambda scr=scr, c=c, uid=uid: p5_store(scr, c, uid)))
        p5_next = [0, 0]

        def p5_drain(n_units):
            for _ in range(n_units):
                if p5_next[0] < len(p5q):
                    p5q[p5_next[0]][0]()
                    p5_next[0] += 1
                lagged = (p5_next[0] - 1 if p5_next[0] < len(p5q)
                          else len(p5q))
                if p5_next[1] < lagged:
                    p5q[p5_next[1]][1]()
                    p5_next[1] += 1

        p3 = ExitStack()
        E3 = p3.enter_context
        kr_sb = E3(nc.sbuf_tensor("kr_sb", [128, 2, HC, 8, 128], BF16))
        v_sb = E3(nc.sbuf_tensor("v_sb", [128, 2, 8, 1024], BF16))
        msk_sb = E3(nc.sbuf_tensor("msk_sb", [128, 2, 8, 128], BF16))
        P_sb = E3(nc.sbuf_tensor("P_sb", [128, 2, 512], BF16))

        s_ps = [E3(nc.psum_tensor(f"s_ps{i}", [128, 512], F32)) for i in range(2)]
        o_ps = [E3(nc.psum_tensor(f"o_ps{i}", [128, 1024], F32)) for i in range(2)]
        den_ps = E3(nc.psum_tensor("den_ps", [128, 16], F32))

        def load_krm(b):
            sl = b % 2
            bc_ = "dB0" if sl == 0 else "dB1"
            half = b // 8
            HT = TOK // 2
            for m in range(8):
                src = bass.AP(kr_out_h[half], 128 * (b % 8) + m * (1024 * HT),
                              [[HT, 128], [128 * HT, 8], [1, 128]])
                dma("sync", kr_sb[:, sl, :, m, :], src, bc_)
            src = bass.AP(dmask, b * (8 * 128 * 128),
                          [[128, 128], [128 * 128, 8], [1, 128]])
            return dma("sync", msk_sb[:, sl, :, :], src, bc_)

        def load_v(b):
            sl = b % 2
            bc_ = "dB0" if sl == 0 else "dB1"
            src = bass.AP(v_out, 128 * b * D,
                          [[D, 128], [TOK * D, 8], [1, D]])
            return dma("sync", v_sb[:, sl, :, :], src, bc_)

        def load_block(b):
            d_krm[b] = load_krm(b)
            d_blk[b] = load_v(b)

        d_krm = {}
        d_blk = {}
        wait("sync", "cc", 1)
        d_blk[0] = load_v(0)
        d_blk[1] = load_v(1)
        wait("sync", "cc", 3)
        d_krm[0] = load_krm(0)
        d_krm[1] = load_krm(1)
        pe_blk_end = {}
        ve_blk_end = {}

        c_evac_prev = 0
        last_exp = {0: 0, 1: 0}
        last_pv = {0: 0, 1: 0}
        for b in range(16):
            sl = b % 2
            ntile = 16 - b
            qlo = 128 * b
            chunks = []
            q = qlo
            if ntile % 2 == 1:
                chunks.append((q, 128))
                q += 128
            while q < TOK:
                chunks.append((q, 256))
                q += 256

            wait("tensor", "dB0" if b % 2 == 0 else "dB1", d_krm[b])
            v_waited = False
            for (qoff, qw) in chunks:
                nqs = qw // 128
                diag = (qoff == qlo)
                for pair in range(4):
                    par = pair % 2
                    sps = s_ps[par]
                    wait("tensor", "sc", last_exp[par])
                    for kk in range(2):
                        ks = 2 * pair + kk
                        for dc in range(HC):
                            fn = (lambda e, sps=sps, kk=kk, ks=ks, dc=dc, sl=sl,
                                  qoff=qoff, qw=qw: e.matmul(
                                      sps[:, kk * qw:(kk + 1) * qw],
                                      kr_sb[:, sl, dc, ks, :],
                                      QR[:, dc, qoff:qoff + qw],
                                      start=(dc == 0), stop=(dc == HC - 1),
                                      skip_group_check=True))
                            if kk == 1 and dc == HC - 1:
                                c_s = inc("tensor", "pe", fn)
                            else:
                                em("tensor", fn)
                    wait("scalar", "pe", c_s)
                    wait("scalar", "pe", last_pv[par])
                    last_exp[par] = inc(
                        "scalar", "sc",
                        lambda e, sps=sps, par=par, qw=qw: e.activation(
                            out=P_sb[:, par, 0:2 * qw], in_=sps[:, 0:2 * qw],
                            func=AF.Exp, bias=negB[:, 0:1], scale=1.0 / 32.0))
                    mask_mark = None
                    if diag:
                        wait("vector", "sc", last_exp[par])
                        for kk in range(2):
                            ks = 2 * pair + kk
                            fn = (lambda e, par=par, kk=kk, ks=ks, sl=sl, qw=qw:
                                  e.tensor_tensor(
                                      out=P_sb[:, par, kk * qw:kk * qw + 128],
                                      in0=P_sb[:, par, kk * qw:kk * qw + 128],
                                      in1=msk_sb[:, sl, ks, :], op=ALU.mult))
                            if kk == 1:
                                mask_mark = inc("vector", "ve", fn)
                            else:
                                em("vector", fn)
                    if not v_waited:
                        wait("tensor", "dB0" if b % 2 == 0 else "dB1", d_blk[b])
                        v_waited = True
                    wait("tensor", "sc", last_exp[par])
                    if mask_mark is not None:
                        wait("tensor", "ve", mask_mark)
                    if pair == 0:
                        wait("tensor", "ve", c_evac_prev)
                    for kk in range(2):
                        ks = 2 * pair + kk
                        for qs in range(nqs):
                            lo = kk * qw + qs * 128
                            st0 = (pair == 0 and kk == 0)
                            for dh in range(2):
                                fn = (lambda e, par=par, ks=ks, qs=qs,
                                      lo=lo, dh=dh, sl=sl, st0=st0: e.matmul(
                                          o_ps[qs][:, dh * 512:(dh + 1) * 512],
                                          P_sb[:, par, lo:lo + 128],
                                          v_sb[:, sl, ks, dh * 512:(dh + 1) * 512],
                                          start=st0, stop=True,
                                          skip_group_check=True))
                                em("tensor", fn)
                            fn = (lambda e, par=par, qs=qs, lo=lo,
                                  st0=st0: e.matmul(
                                      den_ps[:, qs:qs + 1],
                                      P_sb[:, par, lo:lo + 128],
                                      ones_col[:, 0:1],
                                      start=(st0 and qs == 0), stop=True,
                                      skip_group_check=True))
                            if kk == 1 and qs == nqs - 1:
                                last_pv[par] = c_pv_last = inc("tensor", "pe", fn)
                            else:
                                em("tensor", fn)
                wait("vector", "pe", c_pv_last)
                for qs in range(nqs):
                    tidx = qoff // 128 + qs
                    if b == 0:
                        em("vector", lambda e, qs=qs, tidx=tidx: e.tensor_copy(
                            out=O_sb[:, tidx, :], in_=o_ps[qs][:, :]))
                        fn = (lambda e, qs=qs, tidx=tidx: e.tensor_copy(
                            out=den_sb[:, tidx:tidx + 1], in_=den_ps[:, qs:qs + 1]))
                    else:
                        em("vector", lambda e, qs=qs, tidx=tidx: e.tensor_tensor(
                            out=O_sb[:, tidx, :], in0=O_sb[:, tidx, :],
                            in1=o_ps[qs][:, :], op=ALU.add))
                        fn = (lambda e, qs=qs, tidx=tidx: e.tensor_tensor(
                            out=den_sb[:, tidx:tidx + 1],
                            in0=den_sb[:, tidx:tidx + 1],
                            in1=den_ps[:, qs:qs + 1], op=ALU.add))
                    if qs == nqs - 1:
                        c_evac_prev = inc("vector", "ve", fn)
                    else:
                        em("vector", fn)
            p5_drain(2 if b < 4 else 4)
            pe_blk_end[b] = n["pe"]
            ve_blk_end[b] = n["ve"]
            if b + 2 <= 15:
                wait("sync", "pe", pe_blk_end[b])
                wait("sync", "ve", ve_blk_end[b])
                if b + 2 == 8:
                    wait("sync", "cc", 4)
                load_block(b + 2)

        while p5_next[1] < len(p5q):
            p5_drain(4)
        em("vector", lambda e: e.tensor_scalar(
            out=sg_f[:, :], in0=mg_t[:, 0:22], scalar1=1.0 / 127.0,
            scalar2=None, op0=ALU.mult))
        c_fold = inc("vector", "ve", lambda e: e.tensor_scalar(
            out=su_f[:, :], in0=mu_t[:, 0:22], scalar1=1.0 / 127.0,
            scalar2=None, op0=ALU.mult))
        p3.close()
        p5gu.close()
        join()
        if debug and _PHASE == "attn":
            dma("gpsimd", dbg_qr[:, :],
                v_out[0:2048, :].rearrange("(a b) d -> a (b d)", b=2), "dDBG")
            dma("gpsimd", dbg_xn[:, :],
                v_in[:, :].rearrange("(a b) d -> a (b d)", b=2), "dDBG")
            with nc.sbuf_tensor("dbo0", [128, 1024], F32) as dbo0:
                dma("gpsimd", dbg_den[:, :], den_sb[:, :], "dDBG")
                for t in range(R):
                    c_d = inc("vector", "ve", lambda e, t=t: e.tensor_copy(
                        out=dbo0[:, :], in_=O_sb[:, t, :]))
                    wait("gpsimd", "ve", c_d)
                    dD = dma("gpsimd", dbg_osb[:, t * 1024:(t + 1) * 1024],
                             dbo0[:, :], "dDBG")
                    wait("vector", "dDBG", dD)
        gate("attn")

        xmid = E(nc.sbuf_tensor("xmid", [128, R, 1024], F32, side="right"))
        p4 = ExitStack()
        E4 = p4.enter_context
        xrl = E4(nc.sbuf_tensor("xrl", [128, 2, 1024], F32))
        t3 = E4(nc.sbuf_tensor("t3", [128, 2, 1024], F32))

        c_rd = inc("vector", "ve", lambda e: e.reciprocal(rden[:, :], den_sb[:, :]))
        wait("scalar", "ve", c_rd)
        g_fx = [0, 0]
        g_t3 = [0, 0]
        for t in range(R):
            sl = t % 2
            wait("sync", "ve", g_fx[sl])
            xc = "dX0" if sl == 0 else "dX1"
            d_x = dma("sync", xrl[:, sl, :], x_own[t, :, :], xc)
            wait("scalar", "ve", g_t3[sl])
            c_sc = inc("scalar", "sc", lambda e, t=t, sl=sl: e.activation(
                out=t3[:, sl, :], in_=O_sb[:, t, :], func=AF.Identity,
                scale=rden[:, t:t + 1]))
            wait("vector", xc, d_x)
            wait("vector", "sc", c_sc)
            g_t3[sl] = g_fx[sl] = inc(
                "vector", "ve", lambda e, t=t, sl=sl: e.tensor_tensor(
                    out=xmid[:, t, :], in0=t3[:, sl, :], in1=xrl[:, sl, :],
                    op=ALU.add))
        if debug:
            with nc.sbuf_tensor("dbq", [128, TOK], F32) as dbq, \
                 nc.sbuf_tensor("dbo", [128, 1024], F32) as dbo:
                for dc in range(HC):
                    c_d = inc("vector", "ve", lambda e, dc=dc: e.tensor_copy(
                        out=dbq[:, :], in_=QR[:, dc, :]))
                    wait("gpsimd", "ve", c_d)
                    dD = dma("gpsimd", dbg_qr[dc * 128:(dc + 1) * 128, :],
                             dbq[:, :], "dDBG")
                    wait("vector", "dDBG", dD)
                dma("gpsimd", dbg_sml[:, :], sml[:, :], "dDBG")
                dma("gpsimd", dbg_negb[:, :], negB[:, :], "dDBG")
                dD = dma("gpsimd", dbg_den[:, :], den_sb[:, :], "dDBG")
                for t in range(R):
                    c_d = inc("vector", "ve", lambda e, t=t: e.tensor_copy(
                        out=dbo[:, :], in_=O_sb[:, t, :]))
                    wait("gpsimd", "ve", c_d)
                    dD = dma("gpsimd", dbg_osb[:, t * 1024:(t + 1) * 1024],
                             dbo[:, :], "dDBG")
                    wait("vector", "dDBG", dD)
                dD = dma("gpsimd", dbg_xmid[:, :],
                         xmid[:, :, :].rearrange("p a b -> p (a b)"), "dDBG")
                wait("gpsimd", "dDBG", dD)

        p4.close()
        scope_o.close()
        join()
        gate("xmid")

        p5 = ExitStack()
        E5 = p5.enter_context
        wsd = E5(nc.sbuf_tensor("wsd", [128, 2, IFF], F32))
        vvd = E5(nc.sbuf_tensor("vvd", [128, IFF], F32))
        ffd = E5(nc.sbuf_tensor("ffd", [128, IFF], F32))
        intd = E5(nc.sbuf_tensor("intd", [128, 2, IFF], BF16))
        stgd = E5(nc.sbuf_tensor("stgd", [128, 2, CI, 128], BF16))
        srow = E5(nc.sbuf_tensor("srow", [1, IFF], F32))
        wqd = E5(nc.sbuf_tensor("wqd", [128, IFF], mybir.dt.int32))
        tmpd = E5(nc.sbuf_tensor("tmpd", [128, 4], F32))

        ps5 = ExitStack()
        tps = [ps5.enter_context(nc.psum_tensor(f"tps{i}", [128, 512], BF16))
               for i in range(2)]
        tpf = ps5.enter_context(nc.psum_tensor("tpf", [128, 128], F32))

        g_w = [0, 0]
        g_int = [0, 0]
        g_stW = [0, 0]
        g_tp = [0, 0]
        for c in range(8):
            slw = c % 2
            pc_ = "dP0" if slw == 0 else "dP1"
            ic_ = "dI0" if slw == 0 else "dI1"
            wait("sync", "ve", g_w[slw])
            d_w = dma("sync", wsd[:, slw, :],
                      wd_oi[c * 128:(c + 1) * 128, :], pc_)
            wait("vector", pc_, d_w)
            em("vector", lambda e, c=c, slw=slw: e.reduce_max(
                out=md_t[:, c:c + 1], in_=wsd[:, slw, :],
                axis=AX, apply_absolute_value=True))
            vpad()
            em("vector", lambda e, c=c, slw=slw: e.tensor_scalar(
                out=tmpd[:, 2 * slw:2 * slw + 1], in0=md_t[:, c:c + 1],
                scalar1=1.0 / 127.0, scalar2=None, op0=ALU.mult))
            vpad()
            em("vector", lambda e, slw=slw: e.reciprocal(
                tmpd[:, 2 * slw + 1:2 * slw + 2],
                tmpd[:, 2 * slw:2 * slw + 1]))
            vpad()
            g_w[slw] = inc("vector", "ve", lambda e, slw=slw: e.tensor_scalar(
                out=vvd[:, :], in0=wsd[:, slw, :],
                scalar1=tmpd[:, 2 * slw + 1:2 * slw + 2],
                scalar2=None, op0=ALU.mult))
            em("vector", lambda e: e.tensor_copy(out=wqd[:, :], in_=vvd[:, :]))
            em("vector", lambda e: e.tensor_copy(out=ffd[:, :], in_=wqd[:, :]))
            em("vector", lambda e: e.tensor_tensor(
                out=vvd[:, :], in0=vvd[:, :], in1=ffd[:, :], op=ALU.subtract))
            em("vector", lambda e: e.scalar_tensor_tensor(
                out=ffd[:, :], in0=vvd[:, :], scalar=0.5, in1=ffd[:, :],
                op0=ALU.is_ge, op1=ALU.add))
            em("vector", lambda e: e.tensor_scalar(
                out=vvd[:, :], in0=vvd[:, :], scalar1=-0.5, scalar2=None,
                op0=ALU.is_lt))
            wait("vector", "pe", g_int[slw])
            c_q = inc("vector", "ve", lambda e, slw=slw: e.tensor_tensor(
                out=intd[:, slw, :], in0=ffd[:, :], in1=vvd[:, :],
                op=ALU.subtract))
            wait("tensor", "ve", c_q)
            first_cp = True
            for g0 in range(0, CI, 4):
                gw = min(4, CI - g0)
                pb = (g0 // 4) % 2
                wait("tensor", "sc", g_tp[pb])
                for k in range(gw):
                    fn = (lambda e, pb=pb, k=k, g0=g0, slw=slw: e.transpose(
                        tps[pb][:, k * 128:(k + 1) * 128],
                        intd[:, slw, (g0 + k) * 128:(g0 + k + 1) * 128],
                        ident_bs[:, :]))
                    if k == gw - 1:
                        c_tp = inc("tensor", "pe", fn)
                    else:
                        em("tensor", fn)
                if g0 + 4 >= CI:
                    g_int[slw] = c_tp
                wait("scalar", "pe", c_tp)
                if first_cp:
                    wait("scalar", ic_, g_stW[slw])
                    first_cp = False
                g_tp[pb] = inc("scalar", "sc",
                               lambda e, pb=pb, g0=g0, gw=gw, slw=slw:
                               e.copy(out=stgd[:, slw, g0:g0 + gw, :],
                                      in_=tps[pb][:, 0:gw * 128].rearrange(
                                          "p (a b) -> p a b", a=gw)))
            wait("sync", "sc", g_tp[(max(0, CI - 4) // 4) % 2])
            dstap = bass.AP(id_scr, c * 128,
                            [[D, 128], [128 * D, CI], [1, 128]])
            g_stW[slw] = dma("sync", dstap, stgd[:, slw, 0:CI, :], ic_)

        wait("tensor", "sc", n["sc"])
        wait("tensor", "ve", n["ve"])
        cpe = inc("tensor", "pe", lambda e: e.transpose(
            tpf[:, :], md_t[:, :], ident_fs[:, :]))
        wait("scalar", "pe", cpe)
        c_gth = inc("scalar", "sc", lambda e: e.copy(
            out=gath[:, :], in_=tpf[:, :]))
        wait("sync", "sc", c_gth)
        d_row = dma("sync", srow[0:1, 0:1024], gath[0:8, 0:128], "dRW")
        wait("vector", "dRW", d_row)
        c_sdr = inc("vector", "ve", lambda e: e.tensor_scalar(
            out=sdrow[0:1, 0:D], in0=srow[0:1, 0:D], scalar1=1.0 / 127.0,
            scalar2=None, op0=ALU.mult))
        ps5.close()
        p5.close()
        join()
        gate("wprep")

        p6 = ExitStack()
        E6 = p6.enter_context
        hq = E6(nc.sbuf_tensor("hq", [128, CI, TOK], F16))
        s6ab = ExitStack()
        tq1 = s6ab.enter_context(nc.sbuf_tensor("tq1", [128, 1024], F32))
        tq2 = s6ab.enter_context(nc.sbuf_tensor("tq2", [128, 1024], F32))
        xqt = s6ab.enter_context(nc.sbuf_tensor("xqt", [128, 1024], BF16))
        xqi = s6ab.enter_context(nc.sbuf_tensor("xqi", [128, 1024], mybir.dt.int32))

        def hqb(ci, lo, hi):
            return hq[:, ci, lo:hi].bitcast(BF16)

        ps6a = ExitStack()
        tpf6 = ps6a.enter_context(nc.psum_tensor("tpf6", [128, 128], F32))
        for t in range(R):
            c_sq = inc("scalar", "sc", lambda e, t=t: e.activation(
                out=tq1[:, :], in_=xmid[:, t, :], func=AF.Square))
            wait("vector", "sc", c_sq)
            em("vector", lambda e, t=t: e.reduce_sum(
                lnm[:, t:t + 1], xmid[:, t, :], axis=AX))
            em("vector", lambda e, t=t: e.reduce_sum(
                lnr[:, t:t + 1], tq1[:, :], axis=AX))
            em("vector", lambda e, t=t: e.tensor_scalar(
                out=lnm[:, t:t + 1], in0=lnm[:, t:t + 1], scalar1=1.0 / D,
                scalar2=None, op0=ALU.mult))
            em("vector", lambda e, t=t: e.tensor_scalar(
                out=lnr[:, t:t + 1], in0=lnr[:, t:t + 1], scalar1=1.0 / D,
                scalar2=None, op0=ALU.mult))
            em("vector", lambda e, t=t: e.scalar_tensor_tensor(
                out=bc1[:, 4:5], in0=lnm[:, t:t + 1], scalar=-1.0,
                in1=lnm[:, t:t + 1], op0=ALU.mult, op1=ALU.mult))
            vpad()
            c_var = inc("vector", "ve", lambda e, t=t: e.tensor_tensor(
                out=lnr[:, t:t + 1], in0=lnr[:, t:t + 1], in1=bc1[:, 4:5],
                op=ALU.add))
            wait("scalar", "ve", c_var)
            c_rt = inc("scalar", "sc", lambda e, t=t: e.activation(
                out=bc1[:, 5:6], in_=lnr[:, t:t + 1], func=AF.Sqrt,
                bias=epsb[:, 0:1]))
            wait("vector", "sc", c_rt)
            em("vector", lambda e: e.reciprocal(bc1[:, 6:7], bc1[:, 5:6]))
            em("vector", lambda e, t=t: e.tensor_scalar(
                out=bc1[:, 7:8], in0=lnr[:, t:t + 1], scalar1=EPS, scalar2=-0.5,
                op0=ALU.add, op1=ALU.mult))
            em("vector", lambda e: e.tensor_tensor(
                out=bc1[:, 4:5], in0=bc1[:, 6:7], in1=bc1[:, 6:7], op=ALU.mult))
            vpad()
            em("vector", lambda e: e.tensor_tensor(
                out=bc1[:, 4:5], in0=bc1[:, 4:5], in1=bc1[:, 7:8], op=ALU.mult))
            vpad()
            em("vector", lambda e: e.tensor_scalar(
                out=bc1[:, 4:5], in0=bc1[:, 4:5], scalar1=1.5, scalar2=None,
                op0=ALU.add))
            vpad()
            em("vector", lambda e, t=t: e.tensor_tensor(
                out=lnr[:, t:t + 1], in0=bc1[:, 6:7], in1=bc1[:, 4:5],
                op=ALU.mult))
            em("vector", lambda e, t=t: e.tensor_reduce(
                out=bc1[:, 4:5], in_=xmid[:, t, :], axis=AX, op=ALU.max))
            em("vector", lambda e, t=t: e.tensor_reduce(
                out=bc1[:, 5:6], in_=xmid[:, t, :], axis=AX, op=ALU.min))
            em("vector", lambda e, t=t: e.tensor_tensor(
                out=bc1[:, 4:5], in0=bc1[:, 4:5], in1=lnm[:, t:t + 1],
                op=ALU.subtract))
            em("vector", lambda e, t=t: e.tensor_tensor(
                out=bc1[:, 5:6], in0=lnm[:, t:t + 1], in1=bc1[:, 5:6],
                op=ALU.subtract))
            vpad()
            em("vector", lambda e: e.tensor_tensor(
                out=bc1[:, 4:5], in0=bc1[:, 4:5], in1=bc1[:, 5:6], op=ALU.max))
            vpad()
            em("vector", lambda e, t=t: e.tensor_tensor(
                out=bc1[:, 4:5], in0=bc1[:, 4:5], in1=lnr[:, t:t + 1],
                op=ALU.mult))
            vpad()
            if t == 0:
                em("vector", lambda e: e.tensor_copy(out=hmax[:, :],
                                                     in_=bc1[:, 4:5]))
            else:
                em("vector", lambda e: e.tensor_tensor(
                    out=hmax[:, :], in0=hmax[:, :], in1=bc1[:, 4:5], op=ALU.max))
        pmax_to_sml(hmax[:, 0:1], 11, tpf6)
        c_m1 = inc("vector", "ve", lambda e: e.tensor_copy(
            out=stage16[0:1, 0:1], in_=sml[0:1, 11:12]))
        wait("gpsimd", "ve", c_m1)
        d_m1 = dma("gpsimd", m1_in[:, :], stage16[:, :], "dCC")
        wait("gpsimd", "dCC", d_m1)
        n["cc"] += 1
        em("gpsimd", lambda e: e.collective_compute(
            "AllReduce", ALU.max, replica_groups=[list(range(NCORE))],
            ins=[m1_in.ap()], outs=[m1_out.ap()]).then_inc(sem["cc"], 1))
        wait("gpsimd", "cc", n["cc"])
        d_m1o = dma("gpsimd", stage16[:, :], m1_out[:, :], "dCC")
        wait("scalar", "dCC", d_m1o)
        c_sg1 = inc("scalar", "sc", lambda e: e.activation(
            out=sml[0:1, 10:11], in_=clips_sb[0:1, 0:1], func=AF.Sigmoid))
        wait("vector", "sc", c_sg1)
        em("vector", lambda e: e.scalar_tensor_tensor(
            out=sml[0:1, 5:6], in0=sml[0:1, 10:11], scalar=1.0 / 127.0,
            in1=stage16[0:1, 0:1], op0=ALU.mult, op1=ALU.mult))
        vpad()
        c_s1 = inc("vector", "ve", lambda e: e.reciprocal(
            sml[0:1, 6:7], sml[0:1, 5:6]))
        wait("tensor", "ve", c_s1)
        bcast(sml[0:1, 6:7], 0, tpf6)
        c_bc1 = bcast(sml[0:1, 5:6], 1, tpf6)
        wait("vector", "sc", c_bc1)
        em("vector", lambda e: e.tensor_scalar(
            out=s1sg[:, :], in0=sg_f[:, :], scalar1=bc1[:, 1:2], scalar2=None,
            op0=ALU.mult))
        c_s1f = inc("vector", "ve", lambda e: e.tensor_scalar(
            out=s1su[:, :], in0=su_f[:, :], scalar1=bc1[:, 1:2], scalar2=None,
            op0=ALU.mult))
        ps6a.close()

        ps6b = ExitStack()
        trb = ps6b.enter_context(nc.psum_tensor("trb", [128, 512], BF16))
        trb2 = ps6b.enter_context(nc.psum_tensor("trb2", [128, 512], BF16))
        g_tr6 = [0, 0]
        g_xq = 0
        for t in range(R):
            em("vector", lambda e, t=t: e.tensor_scalar(
                out=bc1[:, 4:5], in0=lnr[:, t:t + 1], scalar1=bc1[:, 0:1],
                scalar2=None, op0=ALU.mult))
            vpad()
            c_ab = inc("vector", "ve", lambda e, t=t: e.scalar_tensor_tensor(
                out=bc1[:, 5:6], in0=lnm[:, t:t + 1], scalar=-1.0,
                in1=bc1[:, 4:5], op0=ALU.mult, op1=ALU.mult))
            wait("scalar", "ve", c_ab)
            c_v = inc("scalar", "sc", lambda e, t=t: e.activation(
                out=tq1[:, :], in_=xmid[:, t, :], func=AF.Identity,
                bias=bc1[:, 5:6], scale=bc1[:, 4:5]))
            wait("vector", "sc", c_v)
            em("vector", lambda e: e.tensor_copy(out=xqi[:, :], in_=tq1[:, :]))
            em("vector", lambda e: e.tensor_copy(out=tq2[:, :], in_=xqi[:, :]))
            em("vector", lambda e: e.tensor_tensor(
                out=tq1[:, :], in0=tq1[:, :], in1=tq2[:, :], op=ALU.subtract))
            em("vector", lambda e: e.scalar_tensor_tensor(
                out=tq2[:, :], in0=tq1[:, :], scalar=0.5, in1=tq2[:, :],
                op0=ALU.is_ge, op1=ALU.add))
            em("vector", lambda e: e.tensor_scalar(
                out=tq1[:, :], in0=tq1[:, :], scalar1=-0.5, scalar2=None,
                op0=ALU.is_lt))
            em("vector", lambda e: e.tensor_tensor(
                out=tq1[:, :], in0=tq2[:, :], in1=tq1[:, :], op=ALU.subtract))
            wait("vector", "pe", g_xq)
            c_xq = inc("vector", "ve", lambda e: e.tensor_scalar(
                out=xqt[:, :], in0=tq1[:, :], scalar1=127.0, scalar2=-128.0,
                op0=ALU.min, op1=ALU.max))
            wait("tensor", "ve", c_xq)
            for g in range(2):
                tp = trb if g == 0 else trb2
                wait("tensor", "sc", g_tr6[g])
                for k4 in range(4):
                    hc = g * 4 + k4
                    fn = (lambda e, tp=tp, k4=k4, hc=hc: e.transpose(
                        tp[:, k4 * 128:(k4 + 1) * 128],
                        xqt[:, hc * 128:(hc + 1) * 128], ident_bs[:, :]))
                    if k4 == 3:
                        g_xq = inc("tensor", "pe", fn)
                    else:
                        em("tensor", fn)
                wait("scalar", "pe", g_xq)
                g_tr6[g] = inc("scalar", "sc", lambda e, tp=tp, g=g, t=t: e.copy(
                    out=QR[:, g * 4:(g + 1) * 4, t * 128:(t + 1) * 128],
                    in_=tp[:, :].rearrange("p (a b) -> p a b", a=4)))
        ps6b.close()
        s6ab.close()
        join()
        if debug and _PHASE == "xq":
            with nc.sbuf_tensor("dbq6", [128, TOK], F32) as dbq6:
                dma("gpsimd", dbg_sml[:, :], sml[:, :], "dDBG")
                for dc in range(HC):
                    c_d = inc("vector", "ve", lambda e, dc=dc: e.tensor_copy(
                        out=dbq6[:, :], in_=QR[:, dc, :]))
                    wait("gpsimd", "ve", c_d)
                    dD = dma("gpsimd", dbg_qr[dc * 128:(dc + 1) * 128, :],
                             dbq6[:, :], "dDBG")
                    wait("vector", "dDBG", dD)
        gate("xq")

        ps6c = ExitStack()
        g_psm = [ps6c.enter_context(nc.psum_tensor(f"g_ps{i}", [128, 512], F32))
                 for i in range(2)]
        u_psm = [ps6c.enter_context(nc.psum_tensor(f"u_ps{i}", [128, 512], F32))
                 for i in range(2)]
        igt = ps6c.enter_context(nc.sbuf_tensor("igt", [128, 2, HC, 128], BF16))
        iut = ps6c.enter_context(nc.sbuf_tensor("iut", [128, 2, HC, 128], BF16))
        sgs = ps6c.enter_context(nc.sbuf_tensor("sgs", [128, 2, 512], F32))

        g_slab = [0, 0]
        g_gu = [0, 0]
        g_sgs = [0, 0]
        wait("scalar", "ve", c_s1f)
        ii = 0
        for ci in range(CI):
            sl2 = ci % 2
            wait("sync", "pe", g_slab[sl2])
            gc_ = "dIG0" if sl2 == 0 else "dIG1"
            dma("sync", igt[:, sl2, :, :].rearrange("p a b -> p (a b)"),
                ig_scr[ci, :, :], gc_)
            d_slab = dma("sync", iut[:, sl2, :, :].rearrange("p a b -> p (a b)"),
                         iu_scr[ci, :, :], gc_)
            wait("tensor", gc_, d_slab)
            wait("tensor", "ve", c_s1f)
            for tt4 in range(4):
                sl = ii % 2
                ii += 1
                wait("tensor", "ve", g_gu[sl])
                for (psm, wt) in ((g_psm, igt), (u_psm, iut)):
                    for hc in range(HC):
                        fn = (lambda e, psm=psm, wt=wt, hc=hc, sl=sl, sl2=sl2,
                              tt4=tt4: e.matmul(
                                  psm[sl][:, :], wt[:, sl2, hc, :],
                                  QR[:, hc, tt4 * 512:(tt4 + 1) * 512],
                                  start=(hc == 0), stop=(hc == HC - 1),
                                  skip_group_check=True))
                        if psm is u_psm and hc == HC - 1:
                            c_gu = inc("tensor", "pe", fn)
                        else:
                            em("tensor", fn)
                if tt4 == 3:
                    g_slab[sl2] = c_gu
                wait("scalar", "pe", c_gu)
                wait("scalar", "ve", g_sgs[sl])
                c_silu = inc("scalar", "sc", lambda e, sl=sl, ci=ci: e.activation(
                    out=sgs[:, sl, :], in_=g_psm[sl][:, :], func=AF.Silu,
                    scale=s1sg[:, ci:ci + 1]))
                wait("vector", "sc", c_silu)
                em("vector", lambda e, sl=sl, ci=ci, tt4=tt4: e.scalar_tensor_tensor(
                    out=hq[:, ci, tt4 * 512:(tt4 + 1) * 512], in0=u_psm[sl][:, :],
                    scalar=s1su[:, ci:ci + 1], in1=sgs[:, sl, :], op0=ALU.mult,
                    op1=ALU.mult))
                g_sgs[sl] = g_gu[sl] = c_h = inc(
                    "vector", "ve",
                    lambda e, ci=ci, tt4=tt4: e.reduce_max(
                        out=bc1[:, 4:5], in_=hq[:, ci, tt4 * 512:(tt4 + 1) * 512],
                        axis=AX, apply_absolute_value=True))
                vpad()
                if ci == 0 and tt4 == 0:
                    em("vector", lambda e: e.tensor_copy(out=hmax[:, :],
                                                         in_=bc1[:, 4:5]))
                else:
                    em("vector", lambda e: e.tensor_tensor(
                        out=hmax[:, :], in0=hmax[:, :], in1=bc1[:, 4:5],
                        op=ALU.max))
        ps6c.close()

        s6d = ExitStack()
        tq1 = s6d.enter_context(nc.sbuf_tensor("tq1b", [128, 1024], F32))
        tq2 = s6d.enter_context(nc.sbuf_tensor("tq2b", [128, 1024], F32))
        hqi = s6d.enter_context(nc.sbuf_tensor("hqi", [128, 1024], mybir.dt.int32))
        ps6d = ExitStack()
        tpf7 = ps6d.enter_context(nc.psum_tensor("tpf7", [128, 128], F32))
        bcp7 = ps6d.enter_context(nc.psum_tensor("bcp7", [128, 512], F32))
        pmax_to_sml(hmax[:, 0:1], 11, tpf7)
        c_m2 = inc("vector", "ve", lambda e: e.tensor_copy(
            out=stage16[0:1, 0:1], in_=sml[0:1, 11:12]))
        wait("gpsimd", "ve", c_m2)
        d_m2 = dma("gpsimd", m2_in[:, :], stage16[:, :], "dCC")
        wait("gpsimd", "dCC", d_m2)
        n["cc"] += 1
        em("gpsimd", lambda e: e.collective_compute(
            "AllReduce", ALU.max, replica_groups=[list(range(NCORE))],
            ins=[m2_in.ap()], outs=[m2_out.ap()]).then_inc(sem["cc"], 1))
        wait("gpsimd", "cc", n["cc"])
        d_m2o = dma("gpsimd", stage16[:, :], m2_out[:, :], "dCC")
        wait("scalar", "dCC", d_m2o)
        c_sg2 = inc("scalar", "sc", lambda e: e.activation(
            out=sml[0:1, 10:11], in_=clips_sb[0:1, 1:2], func=AF.Sigmoid))
        wait("vector", "sc", c_sg2)
        em("vector", lambda e: e.scalar_tensor_tensor(
            out=sml[0:1, 8:9], in0=sml[0:1, 10:11], scalar=1.0 / 127.0,
            in1=stage16[0:1, 0:1], op0=ALU.mult, op1=ALU.mult))
        vpad()
        c_s2 = inc("vector", "ve", lambda e: e.reciprocal(
            sml[0:1, 9:10], sml[0:1, 8:9]))
        wait("tensor", "ve", c_s2)
        bcast(sml[0:1, 9:10], 2, tpf7)
        c_sd1 = inc("vector", "ve", lambda e: e.tensor_scalar(
            out=sdrow[0:1, :], in0=sdrow[0:1, :], scalar1=sml[0:1, 8:9],
            scalar2=None, op0=ALU.mult))
        ps6d.close()

        wait("vector", "sc", n["sc"])
        for ci in range(CI):
            for th in range(2):
                lo, hi = th * 1024, (th + 1) * 1024
                em("vector", lambda e, ci=ci, lo=lo, hi=hi: e.tensor_scalar(
                    out=tq1[:, :], in0=hq[:, ci, lo:hi], scalar1=bc1[:, 2:3],
                    scalar2=None, op0=ALU.mult))
                em("vector", lambda e: e.tensor_copy(out=hqi[:, :], in_=tq1[:, :]))
                em("vector", lambda e: e.tensor_copy(out=tq2[:, :], in_=hqi[:, :]))
                em("vector", lambda e: e.tensor_tensor(
                    out=tq1[:, :], in0=tq1[:, :], in1=tq2[:, :], op=ALU.subtract))
                em("vector", lambda e: e.scalar_tensor_tensor(
                    out=tq2[:, :], in0=tq1[:, :], scalar=0.5, in1=tq2[:, :],
                    op0=ALU.is_ge, op1=ALU.add))
                em("vector", lambda e: e.tensor_scalar(
                    out=tq1[:, :], in0=tq1[:, :], scalar1=-0.5, scalar2=None,
                    op0=ALU.is_lt))
                em("vector", lambda e: e.tensor_tensor(
                    out=tq1[:, :], in0=tq2[:, :], in1=tq1[:, :], op=ALU.subtract))
                em("vector", lambda e, ci=ci, lo=lo, hi=hi: e.tensor_scalar(
                    out=hqb(ci, lo, hi), in0=tq1[:, :], scalar1=127.0,
                    scalar2=-128.0, op0=ALU.min, op1=ALU.max))
        if debug:
            for g0 in range(0, CI, 8):
                g1 = min(g0 + 8, CI)
                for ci in range(g0, g1):
                    em("vector", lambda e, ci=ci, g0=g0: e.tensor_copy(
                        out=tq1[:, (ci - g0) * 128:(ci - g0 + 1) * 128],
                        in_=hqb(ci, 0, 128)))
                c_d = inc("vector", "ve", lambda e: e.engine_nop())
                wait("gpsimd", "ve", c_d)
                dD = dma("gpsimd", dbg_hq[:, g0 * 128:g1 * 128],
                         tq1[:, 0:(g1 - g0) * 128], "dDBG")
                wait("vector", "dDBG", dD)
        c_hq = inc("vector", "ve", lambda e: e.engine_nop())
        s6d.close()
        join()
        gate("hq")

        sdb = E6(nc.sbuf_tensor("sdb", [128, 1024], F32))
        t6e = E6(nc.sbuf_tensor("t6e", [128, 1024], F32))
        idt = E6(nc.sbuf_tensor("idt", [128, 2, 1024], BF16))
        wait("tensor", "ve", c_sd1)
        with nc.psum_tensor("bcpE", [128, 512], F32) as bcpE:
            for k in range(2):
                c_bs = inc("tensor", "pe", lambda e, k=k: e.matmul(
                    bcpE[:, :], ones_row[0:1, :],
                    sdrow[0:1, k * 512:(k + 1) * 512],
                    start=True, stop=True, skip_group_check=True))
                wait("vector", "pe", c_bs)
                c_bv = inc("vector", "ve", lambda e, k=k: e.tensor_copy(
                    out=sdb[:, k * 512:(k + 1) * 512], in_=bcpE[:, :]))
                wait("tensor", "ve", c_bv)
        ps6e = ExitStack()
        o4 = [ps6e.enter_context(nc.psum_tensor(f"o4_{i}", [128, 1024], F32))
              for i in range(4)]
        g_id = [0, 0]
        d_out = 0
        for tg in range(4):
            for ci in range(CI):
                sl = ci % 2
                wait("sync", "pe", g_id[sl])
                dc_ = "dID0" if sl == 0 else "dID1"
                d_id = dma("sync", idt[:, sl, :], id_scr[ci, :, :], dc_)
                wait("tensor", dc_, d_id)
                if ci == 0:
                    wait("tensor", "ve", n["ve"])
                for tt in range(4):
                    t = tg * 4 + tt
                    for dh in range(2):
                        fn = (lambda e, tt=tt, t=t, ci=ci, sl=sl, dh=dh: e.matmul(
                            o4[tt][:, dh * 512:(dh + 1) * 512],
                            hqb(ci, t * 128, (t + 1) * 128),
                            idt[:, sl, dh * 512:(dh + 1) * 512],
                            start=(ci == 0), stop=True,
                            skip_group_check=True))
                        if tt == 3 and dh == 1:
                            c_dn = inc("tensor", "pe", fn)
                        else:
                            em("tensor", fn)
                g_id[sl] = c_dn
            wait("vector", "pe", c_dn)
            for tt in range(4):
                t = tg * 4 + tt
                em("vector", lambda e, tt=tt: e.tensor_tensor(
                    out=t6e[:, :], in0=o4[tt][:, :], in1=sdb[:, :],
                    op=ALU.mult))
                c_fin = inc("vector", "ve", lambda e, t=t: e.tensor_tensor(
                    out=xmid[:, t, :], in0=xmid[:, t, :], in1=t6e[:, :],
                    op=ALU.add))
                wait("gpsimd", "ve", c_fin)
                d_out = dma("gpsimd", out_t[t * 128:(t + 1) * 128, :],
                            xmid[:, t, :], "dOUT")
            wait("tensor", "ve", n["ve"])
        ps6e.close()
        p6.close()

        wait("gpsimd", "dOUT", d_out)
        wait("sync", "dOUT", d_out)


    except _Trunc:
        st.pop_all()

    with nc.Block() as block:
        clr = range(min(s_.num for s_ in sem.values()), 256)
        nc.gpsimd.dma_reset(clr)
        nc.gpsimd.sem_clear(clr)
        nc.all_engine_barrier()

        @block.sync
        def _(eng):
            for fn in prog["sync"]:
                fn(eng)

        @block.tensor
        def _(eng):
            for fn in prog["tensor"]:
                fn(eng)

        @block.vector
        def _(eng):
            for fn in prog["vector"]:
                fn(eng)

        @block.scalar
        def _(eng):
            for fn in prog["scalar"]:
                fn(eng)

        @block.gpsimd
        def _(eng):
            for fn in prog["gpsimd"]:
                fn(eng)

    st.close()
    global _LAST_N, _LAST_SEM
    _LAST_N = dict(n)
    _LAST_SEM = {k: v for k, v in sem.items()}
    return nc



_CACHE = {}


def _host_prep(L0_attn_x, L0_attn_wq, L0_attn_wk, L0_attn_wv, L0_ffn_clip_x,
               L0_ffn_clip_down, L0_ffn_w_gate, L0_ffn_w_up, L0_ffn_w_down):
    x = np.ascontiguousarray(np.asarray(L0_attn_x, dtype=np.float32)
                             .reshape(S, D))
    wq = np.asarray(L0_attn_wq, dtype=np.float32)
    wk = np.asarray(L0_attn_wk, dtype=np.float32)
    wv = np.asarray(L0_attn_wv, dtype=np.float32)
    wg = np.asarray(L0_ffn_w_gate, dtype=np.float32)
    wu = np.asarray(L0_ffn_w_up, dtype=np.float32)
    wd = np.asarray(L0_ffn_w_down, dtype=np.float32)

    perm = np.concatenate([np.arange(0, D, 2), np.arange(1, D, 2)])
    wq_t = np.ascontiguousarray(wq[perm].T)
    wk_t = np.ascontiguousarray(wk[perm].T)
    wv_t = np.ascontiguousarray(wv.T)

    ident_b = np.eye(128, dtype=ml_dtypes.bfloat16)
    ident_f = np.eye(128, dtype=np.float32)
    i_ = np.arange(512, dtype=np.float64)
    phi64 = np.power(10000.0, -2.0 * (i_ - 1.0) / 1024.0)
    phi_hi = phi64.astype(np.float32)
    phi_lo = (phi64 - phi_hi.astype(np.float64)).astype(np.float32)
    phi_hi = np.ascontiguousarray(phi_hi.reshape(4, 128).T)
    phi_lo = np.ascontiguousarray(phi_lo.reshape(4, 128).T)
    clips = np.array([[np.float32(np.asarray(L0_ffn_clip_x).ravel()[0]),
                       np.float32(np.asarray(L0_ffn_clip_down).ravel()[0])]],
                     dtype=np.float32)

    in_maps = []
    rows_of = []
    for j in range(NCORE):
        rows = np.concatenate([
            np.arange(1024 * (r - 1) + 128 * ((j + r) % 8),
                      1024 * (r - 1) + 128 * ((j + r) % 8) + 128)
            for r in range(1, R + 1)])
        rows_of.append(rows)
        x_own = np.ascontiguousarray(x[rows]).reshape(R, 128, D)
        pos_own = rows.astype(np.float32).reshape(1, TOK)
        dm = np.zeros((R, 8, 128, 128), dtype=np.float32)
        tri = np.triu(np.ones((128, 128), dtype=np.float32))
        for r in range(1, R + 1):
            o = (j + r) % 8
            for m in range(8):
                op = (m + r) % 8
                if op < o:
                    dm[r - 1, m] = 1.0
                elif op == o:
                    dm[r - 1, m] = tri
        in_maps.append({
            "x_own": x_own, "pos_own": pos_own,
            "dmask": dm.astype(ml_dtypes.bfloat16),
            "phi_hi": phi_hi, "phi_lo": phi_lo,
            "ident_b": ident_b, "ident_f": ident_f,
            "clips": clips, "wq_t": wq_t, "wk_t": wk_t, "wv_t": wv_t,
            "wg_ih": wg, "wu_ih": wu, "wd_oi": wd,
        })
    return in_maps, rows_of


def kernel(L0_attn_x, L0_attn_wq, L0_attn_wk, L0_attn_wv, L0_ffn_clip_x,
           L0_ffn_clip_down, L0_ffn_w_gate, L0_ffn_w_up, L0_ffn_w_down):
    in_maps, rows_of = _host_prep(
        L0_attn_x, L0_attn_wq, L0_attn_wk, L0_attn_wv, L0_ffn_clip_x,
        L0_ffn_clip_down, L0_ffn_w_gate, L0_ffn_w_up, L0_ffn_w_down)
    if "nc" not in _CACHE:
        _CACHE["nc"] = build_nc()
    nc = _CACHE["nc"]
    res = run_bass_kernel_spmd(nc, in_maps, core_ids=list(range(NCORE)))
    out = np.empty((S, D), dtype=np.float32)
    for j in range(NCORE):
        out[rows_of[j]] = res.results[j]["out"]
    return out.reshape(1, S, D)
